# version 27
# baseline (speedup 1.0000x reference)
import numpy as np
import ml_dtypes
from contextlib import ExitStack

import concourse.bass as bass
import concourse.mybir as mybir
from concourse.bass_utils import run_bass_kernel_spmd

F32 = mybir.dt.float32
BF16 = mybir.dt.bfloat16
AF = mybir.ActivationFunctionType
ALU = mybir.AluOpType
AX = mybir.AxisListType
F32R = mybir.dt.float32r


def r32(ap):
    return ap.bitcast(F32R)

B, L, D = 4, 2048, 2048
NT = 16
NKH, DH = 8, 128
NQH = 16
QSC = float(1.0 / np.sqrt(128.0))

DEBUG = False


class Sem:
    def __init__(self, stack, nc, name):
        self.h = stack.enter_context(nc.semaphore(name))
        self.n = 0

    def inc(self, instr, k=1):
        instr.then_inc(self.h, k)
        self.n += k
        return instr

    def wait(self, eng, val=None):
        v = self.n if val is None else val
        assert v <= self.n, f"sem wait {v} > emitted {self.n}"
        if v > 0:
            eng.wait_ge(self.h, v)


def build_program():
    nc = bass.Bass("TRN2", target_bir_lowering=False)
    TE, SC, VE, SY, GP = nc.tensor, nc.scalar, nc.vector, nc.sync, nc.gpsimd

    def din(name, shape, dt=F32):
        return nc.dram_tensor(name, shape, dt, kind="ExternalInput").ap()

    def dscr(name, shape, dt=F32):
        kind = "ExternalOutput" if DEBUG else "Internal"
        return nc.dram_tensor(name, shape, dt, kind=kind).ap()

    x_b = din("x_b", [L, D])
    x_own = din("x_own", [L // 2, D])
    x_last4 = din("x_last4", [4, D])
    wdkv = din("wdkv", [D, 512])
    wuk = din("wuk", [512, NKH * DH])
    wuv = din("wuv", [512, NKH * DH])
    wdq = din("wdq", [D, 1024])
    wuq = din("wuq", [1024, NQH * DH])
    wlam = din("wlam", [D, NKH])
    blam = din("blam", [NKH])
    wout = din("wout", [NKH * DH, D])
    gatew = din("gatew", [D, D])
    gateb = din("gateb", [D])
    aggw = din("aggw", [D, 256])
    aggb = din("aggb", [256])
    bcw = din("bcw", [D, 256])
    bcb = din("bcb", [256])
    ing = din("ing", [D]); inb = din("inb", [D])
    png = din("png", [D]); pnb = din("pnb", [D])
    outg = din("outg", [D]); outb = din("outb", [D])
    d1w = din("d1w", [D, 1024])
    d1b = din("d1b", [1024])
    d2w = din("d2w", [1024, D])
    d2b = din("d2b", [D])
    cosk = din("cosk", [128, L])
    sinkS = din("sinkS", [128, L])
    ident = din("ident", [128, 128], BF16)
    identf = din("identf", [128, 128], F32)
    maskdiag = din("maskdiag", [128, 128], BF16)
    onehot4 = din("onehot4", [4, 1])

    out = nc.dram_tensor("out", [L // 2, D], F32, kind="ExternalOutput").ap()

    k_scr = dscr("k_scr", [NKH, 128, L], BF16)
    q_scr = dscr("q_scr", [NQH, 128, L], BF16)
    gate_scr = dscr("gate_scr", [L // 2, D])
    po_scr = dscr("po_scr", [4, L, 512])
    ao_scr = dscr("ao_scr", [4, L // 2, 512])
    cc_tin = dscr("cc_tin", [4, D]) if DEBUG else nc.dram_tensor("cc_tin", [4, D], F32).ap()
    cc_tout = dscr("cc_tout", [4, D]) if DEBUG else nc.dram_tensor("cc_tout", [4, D], F32).ap()
    cc_ag = nc.dram_tensor("cc_ag", [32, 256], F32).ap()
    cc_ag2 = nc.dram_tensor("cc_ag2", [32, 256], F32).ap()
    cc_tps = nc.dram_tensor("cc_tps", [4, 256], F32).ap()
    ckv_scr = nc.dram_tensor("ckv_scr", [128, 4 * L], BF16).ap()
    ffx_scr = nc.dram_tensor("ffx_scr", [L // 2, D], F32).ap()
    d1w_scr = nc.dram_tensor("d1w_scr", [D, 1024], BF16).ap()
    d2w_scr = nc.dram_tensor("d2w_scr", [1024, D], BF16).ap()
    wout_scr = nc.dram_tensor("wout_scr", [1024, D], BF16).ap()
    if DEBUG:
        th_dbg = nc.dram_tensor("th_dbg", [6, 4, D], F32, kind="ExternalOutput").ap()
        tln_dbg = nc.dram_tensor("tln_dbg", [4, D], F32, kind="ExternalOutput").ap()
        t2_dbg = nc.dram_tensor("t2_dbg", [128, 32], F32, kind="ExternalOutput").ap()
        cin_dbg = nc.dram_tensor("cin_dbg", [4, 4, D], F32, kind="ExternalOutput").ap()
        cout_dbg = nc.dram_tensor("cout_dbg", [4, 4, D], F32, kind="ExternalOutput").ap()
        tln_dbg2 = nc.dram_tensor("tln_dbg2", [4, 4, D], F32, kind="ExternalOutput").ap()
        s_dbg = Sem if False else None

    st = ExitStack()
    with st:
        E = st.enter_context
        s_pe = Sem(st, nc, "s_pe")
        s_act = Sem(st, nc, "s_act")
        s_dve = Sem(st, nc, "s_dve")
        s_pl = Sem(st, nc, "s_pl")
        s_cc = Sem(st, nc, "s_cc")
        s_lc = Sem(st, nc, "s_lc")
        s_row = Sem(st, nc, "s_row")
        s_x2 = [Sem(st, nc, f"s_x{i}") for i in range(2)]
        s_st2 = [Sem(st, nc, f"s_s{i}") for i in range(2)]
        s_gl = Sem(st, nc, "s_gl")
        s_wk2 = [Sem(st, nc, f"s_w{i}") for i in range(2)]
        s_rot = Sem(st, nc, "s_rot")
        s_rt4 = [Sem(st, nc, f"s_rt{i}") for i in range(4)]
        s_kb2 = [Sem(st, nc, f"s_k{i}") for i in range(2)]
        s_qb2 = [Sem(st, nc, f"s_q{i}") for i in range(2)]
        s_lam = Sem(st, nc, "s_lam")
        s_sp = Sem(st, nc, "s_sp")
        s_ct = Sem(st, nc, "s_ct")
        s_dbg = Sem(st, nc, "s_dbg")
        s_out = Sem(st, nc, "s_out")
        s_res = Sem(st, nc, "s_res")
        s_d1s = Sem(st, nc, "s_d1s")
        s_d2s = Sem(st, nc, "s_d2s")
        s_rg = Sem(st, nc, "s_rg")
        s_gpc = Sem(st, nc, "s_gpc")
        s_po8 = [Sem(st, nc, f"s_po{i}") for i in range(8)]
        s_ow2 = [Sem(st, nc, f"s_ow{i}") for i in range(2)]
        s_fb4 = [Sem(st, nc, f"s_fb{i}") for i in range(4)]
        s_ob4 = [Sem(st, nc, f"s_ob{i}") for i in range(4)]

        _block = E(nc.Block())

        class _EngWrap:
            def __init__(self, eng, sem):
                self._eng = eng
                self._sem = sem

            def __getattr__(self, name):
                f = getattr(self._eng, name)
                sem, eng = self._sem, self._eng

                def g(*a, **k):
                    sem.wait(eng)
                    return sem.inc(f(*a, **k))
                return g

        VEw = _EngWrap(VE, s_dve)
        SCw = _EngWrap(SC, s_act)
        GPw = _EngWrap(GP, s_gpc)

        def recip_acc(out, in_, scratch):
            SCw.activation(scratch, in_, AF.Ln)
            freed = s_act.n
            SCw.activation(out, scratch, AF.Exp, scale=c_neg1[0:1, :])
            return freed, s_act.n

        ident_s = E(nc.sbuf_tensor("ident_s", [128, 128], BF16, side="left"))
        identf_s = E(nc.sbuf_tensor("identf_s", [128, 128], F32, side="left"))
        mask_s = E(nc.sbuf_tensor("mask_s", [128, 128], BF16, side="left"))
        ones_rf = E(nc.sbuf_tensor("ones_rf", [1, 128], F32, side="left"))
        ones_rb = E(nc.sbuf_tensor("ones_rb", [1, 128], BF16, side="left"))
        row_gb = E(nc.sbuf_tensor("row_gb", [1, D], BF16, side="left"))
        row_ab = row_gb
        ones_cb = E(nc.sbuf_tensor("ones_cb", [128, 1], BF16, side="left"))
        oneh4 = E(nc.sbuf_tensor("oneh4", [4, 1], F32, side="left"))
        c_eps5 = E(nc.sbuf_tensor("c_eps5", [128, 1], F32, side="left"))
        c_eps6 = E(nc.sbuf_tensor("c_eps6", [128, 1], F32, side="left"))
        c_half = E(nc.sbuf_tensor("c_half", [128, 1], F32, side="left"))
        c_nhalf = E(nc.sbuf_tensor("c_nhalf", [128, 1], F32, side="left"))
        c_n102 = E(nc.sbuf_tensor("c_n102", [128, 1], F32, side="left"))
        c_neg1 = E(nc.sbuf_tensor("c_neg1", [128, 1], F32, side="left"))
        c_qsc = E(nc.sbuf_tensor("c_qsc", [128, 1], F32, side="left"))
        lamT = E(nc.sbuf_tensor("lamT", [NKH, L], F32, side="left"))
        negB = E(nc.sbuf_tensor("negB", [128, NQH], F32, side="left"))
        mk2 = E(nc.sbuf_tensor("mk2", [1, NKH], F32, side="left"))
        mq2 = E(nc.sbuf_tensor("mq2", [1, NQH], F32, side="left"))
        m4t = E(nc.sbuf_tensor("m4t", [1, 4], F32, side="left"))
        v_bnG = E(nc.sbuf_tensor("v_bnG", [128, 48], F32, side="left"))
        v_agG = E(nc.sbuf_tensor("v_agG", [128, 4], F32, side="left"))
        sc11 = E(nc.sbuf_tensor("sc11", [1, 1], F32, side="left"))
        thought = E(nc.sbuf_tensor("thought", [4, D], F32, side="left"))
        png_s = E(nc.sbuf_tensor("png_s", [128, NT], F32, side="left"))
        pnb_s = E(nc.sbuf_tensor("pnb_s", [128, NT], F32, side="left"))
        d1b_s = E(nc.sbuf_tensor("d1b_s", [128, 8], F32, side="left"))
        xnl_fm = E(nc.sbuf_tensor("xnl_fm", [128, NT * 4], BF16, side="left"))
        row_a = E(nc.sbuf_tensor("row_a", [1, D], F32, side="left"))
        row_b = E(nc.sbuf_tensor("row_b", [1, D], F32, side="left"))
        s4tmp = E(nc.sbuf_tensor("s4tmp", [4, D], F32, side="left"))
        tln_f = E(nc.sbuf_tensor("tln_f", [4, D], F32, side="left"))
        bc_bc = E(nc.sbuf_tensor("bc_bc", [128, D], F32, side="left"))
        v_sum = E(nc.sbuf_tensor("v_sum", [128, 4], F32, side="left"))
        v_ssq = E(nc.sbuf_tensor("v_ssq", [128, 4], F32, side="left"))
        v_var = E(nc.sbuf_tensor("v_var", [128, 4], F32, side="left"))
        v_rstd = E(nc.sbuf_tensor("v_rstd", [128, 4], F32, side="left"))
        v_mb = E(nc.sbuf_tensor("v_mb", [128, 4], F32, side="left"))

        for t, val in ((ones_rf, 1.0), (ones_rb, 1.0),
                       (ones_cb, 1.0), (c_eps5, 1e-5),
                       (c_eps6, 1e-6), (c_half, 0.5), (c_nhalf, -0.5),
                       (c_n102, -1.02), (c_neg1, -1.0), (c_qsc, QSC)):
            s_pl.inc(GP.memset(t[:, :], val))
        s_lc.inc(SY.dma_start(ident_s[:, :], ident[:, :]), 16)
        s_lc.inc(SY.dma_start(identf_s[:, :], identf[:, :]), 16)
        s_lc.inc(SY.dma_start(mask_s[:, :], maskdiag[:, :]), 16)
        s_lc.inc(SY.dma_start(oneh4[:, :], onehot4[:, :]), 16)
        s_lc.inc(SY.dma_start(d1b_s[:, :], d1b.rearrange("(j p) -> p j", p=128),
                              allow_slow_non_contiguous=True), 16)
        s_lc.inc(SY.dma_start(png_s[:, :], png.rearrange("(j p) -> p j", p=128),
                              allow_slow_non_contiguous=True), 16)
        s_lc.inc(SY.dma_start(pnb_s[:, :], pnb.rearrange("(j p) -> p j", p=128),
                              allow_slow_non_contiguous=True), 16)
        lc_consts = s_lc.n

        def row1(ap1d):
            return ap1d.rearrange("(a d) -> a d", a=1)

        def ln_stats(tile, sq, n, eps_ap, r=0):
            vs = v_sum[0:n, r:r + 1]
            vq = v_ssq[0:n, r:r + 1]
            vv = v_var[0:n, r:r + 1]
            vr = v_rstd[0:n, r:r + 1]
            vm = v_mb[0:n, r:r + 1]
            s_act.wait(VE)
            VEw.reduce_sum(vs, tile[0:n, 0:D], axis=AX.X)
            s_dve.wait(SC)
            SCw.activation(sq[0:n, 0:D], tile[0:n, 0:D], AF.Square, accum_out=vq)
            s_act.wait(VE)
            VEw.tensor_scalar_mul(vs, vs, 1.0 / D)
            VEw.tensor_scalar_mul(vv, vq, 1.0 / D)
            VEw.tensor_scalar(vq, vs, vs, None, ALU.mult)
            VEw.tensor_sub(vv, vv, vq)
            s_dve.wait(SC)
            SCw.activation(vr, vv, AF.Ln, bias=eps_ap[0:n, :])
            SCw.activation(vr, vr, AF.Exp, scale=c_nhalf[0:n, :])
            s_act.wait(VE)
            VEw.tensor_scalar(vm, vs, vr, -1.0, ALU.mult, ALU.mult)

        def ln_stats_bn(tile, eps_ap, r):
            s_act.wait(VE)
            for k in range(4):
                VEw.bn_stats(v_bnG[:, r * 24 + k * 6:r * 24 + (k + 1) * 6],
                             tile[:, k * 512:(k + 1) * 512])
            VEw.bn_aggr(v_agG[:, 2 * r:2 * r + 2], v_bnG[:, r * 24:(r + 1) * 24])
            s_dve.wait(SC)
            SCw.activation(v_rstd[:, r:r + 1], v_agG[:, 2 * r + 1:2 * r + 2],
                           AF.Ln, bias=eps_ap[:, :])
            SCw.activation(v_rstd[:, r:r + 1], v_rstd[:, r:r + 1],
                           AF.Exp, scale=c_nhalf[:, :])
            s_act.wait(VE)
            VEw.tensor_scalar(v_mb[:, r:r + 1], v_agG[:, 2 * r:2 * r + 1],
                              v_rstd[:, r:r + 1], -1.0, ALU.mult, ALU.mult)

        def bcast_rows(ps, row_src, dst, n):
            for og in range(4):
                s_pe.wait(TE)
                s_act.wait(TE)
                s_dve.wait(TE)
                s_pe.inc(TE.matmul(ps[0:n, 0:512], ones_rf[0:1, 0:n],
                                   row_src[0:1, og * 512:(og + 1) * 512],
                                   start=True, stop=True))
                s_pe.wait(SC)
                SCw.copy(dst[0:n, og * 512:(og + 1) * 512], ps[0:n, 0:512])

        with ExitStack() as p0:
            E0 = p0.enter_context
            x4 = E0(nc.sbuf_tensor("x4", [4, D], F32, side="right"))
            xn4 = E0(nc.sbuf_tensor("xn4", [4, D], F32, side="right"))
            g4 = E0(nc.sbuf_tensor("g4", [4, D], F32, side="right"))
            b4 = E0(nc.sbuf_tensor("b4", [4, D], F32, side="right"))
            xnl_bf = E0(nc.sbuf_tensor("xnl_bf", [4, D], BF16, side="right"))
            ps4 = E0(nc.psum_tensor("ps4", [4, 512], F32))
            pstr0 = E0(nc.psum_tensor("pstr0", [128, NT * 4], BF16))

            s_x2[0].inc(SY.dma_start(x4[:, :], x_last4[:, :]), 16)
            s_row.inc(SY.dma_start(row_a[:, :], row1(ing)), 16)
            s_row.inc(SY.dma_start(row_b[:, :], row1(inb)), 16)
            for eng in (VE, SC, TE):
                s_lc.wait(eng)
                s_row.wait(eng)
                s_x2[0].wait(eng)
                s_pl.wait(eng)
            bcast_rows(ps4, row_a, g4, 4)
            bcast_rows(ps4, row_b, b4, 4)
            ln_stats(x4, xn4, 4, c_eps5)
            s_dve.wait(SC)
            SCw.activation(xn4[0:4, :], x4[0:4, :], AF.Identity,
                                    scale=v_rstd[0:4, 0:1], bias=v_mb[0:4, 0:1])
            ln_stats(xn4, tln_f, 4, c_eps6)
            s_dve.wait(SC)
            SCw.activation(tln_f[0:4, :], xn4[0:4, :], AF.Identity,
                                    scale=v_rstd[0:4, 0:1], bias=v_mb[0:4, 0:1])
            s_act.wait(VE)
            VEw.tensor_mul(tln_f[0:4, :], tln_f[0:4, :], g4[0:4, :])
            VEw.tensor_add(tln_f[0:4, :], tln_f[0:4, :], b4[0:4, :])
            VEw.tensor_copy(xnl_bf[0:4, :], tln_f[0:4, :])
            s_dve.wait(TE)
            for ch in range(NT):
                s_pe.inc(TE.transpose(pstr0[:, ch * 4:(ch + 1) * 4],
                                      xnl_bf[0:4, ch * 128:(ch + 1) * 128],
                                      ident_s[0:4, 0:4]))
            s_pe.wait(SC)
            SCw.copy(xnl_fm[:, :], pstr0[:, :])

        with ExitStack() as p13:
            E1 = p13.enter_context
            xn_fm = E1(nc.sbuf_tensor("xn_fm", [128, NT * L], BF16, side="right"))
            cosk_s = E1(nc.sbuf_tensor("cosk_s", [128, L], F32, side="right"))
            sinkS_s = E1(nc.sbuf_tensor("sinkS_s", [128, L], F32, side="right"))
            blam_s = E1(nc.sbuf_tensor("blam_s", [NKH, 1], F32, side="right"))
            s_act.wait(SY)
            s_lc.inc(SY.dma_start(cosk_s[:, :], cosk[:, :]), 16)
            s_lc.inc(SY.dma_start(sinkS_s[:, :], sinkS[:, :]), 16)
            s_lc.inc(SY.dma_start(blam_s[:, :],
                                  blam.rearrange("(p a) -> p a", a=1)), 16)
            lc_all = s_lc.n

            with ExitStack() as p1:
                E2 = p1.enter_context
                xt = [E2(nc.sbuf_tensor(f"xt{i}", [128, D], F32, side="right"))
                      for i in range(2)]
                xnbf = [E2(nc.sbuf_tensor(f"xnbf{i}", [128, D], BF16, side="right"))
                        for i in range(2)]
                trp = [E2(nc.psum_tensor(f"trp{i}", [128, 512], BF16))
                       for i in range(2)]
                ld_hist, act_xn, pe_tile, evac_hist = [], [], [], []
                mb_mark = [0, 0]

                def _s1(tt):
                    bi = tt % 2
                    if tt >= 2:
                        s_act.wait(SY, act_xn[tt - 2])
                    s_x2[bi].inc(SY.dma_start(
                        xt[bi][:, :], x_b[tt * 128:(tt + 1) * 128, :]), 16)
                    ld_hist.append(s_x2[bi].n)
                    s_x2[bi].wait(VE, ld_hist[tt])
                    ln_stats_bn(xt[bi], c_eps5, r=bi)
                    mb_mark[bi] = s_dve.n

                def _s2(tt):
                    bi = tt % 2
                    s_x2[bi].wait(SC, ld_hist[tt])
                    s_dve.wait(SC, mb_mark[bi])
                    if tt >= 2:
                        s_pe.wait(SC, pe_tile[tt - 2])
                    for grp in range(4):
                        SCw.activation(xnbf[bi][:, grp * 512:(grp + 1) * 512],
                                       xt[bi][:, grp * 512:(grp + 1) * 512],
                                       AF.Identity,
                                       scale=v_rstd[:, bi:bi + 1],
                                       bias=v_mb[:, bi:bi + 1])
                        s_act.wait(TE)
                        if len(evac_hist) >= 2:
                            s_dve.wait(TE, evac_hist[-2])
                        for j in range(4):
                            ch = grp * 4 + j
                            s_pe.inc(TE.transpose(
                                trp[grp % 2][:, j * 128:(j + 1) * 128],
                                xnbf[bi][:, ch * 128:(ch + 1) * 128], ident_s[:, :]))
                        s_pe.wait(VE)
                        dst = xn_fm[:, :].rearrange("p (c t) -> p c t", c=NT)[
                            :, grp * 4:(grp + 1) * 4, tt * 128:(tt + 1) * 128]
                        src = trp[grp % 2][:, :].rearrange("p (c t) -> p c t", c=4)
                        VEw.tensor_copy(dst, src)
                        evac_hist.append(s_dve.n)
                    act_xn.append(s_act.n)
                    pe_tile.append(s_pe.n)

                _s1(0)
                for tt in range(NT):
                    if tt + 1 < NT:
                        _s1(tt + 1)
                    _s2(tt)
            xn_ready_dve = s_dve.n
            xn_ready = s_act.n
            pvX = ExitStack()

            with ExitStack() as p3:
                E2 = p3.enter_context
                cq_fm = E2(nc.sbuf_tensor("cq_fm", [128, 8 * L], BF16, side="right"))
                pre = E2(nc.sbuf_tensor("pre3", [128, L], F32, side="right"))
                rot = E2(nc.sbuf_tensor("rot3", [128, L], F32, side="right"))
                rq_bf = E2(nc.sbuf_tensor("rq_bf", [128, L], BF16, side="right"))
                rsq = E2(nc.sbuf_tensor("rsq3", [128, L], BF16, side="right"))
                wlam_bf = E2(nc.sbuf_tensor("wlam_bf", [128, NT * NKH], BF16,
                                            side="right"))
                wqb = [E2(nc.sbuf_tensor(f"wqb{i}", [128, NT * 128], BF16,
                                         side="right")) for i in range(2)]
                mmp = [E2(nc.psum_tensor(f"mm3{i}", [128, 512], F32)) for i in range(2)]
                nrmp = E2(nc.psum_tensor("nrmp3", [1, 512], F32))

                s_lam.inc(GP.dma_start(
                    wlam_bf[:, :].rearrange("p (c n) -> p c n", c=NT),
                    wlam.rearrange("(c p) n -> p c n", p=128)), 16)
                lam_w = s_lam.n
                wdq_r = wdq.rearrange("(c p) n -> p c n", p=128)

                mm_user = [0, 0]
                wm_hist, pe_m = [], []
                for m in range(8):
                    bi = m % 2
                    if m >= 2:
                        s_pe.wait(GP, pe_m[m - 2])
                    s_wk2[bi].inc(GP.dma_start(
                        wqb[bi][:, :].rearrange("p (c n) -> p c n", c=NT),
                        wdq_r[:, :, m * 128:(m + 1) * 128]), 16)
                    wm_hist.append(s_wk2[bi].n)
                    for tg in range(4):
                        pi = (m * 4 + tg) % 2
                        if m == 0 and tg == 0:
                            s_act.wait(TE, xn_ready)
                            s_dve.wait(TE, xn_ready_dve)
                            s_dve.wait(TE)
                        s_wk2[bi].wait(TE, wm_hist[m])
                        s_act.wait(TE, mm_user[pi])
                        for ch in range(NT):
                            s_pe.inc(TE.matmul(
                                mmp[pi][:, :],
                                wqb[bi][:, ch * 128:(ch + 1) * 128],
                                xn_fm[:, ch * L + tg * 512:ch * L + (tg + 1) * 512],
                                start=(ch == 0), stop=(ch == NT - 1)))
                        s_pe.wait(SC)
                        SCw.copy(
                            cq_fm[:, m * L + tg * 512:m * L + (tg + 1) * 512],
                            mmp[pi][:, :])
                        mm_user[pi] = s_act.n
                    pe_m.append(s_pe.n)

                wuq_r = wuq.rearrange("(c p) n -> p c n", p=128)
                uq_hist, pe_qh = [], []
                q_spill = 0
                nrm_pe = 0
                pre_free = [0] * 4
                rot_free = [0] * 4
                for qh in range(NQH):
                    bi = qh % 2
                    if qh >= 2:
                        s_pe.wait(GP, pe_qh[qh - 2])
                    else:
                        s_pe.wait(GP, pe_m[qh - 2])
                    s_wk2[bi].inc(GP.dma_start(
                        wqb[bi][:, :].rearrange("p (c n) -> p c n", c=NT)[:, 0:8, :],
                        wuq_r[:, :, qh * 128:(qh + 1) * 128]), 16)
                    uq_hist.append(s_wk2[bi].n)
                    if qh < 8:
                        s_res.inc(GP.dma_start(
                            d1w_scr[qh * 256:(qh + 1) * 256, :],
                            d1w[qh * 256:(qh + 1) * 256, :]), 16)
                        s_res.inc(GP.dma_start(
                            d2w_scr[qh * 128:(qh + 1) * 128, :],
                            d2w[qh * 128:(qh + 1) * 128, :]), 16)
                    elif qh < 12:
                        s_res.inc(GP.dma_start(
                            wout_scr[(qh - 8) * 256:(qh - 7) * 256, :],
                            wout[(qh - 8) * 256:(qh - 7) * 256, :]), 16)
                    for tg in range(4):
                        pi = tg % 2
                        lo, hi = tg * 512, (tg + 1) * 512
                        s_wk2[bi].wait(TE, uq_hist[qh])
                        s_act.wait(TE, mm_user[pi])
                        for c in range(8):
                            s_pe.inc(TE.matmul(
                                mmp[pi][:, :],
                                wqb[bi][:, c * 128:(c + 1) * 128],
                                cq_fm[:, c * L + lo:c * L + hi],
                                start=(c == 0), stop=(c == 7)))
                        s_pe.wait(SC)
                        s_dve.wait(SC, pre_free[tg])
                        SCw.activation(pre[:, lo:hi], mmp[pi][:, :],
                                       AF.Identity, scale=c_qsc[:, :])
                        mm_user[pi] = s_act.n
                        evn = s_act.n
                        s_act.wait(GP, evn)
                        s_dve.wait(GP, rot_free[tg])
                        s_rt4[tg].inc(GP.dma_start(rot[0:64, lo:hi],
                                                   pre[64:128, lo:hi]), 16)
                        s_rt4[tg].inc(GP.dma_start(rot[64:128, lo:hi],
                                                   pre[0:64, lo:hi]), 16)
                        rotw = s_rt4[tg].n
                        s_act.wait(VE, evn)
                        if qh == 0 and tg == 0:
                            s_lc.wait(VE, lc_all)
                        s_rt4[tg].wait(VE, rotw)
                        VEw.tensor_mul(pre[:, lo:hi], pre[:, lo:hi],
                                       cosk_s[:, lo:hi])
                        pre_free[tg] = s_dve.n
                        VEw.tensor_mul(rot[:, lo:hi], rot[:, lo:hi],
                                       sinkS_s[:, lo:hi])
                        rot_free[tg] = s_dve.n
                        if tg == 0:
                            s_sp.wait(VE, q_spill)
                        VEw.tensor_tensor(rq_bf[:, lo:hi], pre[:, lo:hi],
                                          rot[:, lo:hi], ALU.add)
                        if tg == 0:
                            s_pe.wait(VE, nrm_pe)
                        VEw.tensor_mul(rsq[:, lo:hi], rq_bf[:, lo:hi],
                                       rq_bf[:, lo:hi])
                    pe_qh.append(s_pe.n)
                    qdone = s_dve.n
                    s_dve.wait(GP, qdone)
                    s_sp.inc(GP.dma_start(q_scr[qh, :, :], rq_bf[:, :]), 16)
                    q_spill = s_sp.n
                    s_dve.wait(TE, qdone)
                    for tg in range(4):
                        s_pe.inc(TE.matmul(nrmp[0:1, :], ones_cb[:, :],
                                           rsq[:, tg * 512:(tg + 1) * 512],
                                           start=(tg == 0), stop=(tg == 3),
                                           skip_group_check=True))
                    nrm_pe = s_pe.n
                    s_pe.wait(VE, nrm_pe)
                    VEw.reduce_max(mq2[0:1, qh:qh + 1], nrmp[0:1, :],
                                   axis=AX.X)

                for tg in range(4):
                    s_act.wait(TE)
                    s_dve.wait(TE)
                    s_lam.wait(TE, lam_w)
                    for ch in range(NT):
                        s_pe.inc(TE.matmul(
                            mmp[0][0:NKH, :],
                            wlam_bf[:, ch * NKH:(ch + 1) * NKH],
                            xn_fm[:, ch * L + tg * 512:ch * L + (tg + 1) * 512],
                            start=(ch == 0), stop=(ch == NT - 1)))
                    s_pe.wait(SC)
                    s_lc.wait(SC, lc_all)
                    SCw.activation(lamT[0:NKH, tg * 512:(tg + 1) * 512],
                                            mmp[0][0:NKH, :], AF.Sigmoid,
                                            bias=blam_s[0:NKH, :])

            with ExitStack() as p2:
                E2 = p2.enter_context
                ckv_fm = E2(nc.sbuf_tensor("ckv_fm", [128, 4 * L], BF16, side="right"))
                wuk_bf = E2(nc.sbuf_tensor("wuk_bf", [128, 4 * 1024], BF16, side="right"))
                wkvb = [E2(nc.sbuf_tensor(f"wkvb{i}", [128, NT * 128], BF16,
                                          side="right")) for i in range(2)]
                pre = E2(nc.sbuf_tensor("pre", [128, L], F32, side="right"))
                rot = E2(nc.sbuf_tensor("rot", [128, L], F32, side="right"))
                rk_bf = E2(nc.sbuf_tensor("rk_bf", [128, L], BF16, side="right"))
                rsq = E2(nc.sbuf_tensor("rsq", [128, L], BF16, side="right"))
                mmp = [E2(nc.psum_tensor(f"mmp{i}", [128, 512], F32)) for i in range(2)]
                nrmp = E2(nc.psum_tensor("nrmp", [1, 512], F32))

                s_gl.inc(GP.dma_start(
                    wuk_bf[:, :].rearrange("p (c n) -> p c n", c=4),
                    wuk.rearrange("(c p) n -> p c n", p=128)), 16)
                wready = s_gl.n
                wdkv_r = wdkv.rearrange("(c p) n -> p c n", p=128)

                s_act.wait(TE)
                s_dve.wait(TE)
                evh = []
                wm_hist, pe_m = [], []
                for m in range(4):
                    bi = m % 2
                    if m >= 2:
                        s_pe.wait(GP, pe_m[m - 2])
                    else:
                        s_pe.wait(GP)
                    s_wk2[bi].inc(GP.dma_start(
                        wkvb[bi][:, :].rearrange("p (c n) -> p c n", c=NT),
                        wdkv_r[:, :, m * 128:(m + 1) * 128]), 16)
                    wm_hist.append(s_wk2[bi].n)
                    for tg in range(4):
                        pi = (m * 4 + tg) % 2
                        s_wk2[bi].wait(TE, wm_hist[m])
                        if len(evh) >= 2:
                            s_act.wait(TE, evh[-2])
                        for ch in range(NT):
                            s_pe.inc(TE.matmul(
                                mmp[pi][:, :],
                                wkvb[bi][:, ch * 128:(ch + 1) * 128],
                                xn_fm[:, ch * L + tg * 512:ch * L + (tg + 1) * 512],
                                start=(ch == 0), stop=(ch == NT - 1)))
                        s_pe.wait(SC)
                        SCw.copy(
                            ckv_fm[:, m * L + tg * 512:m * L + (tg + 1) * 512],
                            mmp[pi][:, :])
                        evh.append(s_act.n)
                    pe_m.append(s_pe.n)

                k_spill = s_sp.n
                nrm_pe = 0
                pre_free = [0] * 4
                rot_free = [0] * 4
                for kh in range(NKH):
                    for tg in range(4):
                        lo, hi = tg * 512, (tg + 1) * 512
                        if kh == 0 and tg == 0:
                            s_gl.wait(TE, wready)
                        if len(evh) >= 2:
                            s_act.wait(TE, evh[-2])
                        for c in range(4):
                            s_pe.inc(TE.matmul(
                                mmp[tg % 2][:, :],
                                wuk_bf[:, c * 1024 + kh * 128:c * 1024 + (kh + 1) * 128],
                                ckv_fm[:, c * L + lo:c * L + hi],
                                start=(c == 0), stop=(c == 3)))
                        s_pe.wait(SC)
                        s_dve.wait(SC, pre_free[tg])
                        SCw.copy(pre[:, lo:hi], mmp[tg % 2][:, :])
                        evh.append(s_act.n)
                        evn = s_act.n
                        s_act.wait(GP, evn)
                        s_dve.wait(GP, rot_free[tg])
                        s_rt4[tg].inc(GP.dma_start(rot[0:64, lo:hi],
                                                   pre[64:128, lo:hi]), 16)
                        s_rt4[tg].inc(GP.dma_start(rot[64:128, lo:hi],
                                                   pre[0:64, lo:hi]), 16)
                        rotw = s_rt4[tg].n
                        s_act.wait(VE, evn)
                        s_rt4[tg].wait(VE, rotw)
                        VEw.tensor_mul(pre[:, lo:hi], pre[:, lo:hi],
                                       cosk_s[:, lo:hi])
                        pre_free[tg] = s_dve.n
                        VEw.tensor_mul(rot[:, lo:hi], rot[:, lo:hi],
                                       sinkS_s[:, lo:hi])
                        rot_free[tg] = s_dve.n
                        if tg == 0:
                            s_sp.wait(VE, k_spill)
                        VEw.tensor_tensor(rk_bf[:, lo:hi], pre[:, lo:hi],
                                          rot[:, lo:hi], ALU.add)
                        if tg == 0:
                            s_pe.wait(VE, nrm_pe)
                        VEw.tensor_mul(rsq[:, lo:hi], rk_bf[:, lo:hi],
                                       rk_bf[:, lo:hi])
                    kdone = s_dve.n
                    s_dve.wait(GP, kdone)
                    s_sp.inc(GP.dma_start(k_scr[kh, :, :], rk_bf[:, :]), 16)
                    k_spill = s_sp.n
                    s_dve.wait(TE, kdone)
                    for tg in range(4):
                        s_pe.inc(TE.matmul(nrmp[0:1, :], ones_cb[:, :],
                                           rsq[:, tg * 512:(tg + 1) * 512],
                                           start=(tg == 0), stop=(tg == 3),
                                           skip_group_check=True))
                    nrm_pe = s_pe.n
                    s_pe.wait(VE, nrm_pe)
                    VEw.reduce_max(mk2[0:1, kh:kh + 1], nrmp[0:1, :],
                                   axis=AX.X)

                s_act.wait(GP)
                s_sp.inc(GP.dma_start(ckv_scr[:, :], ckv_fm[:, :]), 16)
                ckv_spilled = s_sp.n

                for qh in range(NQH):
                    kh = qh // 2
                    s_act.wait(VE)
                    s_pe.wait(VE)
                    VEw.tensor_tensor(sc11[0:1, :], mq2[0:1, qh:qh + 1],
                                               mk2[0:1, kh:kh + 1], ALU.mult)
                    s_dve.wait(SC)
                    SCw.activation(sc11[0:1, :], sc11[0:1, :], AF.Ln)
                    SCw.activation(sc11[0:1, :], sc11[0:1, :], AF.Exp,
                                            scale=c_half[0:1, :])
                    s_act.wait(TE)
                    s_pe.inc(TE.matmul(mmp[1][:, 0:1], ones_rf[0:1, :],
                                       sc11[0:1, 0:1], start=True, stop=True))
                    s_pe.wait(SC)
                    SCw.activation(negB[:, qh:qh + 1], mmp[1][:, 0:1],
                                            AF.Identity, scale=c_n102[:, :])

        v_full = pvX.enter_context(
            nc.sbuf_tensor("v_full", [128, NKH * NT * 128], BF16, side="left"))
        with ExitStack() as p2b:
            E2 = p2b.enter_context
            ckv2 = E2(nc.sbuf_tensor("ckv2", [128, 4 * L], BF16, side="right"))
            wuv_bf = E2(nc.sbuf_tensor("wuv_bf", [128, 4 * 1024], BF16, side="right"))
            mmp = [E2(nc.psum_tensor(f"mmv{i}", [128, 512], F32)) for i in range(2)]
            s_sp.wait(GP, ckv_spilled)
            s_gl.inc(GP.dma_start(ckv2[:, :], ckv_scr[:, :]), 16)
            s_gl.inc(GP.dma_start(
                wuv_bf[:, :].rearrange("p (c n) -> p c n", c=4),
                wuv.rearrange("(c p) n -> p c n", p=128)), 16)
            vready = s_gl.n
            s_act.wait(TE)
            s_dve.wait(TE)
            evh = []
            for kt in range(NT):
                for og in range(2):
                    pi = (kt * 2 + og) % 2
                    if kt == 0 and og == 0:
                        s_gl.wait(TE, vready)
                    if len(evh) >= 2:
                        s_act.wait(TE, evh[-2])
                    for c in range(4):
                        s_pe.inc(TE.matmul(
                            mmp[pi][:, :],
                            ckv2[:, c * L + kt * 128:c * L + (kt + 1) * 128],
                            wuv_bf[:, c * 1024 + og * 512:c * 1024 + (og + 1) * 512],
                            start=(c == 0), stop=(c == 3)))
                    s_pe.wait(SC)
                    dst = v_full[:, :].rearrange(
                        "p (h t d) -> p h t d", h=NKH, t=NT)[
                        :, og * 4:(og + 1) * 4, kt, :]
                    src = mmp[pi][:, :].rearrange("p (h d) -> p h d", h=4)
                    SCw.copy(dst, src)
                    evh.append(s_act.n)

        with ExitStack() as p35:
            E2 = p35.enter_context
            xno_fm = E2(nc.sbuf_tensor("xno_fm", [128, NT * 1024], BF16, side="right"))
            with ExitStack() as p35a:
                E3 = p35a.enter_context
                xt = [E3(nc.sbuf_tensor(f"xot{i}", [128, D], F32, side="right"))
                      for i in range(2)]
                xnbf = [E3(nc.sbuf_tensor(f"xobf{i}", [128, D], BF16, side="right"))
                        for i in range(2)]
                trp = [E3(nc.psum_tensor(f"otrp{i}", [128, 512], BF16))
                       for i in range(2)]
                ld_hist, act_xn, pe_tile, evac_hist = [], [], [], []
                mb_mark = [0, 0]

                def _s1(tt):
                    bi = tt % 2
                    if tt >= 2:
                        s_act.wait(SY, act_xn[tt - 2])
                    if tt == 0:
                        s_dve.wait(SY)
                    s_x2[bi].inc(SY.dma_start(
                        xt[bi][:, :], x_own[tt * 128:(tt + 1) * 128, :]), 16)
                    ld_hist.append(s_x2[bi].n)
                    s_x2[bi].wait(VE, ld_hist[tt])
                    ln_stats_bn(xt[bi], c_eps5, r=bi)
                    mb_mark[bi] = s_dve.n

                def _s2(tt):
                    bi = tt % 2
                    s_x2[bi].wait(SC, ld_hist[tt])
                    s_dve.wait(SC, mb_mark[bi])
                    if tt >= 2:
                        s_pe.wait(SC, pe_tile[tt - 2])
                    for grp in range(4):
                        SCw.activation(xnbf[bi][:, grp * 512:(grp + 1) * 512],
                                       xt[bi][:, grp * 512:(grp + 1) * 512],
                                       AF.Identity,
                                       scale=v_rstd[:, bi:bi + 1],
                                       bias=v_mb[:, bi:bi + 1])
                        s_act.wait(TE)
                        if len(evac_hist) >= 2:
                            s_dve.wait(TE, evac_hist[-2])
                        for j in range(4):
                            ch = grp * 4 + j
                            s_pe.inc(TE.transpose(
                                trp[grp % 2][:, j * 128:(j + 1) * 128],
                                xnbf[bi][:, ch * 128:(ch + 1) * 128], ident_s[:, :]))
                        s_pe.wait(VE)
                        dst = xno_fm[:, :].rearrange("p (c t) -> p c t", c=NT)[
                            :, grp * 4:(grp + 1) * 4, tt * 128:(tt + 1) * 128]
                        src = trp[grp % 2][:, :].rearrange("p (c t) -> p c t", c=4)
                        VEw.tensor_copy(dst, src)
                        evac_hist.append(s_dve.n)
                    act_xn.append(s_act.n)
                    pe_tile.append(s_pe.n)

                _s1(0)
                for tt in range(8):
                    if tt + 1 < 8:
                        _s1(tt + 1)
                    _s2(tt)

            with ExitStack() as p35b:
                E3 = p35b.enter_context
                gwb = E3(nc.sbuf_tensor("gwb", [128, NT * 512], BF16, side="right"))
                gstage = [E3(nc.sbuf_tensor(f"gst{i}", [128, 512], F32, side="right"))
                          for i in range(2)]
                gp_ps = [E3(nc.psum_tensor(f"gp{i}", [128, 512], F32))
                         for i in range(2)]
                gatew_r = gatew.rearrange("(c p) n -> p c n", p=128)
                s_act.wait(SY)
                s_pe.wait(SY)
                s_act.wait(TE)
                s_dve.wait(TE)
                s_rg.inc(GP.dma_start(row_gb[:, :], row1(gateb)), 16)
                gb_ld = s_rg.n
                st_hist = []
                gp_user = [0, 0]
                for og in range(4):
                    s_pe.wait(GP)
                    s_wk2[0].inc(GP.dma_start(
                        gwb[:, :].rearrange("p (c n) -> p c n", c=NT),
                        gatew_r[:, :, og * 512:(og + 1) * 512]), 16)
                    gw_now = s_wk2[0].n
                    for qt in range(8):
                        pi = qt % 2
                        s_wk2[0].wait(TE, gw_now)
                        if og == 0 and qt == 0:
                            s_rg.wait(TE, gb_ld)
                        if len(st_hist) >= 2:
                            sem, val = st_hist[-2]
                            sem.wait(TE, val)
                        s_act.wait(TE, gp_user[pi])
                        for ch in range(NT):
                            s_pe.inc(TE.matmul(
                                gp_ps[pi][:, :],
                                xno_fm[:, ch * 1024 + qt * 128:ch * 1024 + (qt + 1) * 128],
                                gwb[:, ch * 512:(ch + 1) * 512],
                                start=(ch == 0), stop=False,
                                skip_group_check=True))
                        s_pe.inc(TE.matmul(
                            gp_ps[pi][:, :], ones_rb[0:1, 0:128],
                            row_gb[0:1, og * 512:(og + 1) * 512],
                            start=False, stop=True, skip_group_check=True))
                        s_pe.wait(SC)
                        SCw.activation(gstage[pi][:, :], gp_ps[pi][:, :],
                                                AF.Sigmoid)
                        gp_user[pi] = s_act.n
                        s_act.wait(SY)
                        s_st2[pi].inc(SY.dma_start(
                            gate_scr[qt * 128:(qt + 1) * 128,
                                     og * 512:(og + 1) * 512], gstage[pi][:, :]), 16)
                        st_hist.append((s_st2[pi], s_st2[pi].n))

        combT = pvX.enter_context(
            nc.sbuf_tensor("combT", [128, NKH * L], BF16, side="left"))

        with ExitStack() as p4:
            E2 = p4.enter_context

            with ExitStack() as p4a:
                E3 = p4a.enter_context
                awb = E3(nc.sbuf_tensor("awb", [128, NT * 256], BF16, side="right"))
                th4p = E3(nc.psum_tensor("th4p", [4, 512], F32))
                s_act.wait(SY)
                s_pe.wait(SY)
                s_row.inc(SY.dma_start(row_a[0:1, 0:256], row1(aggb)), 16)
                agb_ld = s_row.n
                s_pe.wait(GP)
                s_act.wait(GP)
                s_st2[0].wait(GP)
                s_st2[1].wait(GP)
                s_wk2[0].inc(GP.dma_start(
                    awb[:, :].rearrange("p (c n) -> p c n", c=NT),
                    aggw.rearrange("(c p) n -> p c n", p=128)), 16)
                aw_ld = s_wk2[0].n
                s_wk2[0].wait(TE, aw_ld)
                s_row.wait(TE, agb_ld)
                s_act.wait(TE)
                s_dve.wait(TE)
                for ch in range(NT):
                    s_pe.inc(TE.matmul(
                        th4p[0:4, 0:256],
                        xnl_fm[:, ch * 4:(ch + 1) * 4],
                        awb[:, ch * 256:(ch + 1) * 256],
                        start=(ch == 0), stop=False, skip_group_check=True))
                s_pe.inc(TE.matmul(
                    th4p[0:4, 0:256], ones_rf[0:1, 0:4], row_a[0:1, 0:256],
                    start=False, stop=True, skip_group_check=True))
                s_pe.wait(SC)
                SCw.copy(s4tmp[0:4, 0:256], th4p[0:4, 0:256])
                s_act.wait(GP)
                s_ct.inc(GP.dma_start(cc_tps[:, :], s4tmp[0:4, 0:256]), 16)
                GP.wait_ge(s_ct.h, s_ct.n)
                s_cc.inc(GP.collective_compute(
                    "AllGather", ALU.bypass,
                    replica_groups=[[0, 1, 2, 3, 4, 5, 6, 7]],
                    ins=[cc_tps[:, :].opt()], outs=[cc_ag[:, :].opt()]))

            kbuf = [E2(nc.sbuf_tensor(f"kbuf{i}", [128, L], BF16, side="right"))
                    for i in range(2)]
            qbuf = [E2(nc.sbuf_tensor(f"qbuf{i}", [128, L], BF16, side="right"))
                    for i in range(2)]
            prb = [E2(nc.sbuf_tensor(f"prb{i}", [128, 512], BF16, side="right"))
                   for i in range(2)]
            s1b = E2(nc.sbuf_tensor("s1b", [128, 512], F32, side="right"))
            s2b = E2(nc.sbuf_tensor("s2b", [128, 512], F32, side="right"))
            tmp1 = E2(nc.sbuf_tensor("tmp1", [128, 512], F32, side="right"))
            tmp2 = E2(nc.sbuf_tensor("tmp2", [128, 512], F32, side="right"))
            lamx = E2(nc.sbuf_tensor("lamx", [1, L], F32, side="right"))
            rrow = E2(nc.sbuf_tensor("rrow", [1, 2560], F32, side="right"))
            rrowb = E2(nc.sbuf_tensor("rrowb", [1, 1024], BF16, side="right"))
            t2T = E2(nc.sbuf_tensor("t2T", [128, 32], BF16, side="right"))
            tlnT = E2(nc.sbuf_tensor("tlnT", [128, 64], BF16, side="right"))
            bcwb = E2(nc.sbuf_tensor("bcwb", [128, NT * 256], BF16, side="right"))
            d1wb = [E2(nc.sbuf_tensor(f"d1wb{i}", [128, 2048], BF16, side="right"))
                    for i in range(2)]
            d2wb = [E2(nc.sbuf_tensor(f"d2wb{i}", [128, 4096], BF16, side="right"))
                    for i in range(2)]
            scp = [E2(nc.psum_tensor(f"scp{i}", [128, 512], F32)) for i in range(2)]
            atp = [E2(nc.psum_tensor(f"atp{i}", [128, 512], F32)) for i in range(2)]
            denp = [E2(nc.psum_tensor(f"denp{i}", [1, 512], F32)) for i in range(2)]
            auxp = E2(nc.psum_tensor("auxp", [128, 512], F32))
            thp = E2(nc.psum_tensor("thp", [128, 512], F32))

            kq_hist = {}
            q_hist = {}
            pending_s2 = [None]
            comb1_pe = [0]

            def comb1_part():
                kh_, g_, s2d = pending_s2[0]
                pending_s2[0] = None
                s_dve.wait(TE, s2d)
                s_act.wait(TE)
                s_pe.inc(TE.matmul(auxp[:, :], ones_rb[0:1, :],
                                   rrowb[0:1, 512:1024], start=True, stop=True))
                comb1_pe[0] = s_pe.n
                s_pe.wait(SC)
                s_dve.wait(SC, at_bank_user[1])
                SCw.copy(s2b[:, :], auxp[:, :])
                e2 = s_act.n
                s_act.wait(VE, e2)
                VEw.tensor_mul(tmp2[:, :], atp[1][:, :], s2b[:, :])
                at_bank_user[1] = s_dve.n
                VEw.tensor_tensor(
                    combT[:, kh_ * L + g_ * 512:kh_ * L + (g_ + 1) * 512],
                    tmp1[:, :], tmp2[:, :], ALU.subtract)
            pe_kh_end = [0] * NKH
            pe_br_end = {}
            lam_hist = [0] * (NKH + 1)
            sc_bank_user = [0, 0]
            pr_bank_user = [0, 0]
            at_bank_user = [0, 0]
            den_bank_user = [0, 0]
            den_ready = [0, 0]
            cc_done = 0

            def load_k(kh):
                if kh >= 2:
                    s_pe.wait(SY, pe_kh_end[kh - 2])
                s_kb2[kh % 2].inc(SY.dma_start(kbuf[kh % 2][:, :],
                                               k_scr[kh, :, :]), 16)
                kq_hist[kh] = s_kb2[kh % 2].n

            def load_q(kh, br):
                j = kh * 2 + br
                if j >= 2:
                    s_pe.wait(SY, pe_br_end[j - 2])
                s_qb2[j % 2].inc(SY.dma_start(qbuf[j % 2][:, :],
                                              q_scr[j, :, :]), 16)
                q_hist[j] = s_qb2[j % 2].n

            def load_lam(kh):
                s_dve.wait(GP)
                s_act.wait(GP)
                s_lam.inc(GP.dma_start(lamx[0:1, :], lamT[kh:kh + 1, :]), 16)
                lam_hist[kh] = s_lam.n

            tln_mark = [0]
            d1_hist = [None, None]
            th_d1pe = [0, 0]
            th_d2pe = [0, 0]

            def d1_load(blk):
                sl = d1w_scr[blk * 256:(blk + 1) * 256, :].rearrange(
                    "(c p) n -> p c n", p=128)
                bi = blk % 2
                if bi == 0:
                    s_wk2[0].inc(GP.dma_start(
                        d1wb[0][:, :].rearrange("p (c n) -> p c n", c=2), sl), 16)
                    d1_hist[0] = (s_wk2[0], s_wk2[0].n)
                else:
                    s_d1s.inc(SY.dma_start(
                        d1wb[1][:, :].rearrange("p (c n) -> p c n", c=2), sl), 16)
                    d1_hist[1] = (s_d1s, s_d1s.n)

            def thought_partA(step):
                if step == 0:
                    thought_init_finalize()
                else:
                    thought_step_finalize(step - 1)
                s_act.wait(VE)
                ln_stats(thought, tln_f, 4, c_eps6, r=2)
                s_dve.wait(SC)
                SCw.activation(tln_f[0:4, :], thought[0:4, :], AF.Identity,
                                        scale=v_rstd[0:4, 2:3], bias=v_mb[0:4, 2:3])
                tln_mark[0] = s_act.n
                if step == 0:
                    s_res.wait(GP)
                    s_res.wait(SY)
                s_pe.wait(GP, th_d1pe[0])
                d1_load(0)
                s_pe.wait(SY, th_d1pe[1])
                d1_load(1)

            def thought_partB(step):
                s_act.wait(TE, tln_mark[0])
                s_dve.wait(TE)
                for ch in range(NT):
                    s_pe.inc(TE.transpose(thp[:, 64 + ch * 4:68 + ch * 4],
                                          tln_f[0:4, ch * 128:(ch + 1) * 128],
                                          identf_s[0:4, 0:4]))
                s_pe.wait(SC)
                for ch in range(NT):
                    SCw.activation(tlnT[:, ch * 4:(ch + 1) * 4],
                                            thp[:, 64 + ch * 4:68 + ch * 4],
                                            AF.Identity,
                                            scale=png_s[:, ch:ch + 1],
                                            bias=pnb_s[:, ch:ch + 1])
                tln_bf = s_act.n
                pe_blk = []
                for blk in range(8):
                    bi = blk % 2
                    if blk >= 2:
                        if bi == 0:
                            s_pe.wait(GP, pe_blk[blk - 2])
                        else:
                            s_pe.wait(SY, pe_blk[blk - 2])
                        d1_load(blk)
                    wsem, wval = d1_hist[bi]
                    wsem.wait(TE, wval)
                    if blk == 0:
                        s_act.wait(TE, tln_bf)
                    for ci in range(2):
                        ch = blk * 2 + ci
                        for hb in range(8):
                            s_pe.inc(TE.matmul(
                                thp[:, hb * 4:(hb + 1) * 4],
                                d1wb[bi][:, ci * 1024 + hb * 128:
                                          ci * 1024 + (hb + 1) * 128],
                                tlnT[:, ch * 4:(ch + 1) * 4],
                                start=(ch == 0 and hb == 0),
                                stop=(ch == NT - 1 and hb == 7),
                                skip_group_check=True))
                    pe_blk.append(s_pe.n)
                th_d1pe[0] = pe_blk[6]
                th_d1pe[1] = pe_blk[7]
                s_pe.wait(SC)
                for hb in range(8):
                    SCw.activation(t2T[:, hb * 4:(hb + 1) * 4],
                                            thp[:, hb * 4:(hb + 1) * 4], AF.Gelu,
                                            bias=d1b_s[:, hb:hb + 1])
                geld = s_act.n
                s_act.wait(SY)
                s_pe.wait(SY)
                s_row.inc(SY.dma_start(row_a[:, :], row1(d2b)), 16)
                s_row.wait(VE)
                s_pe.wait(VE)
                VEw.tensor_scalar_mul(row_ab[0:1, :], row_a[0:1, :], 0.125)
                d2brow = s_dve.n
                d2w_r = d2w_scr.rearrange("(c p) n -> p c n", p=128)
                pe_og2 = []
                d2h = [None, None]
                for og in range(4):
                    bi = og % 2
                    if bi == 0:
                        s_pe.wait(GP, pe_og2[og - 2] if og >= 2 else th_d2pe[0])
                        s_wk2[0].inc(GP.dma_start(
                            d2wb[0][:, :].rearrange("p (c n) -> p c n", c=8),
                            d2w_r[:, :, og * 512:(og + 1) * 512]), 16)
                        d2h[0] = (s_wk2[0], s_wk2[0].n)
                    else:
                        s_pe.wait(SY, pe_og2[og - 2] if og >= 2 else th_d2pe[1])
                        s_d2s.inc(SY.dma_start(
                            d2wb[1][:, :].rearrange("p (c n) -> p c n", c=8),
                            d2w_r[:, :, og * 512:(og + 1) * 512]), 16)
                        d2h[1] = (s_d2s, s_d2s.n)
                    wsem, wval = d2h[bi]
                    if og > 0:
                        s_dve.wait(TE)
                    wsem.wait(TE, wval)
                    if og == 0:
                        s_act.wait(TE, geld)
                        s_dve.wait(TE, d2brow)
                    for hb in range(8):
                        s_pe.inc(TE.matmul(
                            thp[0:4, 0:512], t2T[:, hb * 4:(hb + 1) * 4],
                            d2wb[bi][:, hb * 512:(hb + 1) * 512],
                            start=(hb == 0), stop=False,
                            skip_group_check=True))
                    s_pe.inc(TE.matmul(
                        thp[0:4, 0:512], ones_rb[0:1, 0:4],
                        row_ab[0:1, og * 512:(og + 1) * 512],
                        start=False, stop=True, skip_group_check=True))
                    s_pe.wait(VE)
                    VEw.tensor_copy(s4tmp[0:4, og * 512:(og + 1) * 512],
                                             thp[0:4, 0:512])
                    pe_og2.append(s_pe.n)
                th_d2pe[0] = pe_og2[2]
                th_d2pe[1] = pe_og2[3]
                s_dve.wait(GP)
                s_ct.inc(GP.dma_start(cc_tin[:, :], s4tmp[0:4, :]), 16)
                GP.wait_ge(s_ct.h, s_ct.n)
                s_cc.inc(GP.collective_compute(
                    "AllReduce", ALU.add,
                    replica_groups=[[0, 1, 2, 3, 4, 5, 6, 7]],
                    ins=[cc_tin[:, :].opt()], outs=[cc_tout[:, :].opt()]))

            def thought_init_finalize():
                nonlocal cc_done
                cc_done += 1
                GP.wait_ge(s_cc.h, cc_done)
                s_dve.wait(GP)
                s_ct.inc(GP.dma_start(
                    thought[0:4, :].rearrange("b (r n) -> b r n", r=8),
                    cc_ag.rearrange("(r b) n -> b r n", b=4)), 16)
                s_ct.wait(VE)
                s_ct.wait(SC)

            def thought_step_finalize(step):
                nonlocal cc_done
                cc_done += 1
                GP.wait_ge(s_cc.h, cc_done)
                s_dve.wait(GP)
                if DEBUG:
                    GP.wait_ge(s_dbg.h, s_dbg.n)
                s_ct.inc(GP.dma_start(s4tmp[0:4, :], cc_tout[:, :]), 16)
                s_ct.wait(VE)
                if DEBUG:
                    s_ct.wait(GP)
                    s_dbg.inc(GP.dma_start(cout_dbg[step, :, :], s4tmp[0:4, :]), 16)
                    s_dbg.wait(VE)
                VEw.tensor_add(thought[0:4, :], thought[0:4, :],
                                        s4tmp[0:4, :])
                if DEBUG:
                    s_dve.wait(GP)
                    s_dbg.inc(GP.dma_start(th_dbg[1 + step, :, :], thought[0:4, :]), 16)

            s_act.wait(TE)
            s_dve.wait(TE)
            s_sp.wait(SY)
            s_wk2[0].wait(SY)
            s_wk2[1].wait(SY)
            s_pe.wait(SY)
            s_st2[0].wait(GP)
            s_st2[1].wait(GP)
            s_st2[0].wait(SY)
            s_st2[1].wait(SY)
            load_k(0)
            load_q(0, 0)
            load_q(0, 1)
            load_lam(0)
            for kh in range(NKH):
                bi = kh % 2
                if kh + 1 < NKH:
                    load_k(kh + 1)
                s_kb2[bi].wait(TE, kq_hist[kh])
                for g in range(4):
                    nkt = 4 * g + 4
                    pv_done = 0
                    for br in range(2):
                        j = kh * 2 + br
                        ab = br
                        s_qb2[j % 2].wait(TE, q_hist[j])
                        s_dve.wait(TE, at_bank_user[ab])
                        s_act.wait(TE, den_bank_user[ab])
                        pend = {}

                        def emit_score(kt, kh=kh, g=g, j=j, bi=bi, pend=pend):
                            lo = (kt - 4 * g) * 128 if kt >= 4 * g else 0
                            sb = kt % 2
                            s_act.wait(TE, sc_bank_user[sb])
                            s_pe.inc(TE.matmul(
                                scp[sb][:, lo:512],
                                kbuf[bi][:, kt * 128:(kt + 1) * 128],
                                qbuf[j % 2][:, g * 512 + lo:(g + 1) * 512],
                                start=True, stop=True, skip_group_check=True))
                            s_pe.wait(SC)
                            s_pe.wait(SC, pr_bank_user[sb])
                            s_act.inc(SC.activation(
                                prb[sb][:, lo:512], scp[sb][:, lo:512], AF.Exp,
                                bias=negB[:, j:j + 1]))
                            sc_bank_user[sb] = s_act.n
                            expn = s_act.n
                            exp_dve = 0
                            if kt >= 4 * g:
                                s_act.wait(VE, expn)
                                s_dve.inc(VE.tensor_mul(prb[sb][:, lo:lo + 128],
                                                        prb[sb][:, lo:lo + 128],
                                                        mask_s[:, :]))
                                exp_dve = s_dve.n
                            pend[kt] = (lo, sb, expn, exp_dve)

                        def emit_pv(kt, nkt, kh=kh, ab=ab, pend=pend):
                            lo, sb, expn, exp_dve = pend.pop(kt)
                            s_act.wait(TE, expn)
                            if exp_dve:
                                s_dve.wait(TE, exp_dve)
                            s_pe.inc(TE.matmul(
                                atp[ab][:, lo:512],
                                v_full[:, (kh * NT + kt) * 128:(kh * NT + kt + 1) * 128],
                                prb[sb][:, lo:512],
                                start=(kt == 0), stop=(kt == nkt - 1),
                                skip_group_check=True))
                            s_pe.inc(TE.matmul(
                                denp[ab][0:1, lo:512], ones_cb[:, :],
                                prb[sb][:, lo:512],
                                start=(kt == 0), stop=(kt == nkt - 1),
                                skip_group_check=True))
                            pr_bank_user[sb] = s_pe.n

                        def comb0_part():
                            s_act.wait(TE, den_ready[0])
                            s_act.wait(TE)
                            s_pe.inc(TE.matmul(auxp[:, :], ones_rb[0:1, :],
                                               rrowb[0:1, 0:512],
                                               start=True, stop=True))
                            s_pe.wait(SC)
                            s_dve.wait(SC, at_bank_user[0])
                            SCw.copy(s1b[:, :], auxp[:, :])
                            e1 = s_act.n
                            s_act.wait(VE, e1)
                            VEw.tensor_mul(tmp1[:, :], atp[0][:, :], s1b[:, :])
                            at_bank_user[0] = s_dve.n

                        emit_score(0)
                        for kt in range(1, nkt):
                            emit_score(kt)
                            emit_pv(kt - 1, nkt)
                            if br == 0 and kt == 2 and pending_s2[0] is not None:
                                comb1_part()
                            if br == 1 and kt == 3:
                                comb0_part()
                        emit_pv(nkt - 1, nkt)
                        pv_done = s_pe.n
                        if br == 0:
                            s_pe.wait(SC, pv_done)
                            f0, r0 = recip_acc(rrowb[0:1, 0:512],
                                               denp[0][0:1, :],
                                               rrow[0:1, 1536:2048])
                            den_bank_user[0] = f0
                            den_ready[0] = r0
                        if g == 3:
                            pe_br_end[j] = s_pe.n
                            if kh + 1 < NKH:
                                load_q(kh + 1, br)
                    s_pe.wait(SC, pv_done)
                    f1, r1 = recip_acc(rrow[0:1, 512:1024], denp[1][0:1, :],
                                       rrow[0:1, 2048:2560])
                    den_bank_user[1] = f1
                    den_ready[1] = r1
                    s_lam.wait(VE, lam_hist[kh])
                    s_act.wait(VE, r1)
                    s_pe.wait(VE, comb1_pe[0])
                    VEw.tensor_tensor(rrowb[0:1, 512:1024],
                                               lamx[0:1, g * 512:(g + 1) * 512],
                                               rrow[0:1, 512:1024], ALU.mult)
                    pending_s2[0] = (kh, g, s_dve.n)
                    if g == 1 and kh in (1, 3, 5, 6):
                        thought_partA((kh - 1) // 2 if kh < 6 else 3)
                pe_kh_end[kh] = s_pe.n
                if kh + 1 < NKH:
                    load_lam(kh + 1)
                if kh in (1, 3, 5, 6):
                    thought_partB((kh - 1) // 2 if kh < 6 else 3)
                elif kh == 7:
                    thought_step_finalize(3)
            if pending_s2[0] is not None:
                comb1_part()

            s_dve.wait(TE)
            s_act.wait(TE)
            for ch in range(NT):
                s_pe.inc(TE.transpose(thp[:, 64 + ch * 4:68 + ch * 4],
                                      thought[0:4, ch * 128:(ch + 1) * 128],
                                      identf_s[0:4, 0:4]))
            s_pe.wait(SC)
            SCw.copy(tlnT[:, :], thp[:, 64:128])
            thT_bf = s_act.n
            s_act.wait(SY)
            s_pe.wait(SY)
            s_row.inc(SY.dma_start(row_a[0:1, 0:256], row1(bcb)), 16)
            bcb_ld = s_row.n
            s_pe.wait(GP)
            s_wk2[0].inc(GP.dma_start(
                bcwb[:, :].rearrange("p (c n) -> p c n", c=NT),
                bcw.rearrange("(c p) n -> p c n", p=128)), 16)
            bw_ld = s_wk2[0].n
            s_wk2[0].wait(TE, bw_ld)
            s_act.wait(TE, thT_bf)
            s_row.wait(TE, bcb_ld)
            s_dve.wait(TE)
            for ch in range(NT):
                s_pe.inc(TE.matmul(
                    thp[0:4, 0:256], tlnT[:, ch * 4:(ch + 1) * 4],
                    bcwb[:, ch * 256:(ch + 1) * 256], start=(ch == 0),
                    stop=False, skip_group_check=True))
            s_pe.inc(TE.matmul(thp[0:4, 0:256], ones_rf[0:1, 0:4],
                               row_a[0:1, 0:256],
                               start=False, stop=True, skip_group_check=True))
            s_pe.wait(SC)
            SCw.copy(s4tmp[0:4, 0:256], thp[0:4, 0:256])
            pe_bc_mm = s_pe.n
            act_bc = s_act.n
            dve_bc = s_dve.n
            s_pe.wait(GP, pe_bc_mm)
            s_act.wait(GP, act_bc)
            s_dve.wait(GP, dve_bc)
            s_ct.inc(GP.dma_start(cc_tps[:, :], s4tmp[0:4, 0:256]), 16)
            GP.wait_ge(s_ct.h, s_ct.n)
            s_cc.inc(GP.collective_compute(
                "AllGather", ALU.bypass,
                replica_groups=[[0, 1, 2, 3, 4, 5, 6, 7]],
                ins=[cc_tps[:, :].opt()], outs=[cc_ag2[:, :].opt()]))
            cc_done += 1
            GP.wait_ge(s_cc.h, cc_done)
            s_ct.inc(GP.dma_start(
                s4tmp[0:4, :].rearrange("b (r n) -> b r n", r=8),
                cc_ag2.rearrange("(r b) n -> b r n", b=4)), 16)
            bc_ct = s_ct.n

        rs_done = [0, 0, 0, 0]
        with ExitStack() as p56:
            E2 = p56.enter_context
            wob = [E2(nc.sbuf_tensor(f"wob{i}", [128, 8 * 512], BF16, side="right"))
                   for i in range(2)]
            postg = [E2(nc.sbuf_tensor(f"postg{i}", [128, 512], F32, side="right"))
                     for i in range(2)]
            xt6 = [E2(nc.sbuf_tensor(f"xt6{i}", [128, D], F32, side="right"))
                   for i in range(2)]
            gt6s = E2(nc.sbuf_tensor("gt6s", [128, D], F32, side="right"))
            xn6 = [E2(nc.sbuf_tensor(f"xn6{i}", [128, D], F32, side="right"))
                   for i in range(2)]
            goB = E2(nc.sbuf_tensor("goB", [128, D], F32, side="right"))
            boB = E2(nc.sbuf_tensor("boB", [128, D], F32, side="right"))
            v_bn6 = E2(nc.sbuf_tensor("v_bn6", [128, 48], F32, side="right"))
            v_ag = E2(nc.sbuf_tensor("v_ag", [128, 4], F32, side="right"))
            pop = [E2(nc.psum_tensor(f"pop{i}", [128, 512], F32)) for i in range(2)]
            p6p = E2(nc.psum_tensor("p6p", [128, 512], F32))
            fin6 = [xt6[0][:, 0:1024], xt6[1][:, 0:1024],
                    xt6[0][:, 1024:2048], xt6[1][:, 1024:2048]]
            at6 = [xn6[0][:, 0:1024], xn6[1][:, 0:1024],
                   xn6[0][:, 1024:2048], xn6[1][:, 1024:2048]]
            pe_og, st_hist, ow_hist = [], [], []

            wos_r = wout_scr.rearrange("(c p) n -> p c n", p=128)
            s_pe.wait(SY, pe_bc_mm)
            s_act.wait(SY, act_bc)
            s_dve.wait(SY, dve_bc)
            s_res.wait(SY)
            for og01 in range(2):
                s_ow2[og01].inc(SY.dma_start(
                    wob[og01][:, :].rearrange("p (c n) -> p c n", c=8),
                    wos_r[:, :, og01 * 512:(og01 + 1) * 512]), 16)
                ow_hist.append(s_ow2[og01].n)

            s_act.wait(SY)
            s_pe.wait(SY)
            s_dve.wait(SY)
            s_row.inc(SY.dma_start(row_a[:, :], row1(outg)), 16)
            s_row.inc(SY.dma_start(row_b[:, :], row1(outb)), 16)
            s_row.wait(TE)
            s_act.wait(TE)
            s_dve.wait(TE)
            bcast_rows(p6p, row_a, goB, 128)
            bcast_rows(p6p, row_b, boB, 128)

            def ln_stats_vg(tile, eps_ap, r):
                s_act.wait(VE)
                s_gpc.wait(VE)
                for k in range(4):
                    VEw.bn_stats(v_bn6[:, r * 24 + k * 6:r * 24 + (k + 1) * 6],
                                 tile[:, k * 512:(k + 1) * 512])
                VEw.bn_aggr(v_ag[:, 2 * r:2 * r + 2],
                            v_bn6[:, r * 24:(r + 1) * 24])
                s_dve.wait(SC)
                s_gpc.wait(SC)
                SCw.activation(v_rstd[:, r:r + 1], v_ag[:, 2 * r + 1:2 * r + 2],
                               AF.Ln, bias=eps_ap[:, :])
                SCw.activation(v_rstd[:, r:r + 1], v_rstd[:, r:r + 1],
                               AF.Exp, scale=c_nhalf[:, :])
                s_act.wait(VE)
                VEw.tensor_scalar(v_mb[:, r:r + 1], v_ag[:, 2 * r:2 * r + 1],
                                  v_rstd[:, r:r + 1], -1.0, ALU.mult, ALU.mult)

            ffx_hist = []
            ld6 = []
            gread_mark = [0]

            def p6a_tile(qt):
                bi = qt % 2
                if qt >= 2:
                    sem, val = ffx_hist[qt - 2]
                    sem.wait(SY, val)
                sx = s_x2[bi]
                sx.inc(SY.dma_start(xt6[bi][:, :],
                                    x_own[qt * 128:(qt + 1) * 128, :]), 16)
                if qt >= 1:
                    s_dve.wait(SY, gread_mark[0])
                sx.inc(SY.dma_start(gt6s[:, :],
                                    gate_scr[qt * 128:(qt + 1) * 128, :]), 16)
                ld6.append(sx.n)
                sx.wait(VE, ld6[qt])
                ln_stats_vg(xt6[bi], c_eps5, r=bi)
                s_act.wait(VE)
                VEw.tensor_scalar(xn6[bi][:, :], xt6[bi][:, :],
                                  v_rstd[:, bi:bi + 1], v_mb[:, bi:bi + 1],
                                  ALU.mult, ALU.add)
                VEw.tensor_mul(gt6s[:, :], gt6s[:, :], bc_bc[:, :])
                VEw.tensor_add(gt6s[:, :], gt6s[:, :], xn6[bi][:, :])
                ln_stats_vg(gt6s, c_eps6, r=bi)
                s_act.wait(VE)
                VEw.tensor_scalar(xn6[bi][:, :], gt6s[:, :],
                                  v_rstd[:, bi:bi + 1], v_mb[:, bi:bi + 1],
                                  ALU.mult, ALU.add)
                gread_mark[0] = s_dve.n
                VEw.tensor_mul(xn6[bi][:, :], xn6[bi][:, :], goB[:, :])
                VEw.tensor_add(xn6[bi][:, :], xn6[bi][:, :], boB[:, :])
                VEw.tensor_add(xn6[bi][:, :], xn6[bi][:, :], xt6[bi][:, :])
                fin = s_dve.n
                s_dve.wait(SY, fin)
                s_st2[bi].inc(SY.dma_start(
                    ffx_scr[qt * 128:(qt + 1) * 128, :], xn6[bi][:, :]), 16)
                ffx_hist.append((s_st2[bi], s_st2[bi].n))

            dve_pre = s_dve.n
            pop_user = [0, 0]

            def p5_og_block(og):
                bi = og % 2
                h = og // 2
                if og == 0:
                    s_dve.wait(TE, dve_pre)
                for qt in range(NT):
                    pi = qt % 2
                    s_ow2[bi].wait(TE, ow_hist[og])
                    if len(st_hist) >= 2:
                        sem, val = st_hist[-2]
                        sem.wait(TE, val)
                    s_dve.wait(TE, pop_user[pi])
                    for ch in range(NKH):
                        s_pe.inc(TE.matmul(
                            pop[pi][:, :],
                            combT[:, ch * L + qt * 128:ch * L + (qt + 1) * 128],
                            wob[bi][:, ch * 512:(ch + 1) * 512],
                            start=(ch == 0), stop=(ch == NKH - 1)))
                    s_pe.wait(VE)
                    VEw.tensor_copy(postg[pi][:, :], pop[pi][:, :])
                    pop_user[pi] = s_dve.n
                    s_dve.wait(SY, pop_user[pi])
                    s_po8[og * 2 + pi].inc(SY.dma_start(
                        po_scr[og, qt * 128:(qt + 1) * 128, :],
                        postg[pi][:, :]), 16)
                    st_hist.append((s_po8[og * 2 + pi], s_po8[og * 2 + pi].n))
                pe_og.append(s_pe.n)

            def og_prefetch(og):
                bi = og % 2
                s_pe.wait(SY, pe_og[og - 2])
                s_ow2[bi].inc(SY.dma_start(
                    wob[bi][:, :].rearrange("p (c n) -> p c n", c=8),
                    wos_r[:, :, og * 512:(og + 1) * 512]), 16)
                ow_hist.append(s_ow2[bi].n)

            def rs_emit(og):
                s_po8[og * 2].wait(GP)
                s_po8[og * 2 + 1].wait(GP)
                s_cc.inc(GP.collective_compute(
                    "ReduceScatter", ALU.add,
                    replica_groups=[[0, 1], [2, 3], [4, 5], [6, 7]],
                    ins=[po_scr[og, :, :].opt()],
                    outs=[ao_scr[og, :, :].opt()]))
                rs_done[og] = s_cc.n

            p5_og_block(0)
            rs_emit(0)
            s_ct.wait(TE, bc_ct)
            s_pe.wait(SC)
            for og in range(4):
                s_dve.wait(TE)
                s_act.wait(TE)
                s_pe.inc(TE.matmul(p6p[0:1, :], oneh4[0:4, :],
                                   s4tmp[0:4, og * 512:(og + 1) * 512],
                                   start=True, stop=True))
                s_pe.wait(SC)
                SCw.copy(row_b[0:1, og * 512:(og + 1) * 512], p6p[0:1, :])
            bcast_rows(p6p, row_b, bc_bc, 128)
            p6a_tile(0)
            p5_og_block(1)
            rs_emit(1)
            p6a_tile(1)
            p6a_tile(2)
            og_prefetch(2)
            p5_og_block(2)
            rs_emit(2)
            p6a_tile(3)
            p6a_tile(4)
            og_prefetch(3)
            p5_og_block(3)
            rs_emit(3)
            p6a_tile(5)
            p6a_tile(6)
            p6a_tile(7)

            it = 0
            ob_hist = [0, 0, 0, 0]
            for half in range(2):
                GP.wait_ge(s_cc.h, rs_done[half * 2 + 1])
                SY.wait_ge(s_cc.h, rs_done[half * 2 + 1])
                for qt in range(8):
                    b = it % 4
                    Q = GP if b % 2 == 0 else SY
                    if it >= 4:
                        s_ob4[b].wait(Q, ob_hist[b])
                    if it < 4:
                        for sem, val in (ffx_hist[6], ffx_hist[7]):
                            sem.wait(Q, val)
                    sem, val = ffx_hist[qt]
                    sem.wait(Q, val)
                    sx = s_fb4[b]
                    sx.inc(Q.dma_start(
                        fin6[b][:, :],
                        ffx_scr[qt * 128:(qt + 1) * 128,
                                half * 1024:(half + 1) * 1024]), 16)
                    sx.inc(Q.dma_start(
                        at6[b][:, 0:512],
                        ao_scr[half * 2, qt * 128:(qt + 1) * 128, :]), 16)
                    sx.inc(Q.dma_start(
                        at6[b][:, 512:1024],
                        ao_scr[half * 2 + 1, qt * 128:(qt + 1) * 128, :]), 16)
                    sx.wait(VE, sx.n)
                    VEw.tensor_add(fin6[b][:, :], fin6[b][:, :], at6[b][:, :])
                    s_dve.wait(Q)
                    s_ob4[b].inc(Q.dma_start(
                        out[qt * 128:(qt + 1) * 128,
                            half * 1024:(half + 1) * 1024], fin6[b][:, :]), 16)
                    ob_hist[b] = s_ob4[b].n
                    it += 1
            for b in range(4):
                s_ob4[b].wait(GP, ob_hist[b])
            if DEBUG:
                GP.wait_ge(s_dbg.h, s_dbg.n)
        pvX.close()

    return nc


_prog_cache = {}


def _get_program():
    if "nc" not in _prog_cache:
        _prog_cache["nc"] = build_program()
    return _prog_cache["nc"]


def _consts():
    pos = np.arange(L, dtype=np.float64)
    inv = 1.0 / (10000.0 ** (np.arange(0, DH, 2, dtype=np.float64) / DH))
    fr = np.outer(pos, inv)
    emb = np.concatenate([fr, fr], axis=1)
    cos_fm = np.ascontiguousarray(np.cos(emb).T.astype(np.float32))
    sin_fm = np.sin(emb).T.astype(np.float32)
    sinS = np.ascontiguousarray(
        np.concatenate([-sin_fm[:64], sin_fm[64:]], axis=0).astype(np.float32))
    identity = np.eye(128).astype(ml_dtypes.bfloat16)
    identityf = np.eye(128).astype(np.float32)
    kk, qq = np.meshgrid(np.arange(128), np.arange(128), indexing="ij")
    mask = (kk <= qq).astype(ml_dtypes.bfloat16)
    return cos_fm, sinS, identity, identityf, mask


def _core_inputs(ii, c, consts):
    x = ii["L0_attn_x"]
    cos_fm, sinS, identity, identityf, mask = consts
    b, hh, th = c // 2, c % 2, c % 2
    A = np.ascontiguousarray
    return {
        "x_b": A(x[b]),
        "x_own": A(x[b, th * 1024:(th + 1) * 1024]),
        "x_last4": A(x[:, L - 1, :]),
        "wdkv": A(ii["L0_attn_W_DKV"]),
        "wuk": A(ii["L0_attn_W_UK"][:, hh * 1024:(hh + 1) * 1024]),
        "wuv": A(ii["L0_attn_W_UV"][:, hh * 1024:(hh + 1) * 1024]),
        "wdq": A(ii["L0_attn_W_DQ"]),
        "wuq": A(ii["L0_attn_W_UQ"][:, hh * 2048:(hh + 1) * 2048]),
        "wlam": A(ii["L0_attn_W_lam"][:, hh * 8:(hh + 1) * 8]),
        "blam": A(ii["L0_attn_b_lam"][hh * 8:(hh + 1) * 8]),
        "wout": A(ii["L0_attn_W_out"][hh * 1024:(hh + 1) * 1024, :]),
        "gatew": A(ii["L0_ffn_gate_w"]),
        "gateb": A(ii["L0_ffn_gate_b"]),
        "aggw": A(ii["L0_ffn_agg_w"][:, c * 256:(c + 1) * 256]),
        "aggb": A(ii["L0_ffn_agg_b"][c * 256:(c + 1) * 256]),
        "bcw": A(ii["L0_ffn_bc_w"][:, c * 256:(c + 1) * 256]),
        "bcb": A(ii["L0_ffn_bc_b"][c * 256:(c + 1) * 256]),
        "d1w": A(ii["L0_ffn_d1_w"][:, c * 1024:(c + 1) * 1024]),
        "d1b": A(ii["L0_ffn_d1_b"][c * 1024:(c + 1) * 1024]),
        "d2w": A(ii["L0_ffn_d2_w"][c * 1024:(c + 1) * 1024, :]),
        "d2b": A(ii["L0_ffn_d2_b"]),
        "ing": A(ii["L0_ffn_in_gamma"]), "inb": A(ii["L0_ffn_in_beta"]),
        "png": A(ii["L0_ffn_pn_gamma"]), "pnb": A(ii["L0_ffn_pn_beta"]),
        "outg": A(ii["L0_ffn_out_gamma"]), "outb": A(ii["L0_ffn_out_beta"]),
        "cosk": cos_fm, "sinkS": sinS,
        "ident": identity, "identf": identityf, "maskdiag": mask,
        "onehot4": A(np.eye(4, dtype=np.float32)[:, b:b + 1]),
    }


def kernel(**inputs):
    ii = {k: np.asarray(v) for k, v in inputs.items()}
    consts = _consts()
    nc = _get_program()
    in_maps = [_core_inputs(ii, c, consts) for c in range(8)]
    res = run_bass_kernel_spmd(nc, in_maps, core_ids=list(range(8)))
    y = np.empty((B, L, D), dtype=np.float32)
    for c in range(8):
        b, th = c // 2, c % 2
        y[b, th * 1024:(th + 1) * 1024, :] = np.asarray(res.results[c]["out"])
    return y


# version 28
# speedup vs baseline: 1.0194x; 1.0194x over previous
import numpy as np
import ml_dtypes
from contextlib import ExitStack

import concourse.bass as bass
import concourse.mybir as mybir
from concourse.bass_utils import run_bass_kernel_spmd

F32 = mybir.dt.float32
BF16 = mybir.dt.bfloat16
AF = mybir.ActivationFunctionType
ALU = mybir.AluOpType
AX = mybir.AxisListType
F32R = mybir.dt.float32r


def r32(ap):
    return ap.bitcast(F32R)

B, L, D = 4, 2048, 2048
NT = 16
NKH, DH = 8, 128
NQH = 16
QSC = float(1.0 / np.sqrt(128.0))

DEBUG = False


class Sem:
    def __init__(self, stack, nc, name):
        self.h = stack.enter_context(nc.semaphore(name))
        self.n = 0

    def inc(self, instr, k=1):
        instr.then_inc(self.h, k)
        self.n += k
        return instr

    def wait(self, eng, val=None):
        v = self.n if val is None else val
        assert v <= self.n, f"sem wait {v} > emitted {self.n}"
        if v > 0:
            eng.wait_ge(self.h, v)


def build_program():
    nc = bass.Bass("TRN2", target_bir_lowering=False)
    TE, SC, VE, SY, GP = nc.tensor, nc.scalar, nc.vector, nc.sync, nc.gpsimd

    def din(name, shape, dt=F32):
        return nc.dram_tensor(name, shape, dt, kind="ExternalInput").ap()

    def dscr(name, shape, dt=F32):
        kind = "ExternalOutput" if DEBUG else "Internal"
        return nc.dram_tensor(name, shape, dt, kind=kind).ap()

    x_b = din("x_b", [L, D])
    x_own = din("x_own", [L // 2, D])
    x_last4 = din("x_last4", [4, D])
    wdkv = din("wdkv", [D, 512])
    wuk = din("wuk", [512, NKH * DH])
    wuv = din("wuv", [512, NKH * DH])
    wdq = din("wdq", [D, 1024])
    wuq = din("wuq", [1024, NQH * DH])
    wlam = din("wlam", [D, NKH])
    blam = din("blam", [NKH])
    wout = din("wout", [NKH * DH, D])
    gatew = din("gatew", [D, D])
    gateb = din("gateb", [D])
    aggw = din("aggw", [D, 256])
    aggb = din("aggb", [256])
    bcw = din("bcw", [D, 256])
    bcb = din("bcb", [256])
    ing = din("ing", [D]); inb = din("inb", [D])
    png = din("png", [D]); pnb = din("pnb", [D])
    outg = din("outg", [D]); outb = din("outb", [D])
    d1w = din("d1w", [D, 1024])
    d1b = din("d1b", [1024])
    d2w = din("d2w", [1024, D])
    d2b = din("d2b", [D])
    cosk = din("cosk", [128, L])
    sinkS = din("sinkS", [128, L])
    ident = din("ident", [128, 128], BF16)
    identf = din("identf", [128, 128], F32)
    maskdiag = din("maskdiag", [128, 128], BF16)
    onehot4 = din("onehot4", [4, 1])

    out = nc.dram_tensor("out", [L // 2, D], F32, kind="ExternalOutput").ap()

    k_scr = dscr("k_scr", [NKH, 128, L], BF16)
    q_scr = dscr("q_scr", [NQH, 128, L], BF16)
    gate_scr = dscr("gate_scr", [L // 2, D])
    po_scr = dscr("po_scr", [4, L, 512])
    ao_scr = dscr("ao_scr", [4, L // 2, 512])
    cc_tin = dscr("cc_tin", [4, D]) if DEBUG else nc.dram_tensor("cc_tin", [4, D], F32).ap()
    cc_tout = dscr("cc_tout", [4, D]) if DEBUG else nc.dram_tensor("cc_tout", [4, D], F32).ap()
    cc_ag = nc.dram_tensor("cc_ag", [32, 256], F32).ap()
    cc_ag2 = nc.dram_tensor("cc_ag2", [32, 256], F32).ap()
    cc_tps = nc.dram_tensor("cc_tps", [4, 256], F32).ap()
    ckv_scr = nc.dram_tensor("ckv_scr", [128, 4 * L], BF16).ap()
    ffx_scr = nc.dram_tensor("ffx_scr", [L // 2, D], F32).ap()
    d1w_scr = nc.dram_tensor("d1w_scr", [D, 1024], BF16).ap()
    d2w_scr = nc.dram_tensor("d2w_scr", [1024, D], BF16).ap()
    wout_scr = nc.dram_tensor("wout_scr", [1024, D], BF16).ap()
    if DEBUG:
        th_dbg = nc.dram_tensor("th_dbg", [6, 4, D], F32, kind="ExternalOutput").ap()
        tln_dbg = nc.dram_tensor("tln_dbg", [4, D], F32, kind="ExternalOutput").ap()
        t2_dbg = nc.dram_tensor("t2_dbg", [128, 32], F32, kind="ExternalOutput").ap()
        cin_dbg = nc.dram_tensor("cin_dbg", [4, 4, D], F32, kind="ExternalOutput").ap()
        cout_dbg = nc.dram_tensor("cout_dbg", [4, 4, D], F32, kind="ExternalOutput").ap()
        tln_dbg2 = nc.dram_tensor("tln_dbg2", [4, 4, D], F32, kind="ExternalOutput").ap()
        s_dbg = Sem if False else None

    st = ExitStack()
    with st:
        E = st.enter_context
        s_pe = Sem(st, nc, "s_pe")
        s_act = Sem(st, nc, "s_act")
        s_dve = Sem(st, nc, "s_dve")
        s_pl = Sem(st, nc, "s_pl")
        s_cc = Sem(st, nc, "s_cc")
        s_lc = Sem(st, nc, "s_lc")
        s_row = Sem(st, nc, "s_row")
        s_x2 = [Sem(st, nc, f"s_x{i}") for i in range(2)]
        s_st2 = [Sem(st, nc, f"s_s{i}") for i in range(2)]
        s_gl = Sem(st, nc, "s_gl")
        s_wk2 = [Sem(st, nc, f"s_w{i}") for i in range(2)]
        s_rot = Sem(st, nc, "s_rot")
        s_rt4 = [Sem(st, nc, f"s_rt{i}") for i in range(4)]
        s_kb2 = [Sem(st, nc, f"s_k{i}") for i in range(2)]
        s_qb2 = [Sem(st, nc, f"s_q{i}") for i in range(2)]
        s_lam = Sem(st, nc, "s_lam")
        s_sp = Sem(st, nc, "s_sp")
        s_ct = Sem(st, nc, "s_ct")
        s_dbg = Sem(st, nc, "s_dbg")
        s_out = Sem(st, nc, "s_out")
        s_res = Sem(st, nc, "s_res")
        s_d1s = Sem(st, nc, "s_d1s")
        s_d2s = Sem(st, nc, "s_d2s")
        s_rg = Sem(st, nc, "s_rg")
        s_gpc = Sem(st, nc, "s_gpc")
        s_po8 = [Sem(st, nc, f"s_po{i}") for i in range(8)]
        s_ow2 = [Sem(st, nc, f"s_ow{i}") for i in range(2)]
        s_fb4 = [Sem(st, nc, f"s_fb{i}") for i in range(4)]
        s_ob4 = [Sem(st, nc, f"s_ob{i}") for i in range(4)]

        _block = E(nc.Block())

        class _EngWrap:
            def __init__(self, eng, sem):
                self._eng = eng
                self._sem = sem

            def __getattr__(self, name):
                f = getattr(self._eng, name)
                sem, eng = self._sem, self._eng

                def g(*a, **k):
                    sem.wait(eng)
                    return sem.inc(f(*a, **k))
                return g

        VEw = _EngWrap(VE, s_dve)
        SCw = _EngWrap(SC, s_act)
        GPw = _EngWrap(GP, s_gpc)

        def recip_acc(out, in_, scratch):
            SCw.activation(scratch, in_, AF.Ln)
            freed = s_act.n
            SCw.activation(out, scratch, AF.Exp, scale=c_neg1[0:1, :])
            return freed, s_act.n

        ident_s = E(nc.sbuf_tensor("ident_s", [128, 128], BF16, side="left"))
        identf_s = E(nc.sbuf_tensor("identf_s", [128, 128], F32, side="left"))
        mask_s = E(nc.sbuf_tensor("mask_s", [128, 128], BF16, side="left"))
        ones_rf = E(nc.sbuf_tensor("ones_rf", [1, 128], F32, side="left"))
        ones_rb = E(nc.sbuf_tensor("ones_rb", [1, 128], BF16, side="left"))
        row_gb = E(nc.sbuf_tensor("row_gb", [1, D], BF16, side="left"))
        row_ab = row_gb
        ones_cb = E(nc.sbuf_tensor("ones_cb", [128, 1], BF16, side="left"))
        oneh4 = E(nc.sbuf_tensor("oneh4", [4, 1], F32, side="left"))
        c_eps5 = E(nc.sbuf_tensor("c_eps5", [128, 1], F32, side="left"))
        c_eps6 = E(nc.sbuf_tensor("c_eps6", [128, 1], F32, side="left"))
        c_half = E(nc.sbuf_tensor("c_half", [128, 1], F32, side="left"))
        c_nhalf = E(nc.sbuf_tensor("c_nhalf", [128, 1], F32, side="left"))
        c_n102 = E(nc.sbuf_tensor("c_n102", [128, 1], F32, side="left"))
        c_neg1 = E(nc.sbuf_tensor("c_neg1", [128, 1], F32, side="left"))
        c_qsc = E(nc.sbuf_tensor("c_qsc", [128, 1], F32, side="left"))
        lamT = E(nc.sbuf_tensor("lamT", [NKH, L], F32, side="left"))
        negB = E(nc.sbuf_tensor("negB", [128, NQH], F32, side="left"))
        mk2 = E(nc.sbuf_tensor("mk2", [1, NKH], F32, side="left"))
        mq2 = E(nc.sbuf_tensor("mq2", [1, NQH], F32, side="left"))
        m4t = E(nc.sbuf_tensor("m4t", [1, 4], F32, side="left"))
        v_bnG = E(nc.sbuf_tensor("v_bnG", [128, 48], F32, side="left"))
        v_agG = E(nc.sbuf_tensor("v_agG", [128, 4], F32, side="left"))
        sc11 = E(nc.sbuf_tensor("sc11", [1, 1], F32, side="left"))
        thought = E(nc.sbuf_tensor("thought", [4, D], F32, side="left"))
        png_s = E(nc.sbuf_tensor("png_s", [128, NT], F32, side="left"))
        pnb_s = E(nc.sbuf_tensor("pnb_s", [128, NT], F32, side="left"))
        d1b_s = E(nc.sbuf_tensor("d1b_s", [128, 8], F32, side="left"))
        xnl_fm = E(nc.sbuf_tensor("xnl_fm", [128, NT * 4], BF16, side="left"))
        row_a = E(nc.sbuf_tensor("row_a", [1, D], F32, side="left"))
        row_b = E(nc.sbuf_tensor("row_b", [1, D], F32, side="left"))
        s4tmp = E(nc.sbuf_tensor("s4tmp", [4, D], F32, side="left"))
        tln_f = E(nc.sbuf_tensor("tln_f", [4, D], F32, side="left"))
        bc_bc = E(nc.sbuf_tensor("bc_bc", [128, D], F32, side="left"))
        v_sum = E(nc.sbuf_tensor("v_sum", [128, 4], F32, side="left"))
        v_ssq = E(nc.sbuf_tensor("v_ssq", [128, 4], F32, side="left"))
        v_var = E(nc.sbuf_tensor("v_var", [128, 4], F32, side="left"))
        v_rstd = E(nc.sbuf_tensor("v_rstd", [128, 4], F32, side="left"))
        v_mb = E(nc.sbuf_tensor("v_mb", [128, 4], F32, side="left"))

        for t, val in ((ones_rf, 1.0), (ones_rb, 1.0),
                       (ones_cb, 1.0), (c_eps5, 1e-5),
                       (c_eps6, 1e-6), (c_half, 0.5), (c_nhalf, -0.5),
                       (c_n102, -1.02), (c_neg1, -1.0), (c_qsc, QSC)):
            s_pl.inc(GP.memset(t[:, :], val))
        s_lc.inc(SY.dma_start(ident_s[:, :], ident[:, :]), 16)
        s_lc.inc(SY.dma_start(identf_s[:, :], identf[:, :]), 16)
        s_lc.inc(SY.dma_start(mask_s[:, :], maskdiag[:, :]), 16)
        s_lc.inc(SY.dma_start(oneh4[:, :], onehot4[:, :]), 16)
        s_lc.inc(SY.dma_start(d1b_s[:, :], d1b.rearrange("(j p) -> p j", p=128),
                              allow_slow_non_contiguous=True), 16)
        s_lc.inc(SY.dma_start(png_s[:, :], png.rearrange("(j p) -> p j", p=128),
                              allow_slow_non_contiguous=True), 16)
        s_lc.inc(SY.dma_start(pnb_s[:, :], pnb.rearrange("(j p) -> p j", p=128),
                              allow_slow_non_contiguous=True), 16)
        lc_consts = s_lc.n

        def row1(ap1d):
            return ap1d.rearrange("(a d) -> a d", a=1)

        def ln_stats(tile, sq, n, eps_ap, r=0):
            vs = v_sum[0:n, r:r + 1]
            vq = v_ssq[0:n, r:r + 1]
            vv = v_var[0:n, r:r + 1]
            vr = v_rstd[0:n, r:r + 1]
            vm = v_mb[0:n, r:r + 1]
            s_act.wait(VE)
            VEw.reduce_sum(vs, tile[0:n, 0:D], axis=AX.X)
            s_dve.wait(SC)
            SCw.activation(sq[0:n, 0:D], tile[0:n, 0:D], AF.Square, accum_out=vq)
            s_act.wait(VE)
            VEw.tensor_scalar_mul(vs, vs, 1.0 / D)
            VEw.tensor_scalar_mul(vv, vq, 1.0 / D)
            VEw.tensor_scalar(vq, vs, vs, None, ALU.mult)
            VEw.tensor_sub(vv, vv, vq)
            s_dve.wait(SC)
            SCw.activation(vr, vv, AF.Ln, bias=eps_ap[0:n, :])
            SCw.activation(vr, vr, AF.Exp, scale=c_nhalf[0:n, :])
            s_act.wait(VE)
            VEw.tensor_scalar(vm, vs, vr, -1.0, ALU.mult, ALU.mult)

        def ln_stats_bn(tile, eps_ap, r):
            s_act.wait(VE)
            for k in range(4):
                VEw.bn_stats(v_bnG[:, r * 24 + k * 6:r * 24 + (k + 1) * 6],
                             tile[:, k * 512:(k + 1) * 512])
            VEw.bn_aggr(v_agG[:, 2 * r:2 * r + 2], v_bnG[:, r * 24:(r + 1) * 24])
            s_dve.wait(SC)
            SCw.activation(v_rstd[:, r:r + 1], v_agG[:, 2 * r + 1:2 * r + 2],
                           AF.Ln, bias=eps_ap[:, :])
            SCw.activation(v_rstd[:, r:r + 1], v_rstd[:, r:r + 1],
                           AF.Exp, scale=c_nhalf[:, :])
            s_act.wait(VE)
            VEw.tensor_scalar(v_mb[:, r:r + 1], v_agG[:, 2 * r:2 * r + 1],
                              v_rstd[:, r:r + 1], -1.0, ALU.mult, ALU.mult)

        def bcast_rows(ps, row_src, dst, n):
            for og in range(4):
                s_pe.wait(TE)
                s_act.wait(TE)
                s_dve.wait(TE)
                s_pe.inc(TE.matmul(ps[0:n, 0:512], ones_rf[0:1, 0:n],
                                   row_src[0:1, og * 512:(og + 1) * 512],
                                   start=True, stop=True))
                s_pe.wait(SC)
                SCw.copy(dst[0:n, og * 512:(og + 1) * 512], ps[0:n, 0:512])

        with ExitStack() as p0:
            E0 = p0.enter_context
            x4 = E0(nc.sbuf_tensor("x4", [4, D], F32, side="right"))
            xn4 = E0(nc.sbuf_tensor("xn4", [4, D], F32, side="right"))
            g4 = E0(nc.sbuf_tensor("g4", [4, D], F32, side="right"))
            b4 = E0(nc.sbuf_tensor("b4", [4, D], F32, side="right"))
            xnl_bf = E0(nc.sbuf_tensor("xnl_bf", [4, D], BF16, side="right"))
            ps4 = E0(nc.psum_tensor("ps4", [4, 512], F32))
            pstr0 = E0(nc.psum_tensor("pstr0", [128, NT * 4], BF16))

            s_x2[0].inc(SY.dma_start(x4[:, :], x_last4[:, :]), 16)
            s_row.inc(SY.dma_start(row_a[:, :], row1(ing)), 16)
            s_row.inc(SY.dma_start(row_b[:, :], row1(inb)), 16)
            for eng in (VE, SC, TE):
                s_lc.wait(eng)
                s_row.wait(eng)
                s_x2[0].wait(eng)
                s_pl.wait(eng)
            bcast_rows(ps4, row_a, g4, 4)
            bcast_rows(ps4, row_b, b4, 4)
            ln_stats(x4, xn4, 4, c_eps5)
            s_dve.wait(SC)
            SCw.activation(xn4[0:4, :], x4[0:4, :], AF.Identity,
                                    scale=v_rstd[0:4, 0:1], bias=v_mb[0:4, 0:1])
            ln_stats(xn4, tln_f, 4, c_eps6)
            s_dve.wait(SC)
            SCw.activation(tln_f[0:4, :], xn4[0:4, :], AF.Identity,
                                    scale=v_rstd[0:4, 0:1], bias=v_mb[0:4, 0:1])
            s_act.wait(VE)
            VEw.tensor_mul(tln_f[0:4, :], tln_f[0:4, :], g4[0:4, :])
            VEw.tensor_add(tln_f[0:4, :], tln_f[0:4, :], b4[0:4, :])
            VEw.tensor_copy(xnl_bf[0:4, :], tln_f[0:4, :])
            s_dve.wait(TE)
            for ch in range(NT):
                s_pe.inc(TE.transpose(pstr0[:, ch * 4:(ch + 1) * 4],
                                      xnl_bf[0:4, ch * 128:(ch + 1) * 128],
                                      ident_s[0:4, 0:4]))
            s_pe.wait(SC)
            SCw.copy(xnl_fm[:, :], pstr0[:, :])

        with ExitStack() as p13:
            E1 = p13.enter_context
            xn_fm = E1(nc.sbuf_tensor("xn_fm", [128, NT * L], BF16, side="right"))
            cosk_s = E1(nc.sbuf_tensor("cosk_s", [128, L], F32, side="right"))
            sinkS_s = E1(nc.sbuf_tensor("sinkS_s", [128, L], F32, side="right"))
            blam_s = E1(nc.sbuf_tensor("blam_s", [NKH, 1], F32, side="right"))
            s_act.wait(SY)
            s_lc.inc(SY.dma_start(cosk_s[:, :], cosk[:, :]), 16)
            s_lc.inc(SY.dma_start(sinkS_s[:, :], sinkS[:, :]), 16)
            s_lc.inc(SY.dma_start(blam_s[:, :],
                                  blam.rearrange("(p a) -> p a", a=1)), 16)
            lc_all = s_lc.n

            with ExitStack() as p1:
                E2 = p1.enter_context
                xt = [E2(nc.sbuf_tensor(f"xt{i}", [128, D], F32, side="right"))
                      for i in range(2)]
                xnbf = [E2(nc.sbuf_tensor(f"xnbf{i}", [128, D], BF16, side="right"))
                        for i in range(2)]
                trp = [E2(nc.psum_tensor(f"trp{i}", [128, 512], BF16))
                       for i in range(2)]
                ld_hist, act_xn, pe_tile, evac_hist = [], [], [], []
                s_x2[0].inc(SY.dma_start(xt[0][:, :], x_b[0:128, :]), 16)
                ld_hist.append(s_x2[0].n)
                for tt in range(NT):
                    bi = tt % 2
                    if tt + 1 < NT:
                        if tt >= 1:
                            s_act.wait(SY, act_xn[tt - 1])
                        s_x2[(tt + 1) % 2].inc(
                            SY.dma_start(xt[(tt + 1) % 2][:, :],
                                         x_b[(tt + 1) * 128:(tt + 2) * 128, :]), 16)
                        ld_hist.append(s_x2[(tt + 1) % 2].n)
                    s_x2[bi].wait(VE, ld_hist[tt])
                    s_x2[bi].wait(SC, ld_hist[tt])
                    ln_stats_bn(xt[bi], c_eps5, r=bi)
                    s_dve.wait(SC)
                    if tt >= 2:
                        s_pe.wait(SC, pe_tile[tt - 2])
                    for grp in range(4):
                        SCw.activation(xnbf[bi][:, grp * 512:(grp + 1) * 512],
                                       xt[bi][:, grp * 512:(grp + 1) * 512],
                                       AF.Identity,
                                       scale=v_rstd[:, bi:bi + 1],
                                       bias=v_mb[:, bi:bi + 1])
                        s_act.wait(TE)
                        if len(evac_hist) >= 2:
                            s_dve.wait(TE, evac_hist[-2])
                        for j in range(4):
                            ch = grp * 4 + j
                            s_pe.inc(TE.transpose(
                                trp[grp % 2][:, j * 128:(j + 1) * 128],
                                xnbf[bi][:, ch * 128:(ch + 1) * 128], ident_s[:, :]))
                        s_pe.wait(VE)
                        dst = xn_fm[:, :].rearrange("p (c t) -> p c t", c=NT)[
                            :, grp * 4:(grp + 1) * 4, tt * 128:(tt + 1) * 128]
                        src = trp[grp % 2][:, :].rearrange("p (c t) -> p c t", c=4)
                        VEw.tensor_copy(dst, src)
                        evac_hist.append(s_dve.n)
                    act_xn.append(s_act.n)
                    pe_tile.append(s_pe.n)
            xn_ready_dve = s_dve.n
            xn_ready = s_act.n
            pvX = ExitStack()

            with ExitStack() as p3:
                E2 = p3.enter_context
                cq_fm = E2(nc.sbuf_tensor("cq_fm", [128, 8 * L], BF16, side="right"))
                pre = E2(nc.sbuf_tensor("pre3", [128, L], F32, side="right"))
                rot = E2(nc.sbuf_tensor("rot3", [128, L], F32, side="right"))
                rq_bf = E2(nc.sbuf_tensor("rq_bf", [128, L], BF16, side="right"))
                rsq = E2(nc.sbuf_tensor("rsq3", [128, L], BF16, side="right"))
                wlam_bf = E2(nc.sbuf_tensor("wlam_bf", [128, NT * NKH], BF16,
                                            side="right"))
                wqb = [E2(nc.sbuf_tensor(f"wqb{i}", [128, NT * 128], BF16,
                                         side="right")) for i in range(2)]
                mmp = [E2(nc.psum_tensor(f"mm3{i}", [128, 512], F32)) for i in range(2)]
                nrmp = E2(nc.psum_tensor("nrmp3", [1, 512], F32))

                s_lam.inc(GP.dma_start(
                    wlam_bf[:, :].rearrange("p (c n) -> p c n", c=NT),
                    wlam.rearrange("(c p) n -> p c n", p=128)), 16)
                lam_w = s_lam.n
                wdq_r = wdq.rearrange("(c p) n -> p c n", p=128)

                mm_user = [0, 0]
                wm_hist, pe_m = [], []
                for m in range(8):
                    bi = m % 2
                    if m >= 2:
                        s_pe.wait(GP, pe_m[m - 2])
                    s_wk2[bi].inc(GP.dma_start(
                        wqb[bi][:, :].rearrange("p (c n) -> p c n", c=NT),
                        wdq_r[:, :, m * 128:(m + 1) * 128]), 16)
                    wm_hist.append(s_wk2[bi].n)
                    for tg in range(4):
                        pi = (m * 4 + tg) % 2
                        if m == 0 and tg == 0:
                            s_act.wait(TE, xn_ready)
                            s_dve.wait(TE, xn_ready_dve)
                            s_dve.wait(TE)
                        s_wk2[bi].wait(TE, wm_hist[m])
                        s_act.wait(TE, mm_user[pi])
                        for ch in range(NT):
                            s_pe.inc(TE.matmul(
                                mmp[pi][:, :],
                                wqb[bi][:, ch * 128:(ch + 1) * 128],
                                xn_fm[:, ch * L + tg * 512:ch * L + (tg + 1) * 512],
                                start=(ch == 0), stop=(ch == NT - 1)))
                        s_pe.wait(SC)
                        SCw.copy(
                            cq_fm[:, m * L + tg * 512:m * L + (tg + 1) * 512],
                            mmp[pi][:, :])
                        mm_user[pi] = s_act.n
                    pe_m.append(s_pe.n)

                wuq_r = wuq.rearrange("(c p) n -> p c n", p=128)
                uq_hist, pe_qh = [], []
                q_spill = 0
                nrm_pe = 0
                pre_free = [0] * 4
                rot_free = [0] * 4
                for qh in range(NQH):
                    bi = qh % 2
                    if qh >= 2:
                        s_pe.wait(GP, pe_qh[qh - 2])
                    else:
                        s_pe.wait(GP, pe_m[qh - 2])
                    s_wk2[bi].inc(GP.dma_start(
                        wqb[bi][:, :].rearrange("p (c n) -> p c n", c=NT)[:, 0:8, :],
                        wuq_r[:, :, qh * 128:(qh + 1) * 128]), 16)
                    uq_hist.append(s_wk2[bi].n)
                    if qh < 8:
                        s_res.inc(GP.dma_start(
                            d1w_scr[qh * 256:(qh + 1) * 256, :],
                            d1w[qh * 256:(qh + 1) * 256, :]), 16)
                        s_res.inc(GP.dma_start(
                            d2w_scr[qh * 128:(qh + 1) * 128, :],
                            d2w[qh * 128:(qh + 1) * 128, :]), 16)
                    elif qh < 12:
                        s_res.inc(GP.dma_start(
                            wout_scr[(qh - 8) * 256:(qh - 7) * 256, :],
                            wout[(qh - 8) * 256:(qh - 7) * 256, :]), 16)
                    for tg in range(4):
                        pi = tg % 2
                        lo, hi = tg * 512, (tg + 1) * 512
                        s_wk2[bi].wait(TE, uq_hist[qh])
                        s_act.wait(TE, mm_user[pi])
                        for c in range(8):
                            s_pe.inc(TE.matmul(
                                mmp[pi][:, :],
                                wqb[bi][:, c * 128:(c + 1) * 128],
                                cq_fm[:, c * L + lo:c * L + hi],
                                start=(c == 0), stop=(c == 7)))
                        s_pe.wait(SC)
                        s_dve.wait(SC, pre_free[tg])
                        SCw.activation(pre[:, lo:hi], mmp[pi][:, :],
                                       AF.Identity, scale=c_qsc[:, :])
                        mm_user[pi] = s_act.n
                        evn = s_act.n
                        s_act.wait(GP, evn)
                        s_dve.wait(GP, rot_free[tg])
                        s_rt4[tg].inc(GP.dma_start(rot[0:64, lo:hi],
                                                   pre[64:128, lo:hi]), 16)
                        s_rt4[tg].inc(GP.dma_start(rot[64:128, lo:hi],
                                                   pre[0:64, lo:hi]), 16)
                        rotw = s_rt4[tg].n
                        s_act.wait(VE, evn)
                        if qh == 0 and tg == 0:
                            s_lc.wait(VE, lc_all)
                        s_rt4[tg].wait(VE, rotw)
                        VEw.tensor_mul(pre[:, lo:hi], pre[:, lo:hi],
                                       cosk_s[:, lo:hi])
                        pre_free[tg] = s_dve.n
                        VEw.tensor_mul(rot[:, lo:hi], rot[:, lo:hi],
                                       sinkS_s[:, lo:hi])
                        rot_free[tg] = s_dve.n
                        if tg == 0:
                            s_sp.wait(VE, q_spill)
                        VEw.tensor_tensor(rq_bf[:, lo:hi], pre[:, lo:hi],
                                          rot[:, lo:hi], ALU.add)
                        if tg == 0:
                            s_pe.wait(VE, nrm_pe)
                        VEw.tensor_mul(rsq[:, lo:hi], rq_bf[:, lo:hi],
                                       rq_bf[:, lo:hi])
                    pe_qh.append(s_pe.n)
                    qdone = s_dve.n
                    s_dve.wait(GP, qdone)
                    s_sp.inc(GP.dma_start(q_scr[qh, :, :], rq_bf[:, :]), 16)
                    q_spill = s_sp.n
                    s_dve.wait(TE, qdone)
                    for tg in range(4):
                        s_pe.inc(TE.matmul(nrmp[0:1, :], ones_cb[:, :],
                                           rsq[:, tg * 512:(tg + 1) * 512],
                                           start=(tg == 0), stop=(tg == 3),
                                           skip_group_check=True))
                    nrm_pe = s_pe.n
                    s_pe.wait(VE, nrm_pe)
                    VEw.reduce_max(mq2[0:1, qh:qh + 1], nrmp[0:1, :],
                                   axis=AX.X)

                for tg in range(4):
                    s_act.wait(TE)
                    s_dve.wait(TE)
                    s_lam.wait(TE, lam_w)
                    for ch in range(NT):
                        s_pe.inc(TE.matmul(
                            mmp[0][0:NKH, :],
                            wlam_bf[:, ch * NKH:(ch + 1) * NKH],
                            xn_fm[:, ch * L + tg * 512:ch * L + (tg + 1) * 512],
                            start=(ch == 0), stop=(ch == NT - 1)))
                    s_pe.wait(SC)
                    s_lc.wait(SC, lc_all)
                    SCw.activation(lamT[0:NKH, tg * 512:(tg + 1) * 512],
                                            mmp[0][0:NKH, :], AF.Sigmoid,
                                            bias=blam_s[0:NKH, :])

            with ExitStack() as p2:
                E2 = p2.enter_context
                ckv_fm = E2(nc.sbuf_tensor("ckv_fm", [128, 4 * L], BF16, side="right"))
                wuk_bf = E2(nc.sbuf_tensor("wuk_bf", [128, 4 * 1024], BF16, side="right"))
                wkvb = [E2(nc.sbuf_tensor(f"wkvb{i}", [128, NT * 128], BF16,
                                          side="right")) for i in range(2)]
                pre = E2(nc.sbuf_tensor("pre", [128, L], F32, side="right"))
                rot = E2(nc.sbuf_tensor("rot", [128, L], F32, side="right"))
                rk_bf = E2(nc.sbuf_tensor("rk_bf", [128, L], BF16, side="right"))
                rsq = E2(nc.sbuf_tensor("rsq", [128, L], BF16, side="right"))
                mmp = [E2(nc.psum_tensor(f"mmp{i}", [128, 512], F32)) for i in range(2)]
                nrmp = E2(nc.psum_tensor("nrmp", [1, 512], F32))

                s_gl.inc(GP.dma_start(
                    wuk_bf[:, :].rearrange("p (c n) -> p c n", c=4),
                    wuk.rearrange("(c p) n -> p c n", p=128)), 16)
                wready = s_gl.n
                wdkv_r = wdkv.rearrange("(c p) n -> p c n", p=128)

                s_act.wait(TE)
                s_dve.wait(TE)
                evh = []
                wm_hist, pe_m = [], []
                for m in range(4):
                    bi = m % 2
                    if m >= 2:
                        s_pe.wait(GP, pe_m[m - 2])
                    else:
                        s_pe.wait(GP)
                    s_wk2[bi].inc(GP.dma_start(
                        wkvb[bi][:, :].rearrange("p (c n) -> p c n", c=NT),
                        wdkv_r[:, :, m * 128:(m + 1) * 128]), 16)
                    wm_hist.append(s_wk2[bi].n)
                    for tg in range(4):
                        pi = (m * 4 + tg) % 2
                        s_wk2[bi].wait(TE, wm_hist[m])
                        if len(evh) >= 2:
                            s_act.wait(TE, evh[-2])
                        for ch in range(NT):
                            s_pe.inc(TE.matmul(
                                mmp[pi][:, :],
                                wkvb[bi][:, ch * 128:(ch + 1) * 128],
                                xn_fm[:, ch * L + tg * 512:ch * L + (tg + 1) * 512],
                                start=(ch == 0), stop=(ch == NT - 1)))
                        s_pe.wait(SC)
                        SCw.copy(
                            ckv_fm[:, m * L + tg * 512:m * L + (tg + 1) * 512],
                            mmp[pi][:, :])
                        evh.append(s_act.n)
                    pe_m.append(s_pe.n)

                k_spill = s_sp.n
                nrm_pe = 0
                pre_free = [0] * 4
                rot_free = [0] * 4
                for kh in range(NKH):
                    for tg in range(4):
                        lo, hi = tg * 512, (tg + 1) * 512
                        if kh == 0 and tg == 0:
                            s_gl.wait(TE, wready)
                        if len(evh) >= 2:
                            s_act.wait(TE, evh[-2])
                        for c in range(4):
                            s_pe.inc(TE.matmul(
                                mmp[tg % 2][:, :],
                                wuk_bf[:, c * 1024 + kh * 128:c * 1024 + (kh + 1) * 128],
                                ckv_fm[:, c * L + lo:c * L + hi],
                                start=(c == 0), stop=(c == 3)))
                        s_pe.wait(SC)
                        s_dve.wait(SC, pre_free[tg])
                        SCw.copy(pre[:, lo:hi], mmp[tg % 2][:, :])
                        evh.append(s_act.n)
                        evn = s_act.n
                        s_act.wait(GP, evn)
                        s_dve.wait(GP, rot_free[tg])
                        s_rt4[tg].inc(GP.dma_start(rot[0:64, lo:hi],
                                                   pre[64:128, lo:hi]), 16)
                        s_rt4[tg].inc(GP.dma_start(rot[64:128, lo:hi],
                                                   pre[0:64, lo:hi]), 16)
                        rotw = s_rt4[tg].n
                        s_act.wait(VE, evn)
                        s_rt4[tg].wait(VE, rotw)
                        VEw.tensor_mul(pre[:, lo:hi], pre[:, lo:hi],
                                       cosk_s[:, lo:hi])
                        pre_free[tg] = s_dve.n
                        VEw.tensor_mul(rot[:, lo:hi], rot[:, lo:hi],
                                       sinkS_s[:, lo:hi])
                        rot_free[tg] = s_dve.n
                        if tg == 0:
                            s_sp.wait(VE, k_spill)
                        VEw.tensor_tensor(rk_bf[:, lo:hi], pre[:, lo:hi],
                                          rot[:, lo:hi], ALU.add)
                        if tg == 0:
                            s_pe.wait(VE, nrm_pe)
                        VEw.tensor_mul(rsq[:, lo:hi], rk_bf[:, lo:hi],
                                       rk_bf[:, lo:hi])
                    kdone = s_dve.n
                    s_dve.wait(GP, kdone)
                    s_sp.inc(GP.dma_start(k_scr[kh, :, :], rk_bf[:, :]), 16)
                    k_spill = s_sp.n
                    s_dve.wait(TE, kdone)
                    for tg in range(4):
                        s_pe.inc(TE.matmul(nrmp[0:1, :], ones_cb[:, :],
                                           rsq[:, tg * 512:(tg + 1) * 512],
                                           start=(tg == 0), stop=(tg == 3),
                                           skip_group_check=True))
                    nrm_pe = s_pe.n
                    s_pe.wait(VE, nrm_pe)
                    VEw.reduce_max(mk2[0:1, kh:kh + 1], nrmp[0:1, :],
                                   axis=AX.X)

                s_act.wait(GP)
                s_sp.inc(GP.dma_start(ckv_scr[:, :], ckv_fm[:, :]), 16)
                ckv_spilled = s_sp.n

                for qh in range(NQH):
                    kh = qh // 2
                    s_act.wait(VE)
                    s_pe.wait(VE)
                    VEw.tensor_tensor(sc11[0:1, :], mq2[0:1, qh:qh + 1],
                                               mk2[0:1, kh:kh + 1], ALU.mult)
                    s_dve.wait(SC)
                    SCw.activation(sc11[0:1, :], sc11[0:1, :], AF.Ln)
                    SCw.activation(sc11[0:1, :], sc11[0:1, :], AF.Exp,
                                            scale=c_half[0:1, :])
                    s_act.wait(TE)
                    s_pe.inc(TE.matmul(mmp[1][:, 0:1], ones_rf[0:1, :],
                                       sc11[0:1, 0:1], start=True, stop=True))
                    s_pe.wait(SC)
                    SCw.activation(negB[:, qh:qh + 1], mmp[1][:, 0:1],
                                            AF.Identity, scale=c_n102[:, :])

        v_full = pvX.enter_context(
            nc.sbuf_tensor("v_full", [128, NKH * NT * 128], BF16, side="left"))
        with ExitStack() as p2b:
            E2 = p2b.enter_context
            ckv2 = E2(nc.sbuf_tensor("ckv2", [128, 4 * L], BF16, side="right"))
            wuv_bf = E2(nc.sbuf_tensor("wuv_bf", [128, 4 * 1024], BF16, side="right"))
            mmp = [E2(nc.psum_tensor(f"mmv{i}", [128, 512], F32)) for i in range(2)]
            s_sp.wait(GP, ckv_spilled)
            s_gl.inc(GP.dma_start(ckv2[:, :], ckv_scr[:, :]), 16)
            s_gl.inc(GP.dma_start(
                wuv_bf[:, :].rearrange("p (c n) -> p c n", c=4),
                wuv.rearrange("(c p) n -> p c n", p=128)), 16)
            vready = s_gl.n
            s_act.wait(TE)
            s_dve.wait(TE)
            evh = []
            for kt in range(NT):
                for og in range(2):
                    pi = (kt * 2 + og) % 2
                    if kt == 0 and og == 0:
                        s_gl.wait(TE, vready)
                    if len(evh) >= 2:
                        s_act.wait(TE, evh[-2])
                    for c in range(4):
                        s_pe.inc(TE.matmul(
                            mmp[pi][:, :],
                            ckv2[:, c * L + kt * 128:c * L + (kt + 1) * 128],
                            wuv_bf[:, c * 1024 + og * 512:c * 1024 + (og + 1) * 512],
                            start=(c == 0), stop=(c == 3)))
                    s_pe.wait(SC)
                    dst = v_full[:, :].rearrange(
                        "p (h t d) -> p h t d", h=NKH, t=NT)[
                        :, og * 4:(og + 1) * 4, kt, :]
                    src = mmp[pi][:, :].rearrange("p (h d) -> p h d", h=4)
                    SCw.copy(dst, src)
                    evh.append(s_act.n)

        with ExitStack() as p35:
            E2 = p35.enter_context
            xno_fm = E2(nc.sbuf_tensor("xno_fm", [128, NT * 1024], BF16, side="right"))
            with ExitStack() as p35a:
                E3 = p35a.enter_context
                xt = [E3(nc.sbuf_tensor(f"xot{i}", [128, D], F32, side="right"))
                      for i in range(2)]
                xnbf = [E3(nc.sbuf_tensor(f"xobf{i}", [128, D], BF16, side="right"))
                        for i in range(2)]
                trp = [E3(nc.psum_tensor(f"otrp{i}", [128, 512], BF16))
                       for i in range(2)]
                ld_hist, act_xn, pe_tile, evac_hist = [], [], [], []
                s_dve.wait(SY)
                s_x2[0].inc(SY.dma_start(xt[0][:, :], x_own[0:128, :]), 16)
                ld_hist.append(s_x2[0].n)
                for tt in range(8):
                    bi = tt % 2
                    if tt + 1 < 8:
                        if tt >= 1:
                            s_act.wait(SY, act_xn[tt - 1])
                        s_x2[(tt + 1) % 2].inc(
                            SY.dma_start(xt[(tt + 1) % 2][:, :],
                                         x_own[(tt + 1) * 128:(tt + 2) * 128, :]), 16)
                        ld_hist.append(s_x2[(tt + 1) % 2].n)
                    s_x2[bi].wait(VE, ld_hist[tt])
                    s_x2[bi].wait(SC, ld_hist[tt])
                    ln_stats_bn(xt[bi], c_eps5, r=bi)
                    s_dve.wait(SC)
                    if tt >= 2:
                        s_pe.wait(SC, pe_tile[tt - 2])
                    for grp in range(4):
                        SCw.activation(xnbf[bi][:, grp * 512:(grp + 1) * 512],
                                       xt[bi][:, grp * 512:(grp + 1) * 512],
                                       AF.Identity,
                                       scale=v_rstd[:, bi:bi + 1],
                                       bias=v_mb[:, bi:bi + 1])
                        s_act.wait(TE)
                        if len(evac_hist) >= 2:
                            s_dve.wait(TE, evac_hist[-2])
                        for j in range(4):
                            ch = grp * 4 + j
                            s_pe.inc(TE.transpose(
                                trp[grp % 2][:, j * 128:(j + 1) * 128],
                                xnbf[bi][:, ch * 128:(ch + 1) * 128], ident_s[:, :]))
                        s_pe.wait(VE)
                        dst = xno_fm[:, :].rearrange("p (c t) -> p c t", c=NT)[
                            :, grp * 4:(grp + 1) * 4, tt * 128:(tt + 1) * 128]
                        src = trp[grp % 2][:, :].rearrange("p (c t) -> p c t", c=4)
                        VEw.tensor_copy(dst, src)
                        evac_hist.append(s_dve.n)
                    act_xn.append(s_act.n)
                    pe_tile.append(s_pe.n)

            with ExitStack() as p35b:
                E3 = p35b.enter_context
                gwb = E3(nc.sbuf_tensor("gwb", [128, NT * 512], BF16, side="right"))
                gstage = [E3(nc.sbuf_tensor(f"gst{i}", [128, 512], F32, side="right"))
                          for i in range(2)]
                gp_ps = [E3(nc.psum_tensor(f"gp{i}", [128, 512], F32))
                         for i in range(2)]
                gatew_r = gatew.rearrange("(c p) n -> p c n", p=128)
                s_act.wait(SY)
                s_pe.wait(SY)
                s_act.wait(TE)
                s_dve.wait(TE)
                s_rg.inc(GP.dma_start(row_gb[:, :], row1(gateb)), 16)
                gb_ld = s_rg.n
                st_hist = []
                gp_user = [0, 0]
                for og in range(4):
                    s_pe.wait(GP)
                    s_wk2[0].inc(GP.dma_start(
                        gwb[:, :].rearrange("p (c n) -> p c n", c=NT),
                        gatew_r[:, :, og * 512:(og + 1) * 512]), 16)
                    gw_now = s_wk2[0].n
                    for qt in range(8):
                        pi = qt % 2
                        s_wk2[0].wait(TE, gw_now)
                        if og == 0 and qt == 0:
                            s_rg.wait(TE, gb_ld)
                        if len(st_hist) >= 2:
                            sem, val = st_hist[-2]
                            sem.wait(TE, val)
                        s_act.wait(TE, gp_user[pi])
                        for ch in range(NT):
                            s_pe.inc(TE.matmul(
                                gp_ps[pi][:, :],
                                xno_fm[:, ch * 1024 + qt * 128:ch * 1024 + (qt + 1) * 128],
                                gwb[:, ch * 512:(ch + 1) * 512],
                                start=(ch == 0), stop=False,
                                skip_group_check=True))
                        s_pe.inc(TE.matmul(
                            gp_ps[pi][:, :], ones_rb[0:1, 0:128],
                            row_gb[0:1, og * 512:(og + 1) * 512],
                            start=False, stop=True, skip_group_check=True))
                        s_pe.wait(SC)
                        SCw.activation(gstage[pi][:, :], gp_ps[pi][:, :],
                                                AF.Sigmoid)
                        gp_user[pi] = s_act.n
                        s_act.wait(SY)
                        s_st2[pi].inc(SY.dma_start(
                            gate_scr[qt * 128:(qt + 1) * 128,
                                     og * 512:(og + 1) * 512], gstage[pi][:, :]), 16)
                        st_hist.append((s_st2[pi], s_st2[pi].n))

        combT = pvX.enter_context(
            nc.sbuf_tensor("combT", [128, NKH * L], BF16, side="left"))

        with ExitStack() as p4:
            E2 = p4.enter_context

            th_d1pe = [0, 0]
            with ExitStack() as p4a:
                E3 = p4a.enter_context
                awb = E3(nc.sbuf_tensor("awb", [128, NT * 256], BF16, side="right"))
                th4p = E3(nc.psum_tensor("th4p", [4, 512], F32))
                s_act.wait(SY)
                s_pe.wait(SY)
                s_row.inc(SY.dma_start(row_a[0:1, 0:256], row1(aggb)), 16)
                agb_ld = s_row.n
                s_pe.wait(GP)
                s_act.wait(GP)
                s_st2[0].wait(GP)
                s_st2[1].wait(GP)
                s_wk2[0].inc(GP.dma_start(
                    awb[:, :].rearrange("p (c n) -> p c n", c=NT),
                    aggw.rearrange("(c p) n -> p c n", p=128)), 16)
                aw_ld = s_wk2[0].n
                s_wk2[0].wait(TE, aw_ld)
                s_row.wait(TE, agb_ld)
                s_act.wait(TE)
                s_dve.wait(TE)
                for ch in range(NT):
                    s_pe.inc(TE.matmul(
                        th4p[0:4, 0:256],
                        xnl_fm[:, ch * 4:(ch + 1) * 4],
                        awb[:, ch * 256:(ch + 1) * 256],
                        start=(ch == 0), stop=False, skip_group_check=True))
                s_pe.inc(TE.matmul(
                    th4p[0:4, 0:256], ones_rf[0:1, 0:4], row_a[0:1, 0:256],
                    start=False, stop=True, skip_group_check=True))
                th_d1pe[0] = th_d1pe[1] = s_pe.n
                s_pe.wait(SC)
                SCw.copy(s4tmp[0:4, 0:256], th4p[0:4, 0:256])
                s_act.wait(GP)
                s_ct.inc(GP.dma_start(cc_tps[:, :], s4tmp[0:4, 0:256]), 16)
                GP.wait_ge(s_ct.h, s_ct.n)
                s_cc.inc(GP.collective_compute(
                    "AllGather", ALU.bypass,
                    replica_groups=[[0, 1, 2, 3, 4, 5, 6, 7]],
                    ins=[cc_tps[:, :].opt()], outs=[cc_ag[:, :].opt()]))

            kbuf = [E2(nc.sbuf_tensor(f"kbuf{i}", [128, L], BF16, side="right"))
                    for i in range(2)]
            qbuf = [E2(nc.sbuf_tensor(f"qbuf{i}", [128, L], BF16, side="right"))
                    for i in range(2)]
            prb = [E2(nc.sbuf_tensor(f"prb{i}", [128, 512], BF16, side="right"))
                   for i in range(2)]
            s1b = E2(nc.sbuf_tensor("s1b", [128, 512], F32, side="right"))
            s2b = E2(nc.sbuf_tensor("s2b", [128, 512], F32, side="right"))
            tmp1 = E2(nc.sbuf_tensor("tmp1", [128, 512], F32, side="right"))
            tmp2 = E2(nc.sbuf_tensor("tmp2", [128, 512], F32, side="right"))
            lamx = E2(nc.sbuf_tensor("lamx", [1, L], F32, side="right"))
            rrow = E2(nc.sbuf_tensor("rrow", [1, 2560], F32, side="right"))
            rrowb = E2(nc.sbuf_tensor("rrowb", [1, 1024], BF16, side="right"))
            t2T = E2(nc.sbuf_tensor("t2T", [128, 32], BF16, side="right"))
            tlnT = E2(nc.sbuf_tensor("tlnT", [128, 64], BF16, side="right"))
            bcwb = E2(nc.sbuf_tensor("bcwb", [128, NT * 256], BF16, side="right"))
            d1wb = [E2(nc.sbuf_tensor(f"d1wb{i}", [128, 2048], BF16, side="right"))
                    for i in range(2)]
            d2wb = [E2(nc.sbuf_tensor(f"d2wb{i}", [128, 4096], BF16, side="right"))
                    for i in range(2)]
            scp = [E2(nc.psum_tensor(f"scp{i}", [128, 512], F32)) for i in range(2)]
            atp = [E2(nc.psum_tensor(f"atp{i}", [128, 512], F32)) for i in range(2)]
            denp = [E2(nc.psum_tensor(f"denp{i}", [1, 512], F32)) for i in range(2)]
            auxp = E2(nc.psum_tensor("auxp", [128, 512], F32))
            thp = E2(nc.psum_tensor("thp", [128, 512], F32))

            kq_hist = {}
            q_hist = {}
            pending_s2 = [None]
            comb1_pe = [0]

            def comb1_part():
                kh_, g_, s2d = pending_s2[0]
                pending_s2[0] = None
                s_dve.wait(TE, s2d)
                s_act.wait(TE)
                s_pe.inc(TE.matmul(auxp[:, :], ones_rb[0:1, :],
                                   rrowb[0:1, 512:1024], start=True, stop=True))
                comb1_pe[0] = s_pe.n
                s_pe.wait(SC)
                s_dve.wait(SC, at_bank_user[1])
                SCw.copy(s2b[:, :], auxp[:, :])
                e2 = s_act.n
                s_act.wait(VE, e2)
                VEw.tensor_mul(tmp2[:, :], atp[1][:, :], s2b[:, :])
                at_bank_user[1] = s_dve.n
                VEw.tensor_tensor(
                    combT[:, kh_ * L + g_ * 512:kh_ * L + (g_ + 1) * 512],
                    tmp1[:, :], tmp2[:, :], ALU.subtract)
            pe_kh_end = [0] * NKH
            pe_br_end = {}
            lam_hist = [0] * (NKH + 1)
            sc_bank_user = [0, 0]
            pr_bank_user = [0, 0]
            at_bank_user = [0, 0]
            den_bank_user = [0, 0]
            den_ready = [0, 0]
            cc_done = 0

            def load_k(kh):
                if kh >= 2:
                    s_pe.wait(SY, pe_kh_end[kh - 2])
                s_kb2[kh % 2].inc(SY.dma_start(kbuf[kh % 2][:, :],
                                               k_scr[kh, :, :]), 16)
                kq_hist[kh] = s_kb2[kh % 2].n

            def load_q(kh, br):
                j = kh * 2 + br
                if j >= 2:
                    s_pe.wait(SY, pe_br_end[j - 2])
                s_qb2[j % 2].inc(SY.dma_start(qbuf[j % 2][:, :],
                                              q_scr[j, :, :]), 16)
                q_hist[j] = s_qb2[j % 2].n

            def load_lam(kh):
                s_dve.wait(GP)
                s_act.wait(GP)
                s_lam.inc(GP.dma_start(lamx[0:1, :], lamT[kh:kh + 1, :]), 16)
                lam_hist[kh] = s_lam.n

            tln_mark = [0]
            d1_hist = [None, None]

            def d1_load(blk):
                sl = d1w_scr[blk * 256:(blk + 1) * 256, :].rearrange(
                    "(c p) n -> p c n", p=128)
                bi = blk % 2
                if bi == 0:
                    s_wk2[0].inc(GP.dma_start(
                        d1wb[0][:, :].rearrange("p (c n) -> p c n", c=2), sl), 16)
                    d1_hist[0] = (s_wk2[0], s_wk2[0].n)
                else:
                    s_d1s.inc(SY.dma_start(
                        d1wb[1][:, :].rearrange("p (c n) -> p c n", c=2), sl), 16)
                    d1_hist[1] = (s_d1s, s_d1s.n)

            def thought_partA(step):
                if step == 0:
                    thought_init_finalize()
                else:
                    thought_step_finalize(step - 1)
                s_act.wait(VE)
                ln_stats(thought, tln_f, 4, c_eps6, r=2)
                s_dve.wait(SC)
                SCw.activation(tln_f[0:4, :], thought[0:4, :], AF.Identity,
                                        scale=v_rstd[0:4, 2:3], bias=v_mb[0:4, 2:3])
                tln_mark[0] = s_act.n
                if step == 0:
                    s_res.wait(GP)
                    s_res.wait(SY)
                s_pe.wait(GP, th_d1pe[0])
                d1_load(0)
                s_pe.wait(SY, th_d1pe[1])
                d1_load(1)

            def thought_partB(step):
                s_act.wait(TE, tln_mark[0])
                s_dve.wait(TE)
                for ch in range(NT):
                    s_pe.inc(TE.transpose(thp[:, 64 + ch * 4:68 + ch * 4],
                                          tln_f[0:4, ch * 128:(ch + 1) * 128],
                                          identf_s[0:4, 0:4]))
                s_pe.wait(SC)
                for ch in range(NT):
                    SCw.activation(tlnT[:, ch * 4:(ch + 1) * 4],
                                            thp[:, 64 + ch * 4:68 + ch * 4],
                                            AF.Identity,
                                            scale=png_s[:, ch:ch + 1],
                                            bias=pnb_s[:, ch:ch + 1])
                tln_bf = s_act.n
                pe_blk = []
                for blk in range(8):
                    bi = blk % 2
                    if blk >= 2:
                        if bi == 0:
                            s_pe.wait(GP, pe_blk[blk - 2])
                        else:
                            s_pe.wait(SY, pe_blk[blk - 2])
                        d1_load(blk)
                    wsem, wval = d1_hist[bi]
                    wsem.wait(TE, wval)
                    if blk == 0:
                        s_act.wait(TE, tln_bf)
                    for ci in range(2):
                        ch = blk * 2 + ci
                        for hb in range(8):
                            s_pe.inc(TE.matmul(
                                thp[:, hb * 4:(hb + 1) * 4],
                                d1wb[bi][:, ci * 1024 + hb * 128:
                                          ci * 1024 + (hb + 1) * 128],
                                tlnT[:, ch * 4:(ch + 1) * 4],
                                start=(ch == 0 and hb == 0),
                                stop=(ch == NT - 1 and hb == 7),
                                skip_group_check=True))
                    pe_blk.append(s_pe.n)
                s_pe.wait(SC)
                for hb in range(8):
                    SCw.activation(t2T[:, hb * 4:(hb + 1) * 4],
                                            thp[:, hb * 4:(hb + 1) * 4], AF.Gelu,
                                            bias=d1b_s[:, hb:hb + 1])
                geld = s_act.n
                s_act.wait(SY)
                s_pe.wait(SY)
                s_row.inc(SY.dma_start(row_a[:, :], row1(d2b)), 16)
                s_row.wait(VE)
                s_pe.wait(VE)
                VEw.tensor_scalar_mul(row_ab[0:1, :], row_a[0:1, :], 0.125)
                d2brow = s_dve.n
                d2w_r = d2w_scr.rearrange("(c p) n -> p c n", p=128)
                pe_og2 = []
                d2h = [None, None]
                for og in range(4):
                    bi = og % 2
                    if bi == 0:
                        s_pe.wait(GP, pe_og2[og - 2] if og >= 2 else pe_blk[6])
                        s_wk2[0].inc(GP.dma_start(
                            d2wb[0][:, :].rearrange("p (c n) -> p c n", c=8),
                            d2w_r[:, :, og * 512:(og + 1) * 512]), 16)
                        d2h[0] = (s_wk2[0], s_wk2[0].n)
                    else:
                        s_pe.wait(SY, pe_og2[og - 2] if og >= 2 else pe_blk[7])
                        s_d2s.inc(SY.dma_start(
                            d2wb[1][:, :].rearrange("p (c n) -> p c n", c=8),
                            d2w_r[:, :, og * 512:(og + 1) * 512]), 16)
                        d2h[1] = (s_d2s, s_d2s.n)
                    wsem, wval = d2h[bi]
                    if og > 0:
                        s_dve.wait(TE)
                    wsem.wait(TE, wval)
                    if og == 0:
                        s_act.wait(TE, geld)
                        s_dve.wait(TE, d2brow)
                    for hb in range(8):
                        s_pe.inc(TE.matmul(
                            thp[0:4, 0:512], t2T[:, hb * 4:(hb + 1) * 4],
                            d2wb[bi][:, hb * 512:(hb + 1) * 512],
                            start=(hb == 0), stop=False,
                            skip_group_check=True))
                    s_pe.inc(TE.matmul(
                        thp[0:4, 0:512], ones_rb[0:1, 0:4],
                        row_ab[0:1, og * 512:(og + 1) * 512],
                        start=False, stop=True, skip_group_check=True))
                    s_pe.wait(VE)
                    VEw.tensor_copy(s4tmp[0:4, og * 512:(og + 1) * 512],
                                             thp[0:4, 0:512])
                    pe_og2.append(s_pe.n)
                th_d1pe[0] = pe_og2[2]
                th_d1pe[1] = pe_og2[3]
                s_dve.wait(GP)
                s_ct.inc(GP.dma_start(cc_tin[:, :], s4tmp[0:4, :]), 16)
                GP.wait_ge(s_ct.h, s_ct.n)
                s_cc.inc(GP.collective_compute(
                    "AllReduce", ALU.add,
                    replica_groups=[[0, 1, 2, 3, 4, 5, 6, 7]],
                    ins=[cc_tin[:, :].opt()], outs=[cc_tout[:, :].opt()]))

            def thought_init_finalize():
                nonlocal cc_done
                cc_done += 1
                GP.wait_ge(s_cc.h, cc_done)
                s_dve.wait(GP)
                s_ct.inc(GP.dma_start(
                    thought[0:4, :].rearrange("b (r n) -> b r n", r=8),
                    cc_ag.rearrange("(r b) n -> b r n", b=4)), 16)
                s_ct.wait(VE)
                s_ct.wait(SC)

            def thought_step_finalize(step):
                nonlocal cc_done
                cc_done += 1
                GP.wait_ge(s_cc.h, cc_done)
                s_dve.wait(GP)
                if DEBUG:
                    GP.wait_ge(s_dbg.h, s_dbg.n)
                s_ct.inc(GP.dma_start(s4tmp[0:4, :], cc_tout[:, :]), 16)
                s_ct.wait(VE)
                if DEBUG:
                    s_ct.wait(GP)
                    s_dbg.inc(GP.dma_start(cout_dbg[step, :, :], s4tmp[0:4, :]), 16)
                    s_dbg.wait(VE)
                VEw.tensor_add(thought[0:4, :], thought[0:4, :],
                                        s4tmp[0:4, :])
                if DEBUG:
                    s_dve.wait(GP)
                    s_dbg.inc(GP.dma_start(th_dbg[1 + step, :, :], thought[0:4, :]), 16)

            s_act.wait(TE)
            s_dve.wait(TE)
            s_sp.wait(SY)
            s_wk2[0].wait(SY)
            s_wk2[1].wait(SY)
            s_pe.wait(SY)
            s_st2[0].wait(GP)
            s_st2[1].wait(GP)
            s_st2[0].wait(SY)
            s_st2[1].wait(SY)
            load_k(0)
            load_q(0, 0)
            load_q(0, 1)
            load_lam(0)
            for kh in range(NKH):
                bi = kh % 2
                if kh + 1 < NKH:
                    load_k(kh + 1)
                s_kb2[bi].wait(TE, kq_hist[kh])
                for g in range(4):
                    nkt = 4 * g + 4
                    pv_done = 0
                    for br in range(2):
                        j = kh * 2 + br
                        ab = br
                        s_qb2[j % 2].wait(TE, q_hist[j])
                        s_dve.wait(TE, at_bank_user[ab])
                        s_act.wait(TE, den_bank_user[ab])
                        pend = {}

                        def emit_score(kt, kh=kh, g=g, j=j, bi=bi, pend=pend):
                            lo = (kt - 4 * g) * 128 if kt >= 4 * g else 0
                            sb = kt % 2
                            s_act.wait(TE, sc_bank_user[sb])
                            s_pe.inc(TE.matmul(
                                scp[sb][:, lo:512],
                                kbuf[bi][:, kt * 128:(kt + 1) * 128],
                                qbuf[j % 2][:, g * 512 + lo:(g + 1) * 512],
                                start=True, stop=True, skip_group_check=True))
                            s_pe.wait(SC)
                            s_pe.wait(SC, pr_bank_user[sb])
                            s_act.inc(SC.activation(
                                prb[sb][:, lo:512], scp[sb][:, lo:512], AF.Exp,
                                bias=negB[:, j:j + 1]))
                            sc_bank_user[sb] = s_act.n
                            expn = s_act.n
                            exp_dve = 0
                            if kt >= 4 * g:
                                s_act.wait(VE, expn)
                                s_dve.inc(VE.tensor_mul(prb[sb][:, lo:lo + 128],
                                                        prb[sb][:, lo:lo + 128],
                                                        mask_s[:, :]))
                                exp_dve = s_dve.n
                            pend[kt] = (lo, sb, expn, exp_dve)

                        def emit_pv(kt, nkt, kh=kh, ab=ab, pend=pend):
                            lo, sb, expn, exp_dve = pend.pop(kt)
                            s_act.wait(TE, expn)
                            if exp_dve:
                                s_dve.wait(TE, exp_dve)
                            s_pe.inc(TE.matmul(
                                atp[ab][:, lo:512],
                                v_full[:, (kh * NT + kt) * 128:(kh * NT + kt + 1) * 128],
                                prb[sb][:, lo:512],
                                start=(kt == 0), stop=(kt == nkt - 1),
                                skip_group_check=True))
                            s_pe.inc(TE.matmul(
                                denp[ab][0:1, lo:512], ones_cb[:, :],
                                prb[sb][:, lo:512],
                                start=(kt == 0), stop=(kt == nkt - 1),
                                skip_group_check=True))
                            pr_bank_user[sb] = s_pe.n

                        def comb0_part():
                            s_act.wait(TE, den_ready[0])
                            s_act.wait(TE)
                            s_pe.inc(TE.matmul(auxp[:, :], ones_rb[0:1, :],
                                               rrowb[0:1, 0:512],
                                               start=True, stop=True))
                            s_pe.wait(SC)
                            s_dve.wait(SC, at_bank_user[0])
                            SCw.copy(s1b[:, :], auxp[:, :])
                            e1 = s_act.n
                            s_act.wait(VE, e1)
                            VEw.tensor_mul(tmp1[:, :], atp[0][:, :], s1b[:, :])
                            at_bank_user[0] = s_dve.n

                        emit_score(0)
                        for kt in range(1, nkt):
                            emit_score(kt)
                            emit_pv(kt - 1, nkt)
                            if br == 0 and kt == 2 and pending_s2[0] is not None:
                                comb1_part()
                            if br == 1 and kt == 3:
                                comb0_part()
                        emit_pv(nkt - 1, nkt)
                        pv_done = s_pe.n
                        if br == 0:
                            s_pe.wait(SC, pv_done)
                            f0, r0 = recip_acc(rrowb[0:1, 0:512],
                                               denp[0][0:1, :],
                                               rrow[0:1, 1536:2048])
                            den_bank_user[0] = f0
                            den_ready[0] = r0
                        if g == 3:
                            pe_br_end[j] = s_pe.n
                            if kh + 1 < NKH:
                                load_q(kh + 1, br)
                    s_pe.wait(SC, pv_done)
                    f1, r1 = recip_acc(rrow[0:1, 512:1024], denp[1][0:1, :],
                                       rrow[0:1, 2048:2560])
                    den_bank_user[1] = f1
                    den_ready[1] = r1
                    s_lam.wait(VE, lam_hist[kh])
                    s_act.wait(VE, r1)
                    s_pe.wait(VE, comb1_pe[0])
                    VEw.tensor_tensor(rrowb[0:1, 512:1024],
                                               lamx[0:1, g * 512:(g + 1) * 512],
                                               rrow[0:1, 512:1024], ALU.mult)
                    pending_s2[0] = (kh, g, s_dve.n)
                    if g == 1 and kh in (1, 3, 5, 6):
                        thought_partA((kh - 1) // 2 if kh < 6 else 3)
                pe_kh_end[kh] = s_pe.n
                if kh + 1 < NKH:
                    load_lam(kh + 1)
                if kh in (1, 3, 5, 6):
                    thought_partB((kh - 1) // 2 if kh < 6 else 3)
                elif kh == 7:
                    thought_step_finalize(3)
            if pending_s2[0] is not None:
                comb1_part()

            s_dve.wait(TE)
            s_act.wait(TE)
            for ch in range(NT):
                s_pe.inc(TE.transpose(thp[:, 64 + ch * 4:68 + ch * 4],
                                      thought[0:4, ch * 128:(ch + 1) * 128],
                                      identf_s[0:4, 0:4]))
            s_pe.wait(SC)
            SCw.copy(tlnT[:, :], thp[:, 64:128])
            thT_bf = s_act.n
            s_act.wait(SY)
            s_pe.wait(SY)
            s_row.inc(SY.dma_start(row_a[0:1, 0:256], row1(bcb)), 16)
            bcb_ld = s_row.n
            s_pe.wait(GP)
            s_wk2[0].inc(GP.dma_start(
                bcwb[:, :].rearrange("p (c n) -> p c n", c=NT),
                bcw.rearrange("(c p) n -> p c n", p=128)), 16)
            bw_ld = s_wk2[0].n
            s_wk2[0].wait(TE, bw_ld)
            s_act.wait(TE, thT_bf)
            s_row.wait(TE, bcb_ld)
            s_dve.wait(TE)
            for ch in range(NT):
                s_pe.inc(TE.matmul(
                    thp[0:4, 0:256], tlnT[:, ch * 4:(ch + 1) * 4],
                    bcwb[:, ch * 256:(ch + 1) * 256], start=(ch == 0),
                    stop=False, skip_group_check=True))
            s_pe.inc(TE.matmul(thp[0:4, 0:256], ones_rf[0:1, 0:4],
                               row_a[0:1, 0:256],
                               start=False, stop=True, skip_group_check=True))
            s_pe.wait(SC)
            SCw.copy(s4tmp[0:4, 0:256], thp[0:4, 0:256])
            pe_bc_mm = s_pe.n
            act_bc = s_act.n
            dve_bc = s_dve.n
            s_pe.wait(GP, pe_bc_mm)
            s_act.wait(GP, act_bc)
            s_dve.wait(GP, dve_bc)
            s_ct.inc(GP.dma_start(cc_tps[:, :], s4tmp[0:4, 0:256]), 16)
            GP.wait_ge(s_ct.h, s_ct.n)
            s_cc.inc(GP.collective_compute(
                "AllGather", ALU.bypass,
                replica_groups=[[0, 1, 2, 3, 4, 5, 6, 7]],
                ins=[cc_tps[:, :].opt()], outs=[cc_ag2[:, :].opt()]))
            cc_done += 1
            GP.wait_ge(s_cc.h, cc_done)
            s_ct.inc(GP.dma_start(
                s4tmp[0:4, :].rearrange("b (r n) -> b r n", r=8),
                cc_ag2.rearrange("(r b) n -> b r n", b=4)), 16)
            bc_ct = s_ct.n

        rs_done = [0, 0, 0, 0]
        with ExitStack() as p56:
            E2 = p56.enter_context
            wob = [E2(nc.sbuf_tensor(f"wob{i}", [128, 8 * 512], BF16, side="right"))
                   for i in range(2)]
            postg = [E2(nc.sbuf_tensor(f"postg{i}", [128, 512], F32, side="right"))
                     for i in range(2)]
            xt6 = [E2(nc.sbuf_tensor(f"xt6{i}", [128, D], F32, side="right"))
                   for i in range(2)]
            gt6s = E2(nc.sbuf_tensor("gt6s", [128, D], F32, side="right"))
            xn6 = [E2(nc.sbuf_tensor(f"xn6{i}", [128, D], F32, side="right"))
                   for i in range(2)]
            goB = E2(nc.sbuf_tensor("goB", [128, D], F32, side="right"))
            boB = E2(nc.sbuf_tensor("boB", [128, D], F32, side="right"))
            v_bn6 = E2(nc.sbuf_tensor("v_bn6", [128, 48], F32, side="right"))
            v_ag = E2(nc.sbuf_tensor("v_ag", [128, 4], F32, side="right"))
            pop = [E2(nc.psum_tensor(f"pop{i}", [128, 512], F32)) for i in range(2)]
            p6p = E2(nc.psum_tensor("p6p", [128, 512], F32))
            fin6 = [xt6[0][:, 0:1024], xt6[1][:, 0:1024],
                    xt6[0][:, 1024:2048], xt6[1][:, 1024:2048]]
            at6 = [xn6[0][:, 0:1024], xn6[1][:, 0:1024],
                   xn6[0][:, 1024:2048], xn6[1][:, 1024:2048]]
            pe_og, st_hist, ow_hist = [], [], []

            wos_r = wout_scr.rearrange("(c p) n -> p c n", p=128)
            s_pe.wait(SY, pe_bc_mm)
            s_act.wait(SY, act_bc)
            s_dve.wait(SY, dve_bc)
            s_res.wait(SY)
            for og01 in range(2):
                s_ow2[og01].inc(SY.dma_start(
                    wob[og01][:, :].rearrange("p (c n) -> p c n", c=8),
                    wos_r[:, :, og01 * 512:(og01 + 1) * 512]), 16)
                ow_hist.append(s_ow2[og01].n)

            s_act.wait(SY)
            s_pe.wait(SY)
            s_dve.wait(SY)
            s_row.inc(SY.dma_start(row_a[:, :], row1(outg)), 16)
            s_row.inc(SY.dma_start(row_b[:, :], row1(outb)), 16)
            s_row.wait(TE)
            s_act.wait(TE)
            s_dve.wait(TE)
            bcast_rows(p6p, row_a, goB, 128)
            bcast_rows(p6p, row_b, boB, 128)

            def ln_stats_vg(tile, eps_ap, r):
                s_act.wait(VE)
                s_gpc.wait(VE)
                for k in range(4):
                    VEw.bn_stats(v_bn6[:, r * 24 + k * 6:r * 24 + (k + 1) * 6],
                                 tile[:, k * 512:(k + 1) * 512])
                VEw.bn_aggr(v_ag[:, 2 * r:2 * r + 2],
                            v_bn6[:, r * 24:(r + 1) * 24])
                s_dve.wait(SC)
                s_gpc.wait(SC)
                SCw.activation(v_rstd[:, r:r + 1], v_ag[:, 2 * r + 1:2 * r + 2],
                               AF.Ln, bias=eps_ap[:, :])
                SCw.activation(v_rstd[:, r:r + 1], v_rstd[:, r:r + 1],
                               AF.Exp, scale=c_nhalf[:, :])
                s_act.wait(VE)
                VEw.tensor_scalar(v_mb[:, r:r + 1], v_ag[:, 2 * r:2 * r + 1],
                                  v_rstd[:, r:r + 1], -1.0, ALU.mult, ALU.mult)

            ffx_hist = []
            ld6 = []
            gread_mark = [0]

            def p6a_tile(qt):
                bi = qt % 2
                if qt >= 2:
                    sem, val = ffx_hist[qt - 2]
                    sem.wait(SY, val)
                sx = s_x2[bi]
                sx.inc(SY.dma_start(xt6[bi][:, :],
                                    x_own[qt * 128:(qt + 1) * 128, :]), 16)
                if qt >= 1:
                    s_dve.wait(SY, gread_mark[0])
                sx.inc(SY.dma_start(gt6s[:, :],
                                    gate_scr[qt * 128:(qt + 1) * 128, :]), 16)
                ld6.append(sx.n)
                sx.wait(VE, ld6[qt])
                ln_stats_vg(xt6[bi], c_eps5, r=bi)
                s_act.wait(VE)
                VEw.tensor_scalar(xn6[bi][:, :], xt6[bi][:, :],
                                  v_rstd[:, bi:bi + 1], v_mb[:, bi:bi + 1],
                                  ALU.mult, ALU.add)
                VEw.tensor_mul(gt6s[:, :], gt6s[:, :], bc_bc[:, :])
                VEw.tensor_add(gt6s[:, :], gt6s[:, :], xn6[bi][:, :])
                ln_stats_vg(gt6s, c_eps6, r=bi)
                s_act.wait(VE)
                VEw.tensor_scalar(xn6[bi][:, :], gt6s[:, :],
                                  v_rstd[:, bi:bi + 1], v_mb[:, bi:bi + 1],
                                  ALU.mult, ALU.add)
                gread_mark[0] = s_dve.n
                VEw.tensor_mul(xn6[bi][:, :], xn6[bi][:, :], goB[:, :])
                VEw.tensor_add(xn6[bi][:, :], xn6[bi][:, :], boB[:, :])
                VEw.tensor_add(xn6[bi][:, :], xn6[bi][:, :], xt6[bi][:, :])
                fin = s_dve.n
                s_dve.wait(SY, fin)
                s_st2[bi].inc(SY.dma_start(
                    ffx_scr[qt * 128:(qt + 1) * 128, :], xn6[bi][:, :]), 16)
                ffx_hist.append((s_st2[bi], s_st2[bi].n))

            dve_pre = s_dve.n
            pop_user = [0, 0]

            def p5_og_block(og):
                bi = og % 2
                h = og // 2
                if og == 0:
                    s_dve.wait(TE, dve_pre)
                for qt in range(NT):
                    pi = qt % 2
                    s_ow2[bi].wait(TE, ow_hist[og])
                    if len(st_hist) >= 2:
                        sem, val = st_hist[-2]
                        sem.wait(TE, val)
                    s_dve.wait(TE, pop_user[pi])
                    for ch in range(NKH):
                        s_pe.inc(TE.matmul(
                            pop[pi][:, :],
                            combT[:, ch * L + qt * 128:ch * L + (qt + 1) * 128],
                            wob[bi][:, ch * 512:(ch + 1) * 512],
                            start=(ch == 0), stop=(ch == NKH - 1)))
                    s_pe.wait(VE)
                    VEw.tensor_copy(postg[pi][:, :], pop[pi][:, :])
                    pop_user[pi] = s_dve.n
                    s_dve.wait(SY, pop_user[pi])
                    s_po8[og * 2 + pi].inc(SY.dma_start(
                        po_scr[og, qt * 128:(qt + 1) * 128, :],
                        postg[pi][:, :]), 16)
                    st_hist.append((s_po8[og * 2 + pi], s_po8[og * 2 + pi].n))
                pe_og.append(s_pe.n)

            def og_prefetch(og):
                bi = og % 2
                s_pe.wait(SY, pe_og[og - 2])
                s_ow2[bi].inc(SY.dma_start(
                    wob[bi][:, :].rearrange("p (c n) -> p c n", c=8),
                    wos_r[:, :, og * 512:(og + 1) * 512]), 16)
                ow_hist.append(s_ow2[bi].n)

            def rs_emit(og):
                s_po8[og * 2].wait(GP)
                s_po8[og * 2 + 1].wait(GP)
                s_cc.inc(GP.collective_compute(
                    "ReduceScatter", ALU.add,
                    replica_groups=[[0, 1], [2, 3], [4, 5], [6, 7]],
                    ins=[po_scr[og, :, :].opt()],
                    outs=[ao_scr[og, :, :].opt()]))
                rs_done[og] = s_cc.n

            p5_og_block(0)
            rs_emit(0)
            s_ct.wait(TE, bc_ct)
            s_pe.wait(SC)
            for og in range(4):
                s_dve.wait(TE)
                s_act.wait(TE)
                s_pe.inc(TE.matmul(p6p[0:1, :], oneh4[0:4, :],
                                   s4tmp[0:4, og * 512:(og + 1) * 512],
                                   start=True, stop=True))
                s_pe.wait(SC)
                SCw.copy(row_b[0:1, og * 512:(og + 1) * 512], p6p[0:1, :])
            bcast_rows(p6p, row_b, bc_bc, 128)
            p6a_tile(0)
            p5_og_block(1)
            rs_emit(1)
            p6a_tile(1)
            p6a_tile(2)
            og_prefetch(2)
            p5_og_block(2)
            rs_emit(2)
            p6a_tile(3)
            p6a_tile(4)
            og_prefetch(3)
            p5_og_block(3)
            rs_emit(3)
            p6a_tile(5)
            p6a_tile(6)
            p6a_tile(7)

            it = 0
            ob_hist = [0, 0, 0, 0]
            for half in range(2):
                GP.wait_ge(s_cc.h, rs_done[half * 2 + 1])
                SY.wait_ge(s_cc.h, rs_done[half * 2 + 1])
                for qt in range(8):
                    b = it % 4
                    Q = GP if b % 2 == 0 else SY
                    if it >= 4:
                        s_ob4[b].wait(Q, ob_hist[b])
                    if it < 4:
                        for sem, val in (ffx_hist[6], ffx_hist[7]):
                            sem.wait(Q, val)
                    sem, val = ffx_hist[qt]
                    sem.wait(Q, val)
                    sx = s_fb4[b]
                    sx.inc(Q.dma_start(
                        fin6[b][:, :],
                        ffx_scr[qt * 128:(qt + 1) * 128,
                                half * 1024:(half + 1) * 1024]), 16)
                    sx.inc(Q.dma_start(
                        at6[b][:, 0:512],
                        ao_scr[half * 2, qt * 128:(qt + 1) * 128, :]), 16)
                    sx.inc(Q.dma_start(
                        at6[b][:, 512:1024],
                        ao_scr[half * 2 + 1, qt * 128:(qt + 1) * 128, :]), 16)
                    sx.wait(VE, sx.n)
                    VEw.tensor_add(fin6[b][:, :], fin6[b][:, :], at6[b][:, :])
                    s_dve.wait(Q)
                    s_ob4[b].inc(Q.dma_start(
                        out[qt * 128:(qt + 1) * 128,
                            half * 1024:(half + 1) * 1024], fin6[b][:, :]), 16)
                    ob_hist[b] = s_ob4[b].n
                    it += 1
            for b in range(4):
                s_ob4[b].wait(GP, ob_hist[b])
            if DEBUG:
                GP.wait_ge(s_dbg.h, s_dbg.n)
        pvX.close()

    return nc


_prog_cache = {}


def _get_program():
    if "nc" not in _prog_cache:
        _prog_cache["nc"] = build_program()
    return _prog_cache["nc"]


def _consts():
    pos = np.arange(L, dtype=np.float64)
    inv = 1.0 / (10000.0 ** (np.arange(0, DH, 2, dtype=np.float64) / DH))
    fr = np.outer(pos, inv)
    emb = np.concatenate([fr, fr], axis=1)
    cos_fm = np.ascontiguousarray(np.cos(emb).T.astype(np.float32))
    sin_fm = np.sin(emb).T.astype(np.float32)
    sinS = np.ascontiguousarray(
        np.concatenate([-sin_fm[:64], sin_fm[64:]], axis=0).astype(np.float32))
    identity = np.eye(128).astype(ml_dtypes.bfloat16)
    identityf = np.eye(128).astype(np.float32)
    kk, qq = np.meshgrid(np.arange(128), np.arange(128), indexing="ij")
    mask = (kk <= qq).astype(ml_dtypes.bfloat16)
    return cos_fm, sinS, identity, identityf, mask


def _core_inputs(ii, c, consts):
    x = ii["L0_attn_x"]
    cos_fm, sinS, identity, identityf, mask = consts
    b, hh, th = c // 2, c % 2, c % 2
    A = np.ascontiguousarray
    return {
        "x_b": A(x[b]),
        "x_own": A(x[b, th * 1024:(th + 1) * 1024]),
        "x_last4": A(x[:, L - 1, :]),
        "wdkv": A(ii["L0_attn_W_DKV"]),
        "wuk": A(ii["L0_attn_W_UK"][:, hh * 1024:(hh + 1) * 1024]),
        "wuv": A(ii["L0_attn_W_UV"][:, hh * 1024:(hh + 1) * 1024]),
        "wdq": A(ii["L0_attn_W_DQ"]),
        "wuq": A(ii["L0_attn_W_UQ"][:, hh * 2048:(hh + 1) * 2048]),
        "wlam": A(ii["L0_attn_W_lam"][:, hh * 8:(hh + 1) * 8]),
        "blam": A(ii["L0_attn_b_lam"][hh * 8:(hh + 1) * 8]),
        "wout": A(ii["L0_attn_W_out"][hh * 1024:(hh + 1) * 1024, :]),
        "gatew": A(ii["L0_ffn_gate_w"]),
        "gateb": A(ii["L0_ffn_gate_b"]),
        "aggw": A(ii["L0_ffn_agg_w"][:, c * 256:(c + 1) * 256]),
        "aggb": A(ii["L0_ffn_agg_b"][c * 256:(c + 1) * 256]),
        "bcw": A(ii["L0_ffn_bc_w"][:, c * 256:(c + 1) * 256]),
        "bcb": A(ii["L0_ffn_bc_b"][c * 256:(c + 1) * 256]),
        "d1w": A(ii["L0_ffn_d1_w"][:, c * 1024:(c + 1) * 1024]),
        "d1b": A(ii["L0_ffn_d1_b"][c * 1024:(c + 1) * 1024]),
        "d2w": A(ii["L0_ffn_d2_w"][c * 1024:(c + 1) * 1024, :]),
        "d2b": A(ii["L0_ffn_d2_b"]),
        "ing": A(ii["L0_ffn_in_gamma"]), "inb": A(ii["L0_ffn_in_beta"]),
        "png": A(ii["L0_ffn_pn_gamma"]), "pnb": A(ii["L0_ffn_pn_beta"]),
        "outg": A(ii["L0_ffn_out_gamma"]), "outb": A(ii["L0_ffn_out_beta"]),
        "cosk": cos_fm, "sinkS": sinS,
        "ident": identity, "identf": identityf, "maskdiag": mask,
        "onehot4": A(np.eye(4, dtype=np.float32)[:, b:b + 1]),
    }


def kernel(**inputs):
    ii = {k: np.asarray(v) for k, v in inputs.items()}
    consts = _consts()
    nc = _get_program()
    in_maps = [_core_inputs(ii, c, consts) for c in range(8)]
    res = run_bass_kernel_spmd(nc, in_maps, core_ids=list(range(8)))
    y = np.empty((B, L, D), dtype=np.float32)
    for c in range(8):
        b, th = c // 2, c % 2
        y[b, th * 1024:(th + 1) * 1024, :] = np.asarray(res.results[c]["out"])
    return y


# version 29 (final)
# speedup vs baseline: 1.0234x; 1.0039x over previous
import numpy as np
import ml_dtypes
from contextlib import ExitStack

import concourse.bass as bass
import concourse.mybir as mybir
from concourse.bass_utils import run_bass_kernel_spmd

F32 = mybir.dt.float32
BF16 = mybir.dt.bfloat16
AF = mybir.ActivationFunctionType
ALU = mybir.AluOpType
AX = mybir.AxisListType
F32R = mybir.dt.float32r


def r32(ap):
    return ap.bitcast(F32R)

B, L, D = 4, 2048, 2048
NT = 16
NKH, DH = 8, 128
NQH = 16
QSC = float(1.0 / np.sqrt(128.0))

DEBUG = False


class Sem:
    def __init__(self, stack, nc, name):
        self.h = stack.enter_context(nc.semaphore(name))
        self.n = 0

    def inc(self, instr, k=1):
        instr.then_inc(self.h, k)
        self.n += k
        return instr

    def wait(self, eng, val=None):
        v = self.n if val is None else val
        assert v <= self.n, f"sem wait {v} > emitted {self.n}"
        if v > 0:
            eng.wait_ge(self.h, v)


def build_program():
    nc = bass.Bass("TRN2", target_bir_lowering=False)
    TE, SC, VE, SY, GP = nc.tensor, nc.scalar, nc.vector, nc.sync, nc.gpsimd

    def din(name, shape, dt=F32):
        return nc.dram_tensor(name, shape, dt, kind="ExternalInput").ap()

    def dscr(name, shape, dt=F32):
        kind = "ExternalOutput" if DEBUG else "Internal"
        return nc.dram_tensor(name, shape, dt, kind=kind).ap()

    x_b = din("x_b", [L, D])
    x_own = din("x_own", [L // 2, D])
    x_last4 = din("x_last4", [4, D])
    wdkv = din("wdkv", [D, 512])
    wuk = din("wuk", [512, NKH * DH])
    wuv = din("wuv", [512, NKH * DH])
    wdq = din("wdq", [D, 1024])
    wuq = din("wuq", [1024, NQH * DH])
    wlam = din("wlam", [D, NKH])
    blam = din("blam", [NKH])
    wout = din("wout", [NKH * DH, D])
    gatew = din("gatew", [D, D])
    gateb = din("gateb", [D])
    aggw = din("aggw", [D, 256])
    aggb = din("aggb", [256])
    bcw = din("bcw", [D, 256])
    bcb = din("bcb", [256])
    ing = din("ing", [D]); inb = din("inb", [D])
    png = din("png", [D]); pnb = din("pnb", [D])
    outg = din("outg", [D]); outb = din("outb", [D])
    d1w = din("d1w", [D, 1024])
    d1b = din("d1b", [1024])
    d2w = din("d2w", [1024, D])
    d2b = din("d2b", [D])
    cosk = din("cosk", [128, L])
    sinkS = din("sinkS", [128, L])
    ident = din("ident", [128, 128], BF16)
    identf = din("identf", [128, 128], F32)
    maskdiag = din("maskdiag", [128, 128], BF16)
    onehot4 = din("onehot4", [4, 1])

    out = nc.dram_tensor("out", [L // 2, D], F32, kind="ExternalOutput").ap()

    k_scr = dscr("k_scr", [NKH, 128, L], BF16)
    q_scr = dscr("q_scr", [NQH, 128, L], BF16)
    gate_scr = dscr("gate_scr", [L // 2, D])
    po_scr = dscr("po_scr", [4, L, 512])
    ao_scr = dscr("ao_scr", [4, L // 2, 512])
    cc_tin = dscr("cc_tin", [4, D]) if DEBUG else nc.dram_tensor("cc_tin", [4, D], F32).ap()
    cc_tout = dscr("cc_tout", [4, D]) if DEBUG else nc.dram_tensor("cc_tout", [4, D], F32).ap()
    cc_ag = nc.dram_tensor("cc_ag", [32, 256], F32).ap()
    cc_ag2 = nc.dram_tensor("cc_ag2", [32, 256], F32).ap()
    cc_tps = nc.dram_tensor("cc_tps", [4, 256], F32).ap()
    ckv_scr = nc.dram_tensor("ckv_scr", [128, 4 * L], BF16).ap()
    ffx_scr = nc.dram_tensor("ffx_scr", [L // 2, D], F32).ap()
    d1w_scr = nc.dram_tensor("d1w_scr", [D, 1024], BF16).ap()
    d2w_scr = nc.dram_tensor("d2w_scr", [1024, D], BF16).ap()
    wout_scr = nc.dram_tensor("wout_scr", [1024, D], BF16).ap()
    if DEBUG:
        th_dbg = nc.dram_tensor("th_dbg", [6, 4, D], F32, kind="ExternalOutput").ap()
        tln_dbg = nc.dram_tensor("tln_dbg", [4, D], F32, kind="ExternalOutput").ap()
        t2_dbg = nc.dram_tensor("t2_dbg", [128, 32], F32, kind="ExternalOutput").ap()
        cin_dbg = nc.dram_tensor("cin_dbg", [4, 4, D], F32, kind="ExternalOutput").ap()
        cout_dbg = nc.dram_tensor("cout_dbg", [4, 4, D], F32, kind="ExternalOutput").ap()
        tln_dbg2 = nc.dram_tensor("tln_dbg2", [4, 4, D], F32, kind="ExternalOutput").ap()
        s_dbg = Sem if False else None

    st = ExitStack()
    with st:
        E = st.enter_context
        s_pe = Sem(st, nc, "s_pe")
        s_act = Sem(st, nc, "s_act")
        s_dve = Sem(st, nc, "s_dve")
        s_pl = Sem(st, nc, "s_pl")
        s_cc = Sem(st, nc, "s_cc")
        s_lc = Sem(st, nc, "s_lc")
        s_row = Sem(st, nc, "s_row")
        s_x2 = [Sem(st, nc, f"s_x{i}") for i in range(2)]
        s_st2 = [Sem(st, nc, f"s_s{i}") for i in range(2)]
        s_gl = Sem(st, nc, "s_gl")
        s_wk2 = [Sem(st, nc, f"s_w{i}") for i in range(2)]
        s_rot = Sem(st, nc, "s_rot")
        s_rt4 = [Sem(st, nc, f"s_rt{i}") for i in range(4)]
        s_kb2 = [Sem(st, nc, f"s_k{i}") for i in range(2)]
        s_qb2 = [Sem(st, nc, f"s_q{i}") for i in range(2)]
        s_lam = Sem(st, nc, "s_lam")
        s_sp = Sem(st, nc, "s_sp")
        s_ct = Sem(st, nc, "s_ct")
        s_dbg = Sem(st, nc, "s_dbg")
        s_out = Sem(st, nc, "s_out")
        s_res = Sem(st, nc, "s_res")
        s_d1s = Sem(st, nc, "s_d1s")
        s_d2s = Sem(st, nc, "s_d2s")
        s_rg = Sem(st, nc, "s_rg")
        s_gpc = Sem(st, nc, "s_gpc")
        s_po8 = [Sem(st, nc, f"s_po{i}") for i in range(8)]
        s_ow2 = [Sem(st, nc, f"s_ow{i}") for i in range(2)]
        s_fb4 = [Sem(st, nc, f"s_fb{i}") for i in range(4)]
        s_ob4 = [Sem(st, nc, f"s_ob{i}") for i in range(4)]

        _block = E(nc.Block())

        class _EngWrap:
            def __init__(self, eng, sem):
                self._eng = eng
                self._sem = sem

            def __getattr__(self, name):
                f = getattr(self._eng, name)
                sem, eng = self._sem, self._eng

                def g(*a, **k):
                    sem.wait(eng)
                    return sem.inc(f(*a, **k))
                return g

        VEw = _EngWrap(VE, s_dve)
        SCw = _EngWrap(SC, s_act)
        GPw = _EngWrap(GP, s_gpc)

        def recip_acc(out, in_, scratch):
            SCw.activation(scratch, in_, AF.Ln)
            freed = s_act.n
            SCw.activation(out, scratch, AF.Exp, scale=c_neg1[0:1, :])
            return freed, s_act.n

        ident_s = E(nc.sbuf_tensor("ident_s", [128, 128], BF16, side="left"))
        identf_s = E(nc.sbuf_tensor("identf_s", [128, 128], F32, side="left"))
        mask_s = E(nc.sbuf_tensor("mask_s", [128, 128], BF16, side="left"))
        ones_rf = E(nc.sbuf_tensor("ones_rf", [1, 128], F32, side="left"))
        ones_rb = E(nc.sbuf_tensor("ones_rb", [1, 128], BF16, side="left"))
        row_gb = E(nc.sbuf_tensor("row_gb", [1, D], BF16, side="left"))
        row_ab = row_gb
        ones_cb = E(nc.sbuf_tensor("ones_cb", [128, 1], BF16, side="left"))
        oneh4 = E(nc.sbuf_tensor("oneh4", [4, 1], F32, side="left"))
        c_eps5 = E(nc.sbuf_tensor("c_eps5", [128, 1], F32, side="left"))
        c_eps6 = E(nc.sbuf_tensor("c_eps6", [128, 1], F32, side="left"))
        c_half = E(nc.sbuf_tensor("c_half", [128, 1], F32, side="left"))
        c_nhalf = E(nc.sbuf_tensor("c_nhalf", [128, 1], F32, side="left"))
        c_n102 = E(nc.sbuf_tensor("c_n102", [128, 1], F32, side="left"))
        c_neg1 = E(nc.sbuf_tensor("c_neg1", [128, 1], F32, side="left"))
        c_qsc = E(nc.sbuf_tensor("c_qsc", [128, 1], F32, side="left"))
        lamT = E(nc.sbuf_tensor("lamT", [NKH, L], F32, side="left"))
        negB = E(nc.sbuf_tensor("negB", [128, NQH], F32, side="left"))
        mk2 = E(nc.sbuf_tensor("mk2", [1, NKH], F32, side="left"))
        mq2 = E(nc.sbuf_tensor("mq2", [1, NQH], F32, side="left"))
        m4t = E(nc.sbuf_tensor("m4t", [1, 4], F32, side="left"))
        v_bnG = E(nc.sbuf_tensor("v_bnG", [128, 48], F32, side="left"))
        v_agG = E(nc.sbuf_tensor("v_agG", [128, 4], F32, side="left"))
        sc11 = E(nc.sbuf_tensor("sc11", [1, 1], F32, side="left"))
        thought = E(nc.sbuf_tensor("thought", [4, D], F32, side="left"))
        png_s = E(nc.sbuf_tensor("png_s", [128, NT], F32, side="left"))
        pnb_s = E(nc.sbuf_tensor("pnb_s", [128, NT], F32, side="left"))
        d1b_s = E(nc.sbuf_tensor("d1b_s", [128, 8], F32, side="left"))
        xnl_fm = E(nc.sbuf_tensor("xnl_fm", [128, NT * 4], BF16, side="left"))
        row_a = E(nc.sbuf_tensor("row_a", [1, D], F32, side="left"))
        row_b = E(nc.sbuf_tensor("row_b", [1, D], F32, side="left"))
        s4tmp = E(nc.sbuf_tensor("s4tmp", [4, D], F32, side="left"))
        tln_f = E(nc.sbuf_tensor("tln_f", [4, D], F32, side="left"))
        bc_bc = E(nc.sbuf_tensor("bc_bc", [128, D], F32, side="left"))
        v_sum = E(nc.sbuf_tensor("v_sum", [128, 4], F32, side="left"))
        v_ssq = E(nc.sbuf_tensor("v_ssq", [128, 4], F32, side="left"))
        v_var = E(nc.sbuf_tensor("v_var", [128, 4], F32, side="left"))
        v_rstd = E(nc.sbuf_tensor("v_rstd", [128, 4], F32, side="left"))
        v_mb = E(nc.sbuf_tensor("v_mb", [128, 4], F32, side="left"))

        for t, val in ((ones_rf, 1.0), (ones_rb, 1.0),
                       (ones_cb, 1.0), (c_eps5, 1e-5),
                       (c_eps6, 1e-6), (c_half, 0.5), (c_nhalf, -0.5),
                       (c_n102, -1.02), (c_neg1, -1.0), (c_qsc, QSC)):
            s_pl.inc(GP.memset(t[:, :], val))
        s_lc.inc(SY.dma_start(ident_s[:, :], ident[:, :]), 16)
        s_lc.inc(SY.dma_start(identf_s[:, :], identf[:, :]), 16)
        s_lc.inc(SY.dma_start(mask_s[:, :], maskdiag[:, :]), 16)
        s_lc.inc(SY.dma_start(oneh4[:, :], onehot4[:, :]), 16)
        s_lc.inc(SY.dma_start(d1b_s[:, :], d1b.rearrange("(j p) -> p j", p=128),
                              allow_slow_non_contiguous=True), 16)
        s_lc.inc(SY.dma_start(png_s[:, :], png.rearrange("(j p) -> p j", p=128),
                              allow_slow_non_contiguous=True), 16)
        s_lc.inc(SY.dma_start(pnb_s[:, :], pnb.rearrange("(j p) -> p j", p=128),
                              allow_slow_non_contiguous=True), 16)
        lc_consts = s_lc.n

        def row1(ap1d):
            return ap1d.rearrange("(a d) -> a d", a=1)

        def ln_stats(tile, sq, n, eps_ap, r=0):
            vs = v_sum[0:n, r:r + 1]
            vq = v_ssq[0:n, r:r + 1]
            vv = v_var[0:n, r:r + 1]
            vr = v_rstd[0:n, r:r + 1]
            vm = v_mb[0:n, r:r + 1]
            s_act.wait(VE)
            VEw.reduce_sum(vs, tile[0:n, 0:D], axis=AX.X)
            s_dve.wait(SC)
            SCw.activation(sq[0:n, 0:D], tile[0:n, 0:D], AF.Square, accum_out=vq)
            s_act.wait(VE)
            VEw.tensor_scalar_mul(vs, vs, 1.0 / D)
            VEw.tensor_scalar_mul(vv, vq, 1.0 / D)
            VEw.tensor_scalar(vq, vs, vs, None, ALU.mult)
            VEw.tensor_sub(vv, vv, vq)
            s_dve.wait(SC)
            SCw.activation(vr, vv, AF.Ln, bias=eps_ap[0:n, :])
            SCw.activation(vr, vr, AF.Exp, scale=c_nhalf[0:n, :])
            s_act.wait(VE)
            VEw.tensor_scalar(vm, vs, vr, -1.0, ALU.mult, ALU.mult)

        def ln_stats_bn(tile, eps_ap, r):
            s_act.wait(VE)
            for k in range(4):
                VEw.bn_stats(v_bnG[:, r * 24 + k * 6:r * 24 + (k + 1) * 6],
                             tile[:, k * 512:(k + 1) * 512])
            VEw.bn_aggr(v_agG[:, 2 * r:2 * r + 2], v_bnG[:, r * 24:(r + 1) * 24])
            s_dve.wait(SC)
            SCw.activation(v_rstd[:, r:r + 1], v_agG[:, 2 * r + 1:2 * r + 2],
                           AF.Ln, bias=eps_ap[:, :])
            SCw.activation(v_rstd[:, r:r + 1], v_rstd[:, r:r + 1],
                           AF.Exp, scale=c_nhalf[:, :])
            s_act.wait(VE)
            VEw.tensor_scalar(v_mb[:, r:r + 1], v_agG[:, 2 * r:2 * r + 1],
                              v_rstd[:, r:r + 1], -1.0, ALU.mult, ALU.mult)

        def bcast_rows(ps, row_src, dst, n):
            for og in range(4):
                s_pe.wait(TE)
                s_act.wait(TE)
                s_dve.wait(TE)
                s_pe.inc(TE.matmul(ps[0:n, 0:512], ones_rf[0:1, 0:n],
                                   row_src[0:1, og * 512:(og + 1) * 512],
                                   start=True, stop=True))
                s_pe.wait(SC)
                SCw.copy(dst[0:n, og * 512:(og + 1) * 512], ps[0:n, 0:512])

        with ExitStack() as p0:
            E0 = p0.enter_context
            x4 = E0(nc.sbuf_tensor("x4", [4, D], F32, side="right"))
            xn4 = E0(nc.sbuf_tensor("xn4", [4, D], F32, side="right"))
            g4 = E0(nc.sbuf_tensor("g4", [4, D], F32, side="right"))
            b4 = E0(nc.sbuf_tensor("b4", [4, D], F32, side="right"))
            xnl_bf = E0(nc.sbuf_tensor("xnl_bf", [4, D], BF16, side="right"))
            ps4 = E0(nc.psum_tensor("ps4", [4, 512], F32))
            pstr0 = E0(nc.psum_tensor("pstr0", [128, NT * 4], BF16))

            s_x2[0].inc(SY.dma_start(x4[:, :], x_last4[:, :]), 16)
            s_row.inc(SY.dma_start(row_a[:, :], row1(ing)), 16)
            s_row.inc(SY.dma_start(row_b[:, :], row1(inb)), 16)
            for eng in (VE, SC, TE):
                s_lc.wait(eng)
                s_row.wait(eng)
                s_x2[0].wait(eng)
                s_pl.wait(eng)
            bcast_rows(ps4, row_a, g4, 4)
            bcast_rows(ps4, row_b, b4, 4)
            ln_stats(x4, xn4, 4, c_eps5)
            s_dve.wait(SC)
            SCw.activation(xn4[0:4, :], x4[0:4, :], AF.Identity,
                                    scale=v_rstd[0:4, 0:1], bias=v_mb[0:4, 0:1])
            ln_stats(xn4, tln_f, 4, c_eps6)
            s_dve.wait(SC)
            SCw.activation(tln_f[0:4, :], xn4[0:4, :], AF.Identity,
                                    scale=v_rstd[0:4, 0:1], bias=v_mb[0:4, 0:1])
            s_act.wait(VE)
            VEw.tensor_mul(tln_f[0:4, :], tln_f[0:4, :], g4[0:4, :])
            VEw.tensor_add(tln_f[0:4, :], tln_f[0:4, :], b4[0:4, :])
            VEw.tensor_copy(xnl_bf[0:4, :], tln_f[0:4, :])
            s_dve.wait(TE)
            for ch in range(NT):
                s_pe.inc(TE.transpose(pstr0[:, ch * 4:(ch + 1) * 4],
                                      xnl_bf[0:4, ch * 128:(ch + 1) * 128],
                                      ident_s[0:4, 0:4]))
            s_pe.wait(SC)
            SCw.copy(xnl_fm[:, :], pstr0[:, :])

        with ExitStack() as p13:
            E1 = p13.enter_context
            xn_fm = E1(nc.sbuf_tensor("xn_fm", [128, NT * L], BF16, side="right"))
            cosk_s = E1(nc.sbuf_tensor("cosk_s", [128, L], F32, side="right"))
            sinkS_s = E1(nc.sbuf_tensor("sinkS_s", [128, L], F32, side="right"))
            blam_s = E1(nc.sbuf_tensor("blam_s", [NKH, 1], F32, side="right"))
            s_act.wait(SY)
            s_lc.inc(SY.dma_start(cosk_s[:, :], cosk[:, :]), 16)
            s_lc.inc(SY.dma_start(sinkS_s[:, :], sinkS[:, :]), 16)
            s_lc.inc(SY.dma_start(blam_s[:, :],
                                  blam.rearrange("(p a) -> p a", a=1)), 16)
            lc_all = s_lc.n

            with ExitStack() as p1:
                E2 = p1.enter_context
                xt = [E2(nc.sbuf_tensor(f"xt{i}", [128, D], F32, side="right"))
                      for i in range(2)]
                xnbf = [E2(nc.sbuf_tensor(f"xnbf{i}", [128, D], BF16, side="right"))
                        for i in range(2)]
                trp = [E2(nc.psum_tensor(f"trp{i}", [128, 512], BF16))
                       for i in range(2)]
                ld_hist, act_xn, pe_tile, evac_hist = [], [], [], []
                s_x2[0].inc(SY.dma_start(xt[0][:, :], x_b[0:128, :]), 16)
                ld_hist.append(s_x2[0].n)
                for tt in range(NT):
                    bi = tt % 2
                    if tt + 1 < NT:
                        if tt >= 1:
                            s_act.wait(SY, act_xn[tt - 1])
                        s_x2[(tt + 1) % 2].inc(
                            SY.dma_start(xt[(tt + 1) % 2][:, :],
                                         x_b[(tt + 1) * 128:(tt + 2) * 128, :]), 16)
                        ld_hist.append(s_x2[(tt + 1) % 2].n)
                    s_x2[bi].wait(VE, ld_hist[tt])
                    s_x2[bi].wait(SC, ld_hist[tt])
                    ln_stats_bn(xt[bi], c_eps5, r=bi)
                    s_dve.wait(SC)
                    if tt >= 2:
                        s_pe.wait(SC, pe_tile[tt - 2])
                    for grp in range(4):
                        SCw.activation(xnbf[bi][:, grp * 512:(grp + 1) * 512],
                                       xt[bi][:, grp * 512:(grp + 1) * 512],
                                       AF.Identity,
                                       scale=v_rstd[:, bi:bi + 1],
                                       bias=v_mb[:, bi:bi + 1])
                        s_act.wait(TE)
                        if len(evac_hist) >= 2:
                            s_dve.wait(TE, evac_hist[-2])
                        for j in range(4):
                            ch = grp * 4 + j
                            s_pe.inc(TE.transpose(
                                trp[grp % 2][:, j * 128:(j + 1) * 128],
                                xnbf[bi][:, ch * 128:(ch + 1) * 128], ident_s[:, :]))
                        s_pe.wait(VE)
                        dst = xn_fm[:, :].rearrange("p (c t) -> p c t", c=NT)[
                            :, grp * 4:(grp + 1) * 4, tt * 128:(tt + 1) * 128]
                        src = trp[grp % 2][:, :].rearrange("p (c t) -> p c t", c=4)
                        VEw.tensor_copy(dst, src)
                        evac_hist.append(s_dve.n)
                    act_xn.append(s_act.n)
                    pe_tile.append(s_pe.n)
            xn_ready_dve = s_dve.n
            xn_ready = s_act.n
            pvX = ExitStack()

            with ExitStack() as p3:
                E2 = p3.enter_context
                cq_fm = E2(nc.sbuf_tensor("cq_fm", [128, 8 * L], BF16, side="right"))
                pre = E2(nc.sbuf_tensor("pre3", [128, L], F32, side="right"))
                rot = E2(nc.sbuf_tensor("rot3", [128, L], F32, side="right"))
                rq_bf = E2(nc.sbuf_tensor("rq_bf", [128, L], BF16, side="right"))
                rsq = E2(nc.sbuf_tensor("rsq3", [128, L], BF16, side="right"))
                wlam_bf = E2(nc.sbuf_tensor("wlam_bf", [128, NT * NKH], BF16,
                                            side="right"))
                wqb = [E2(nc.sbuf_tensor(f"wqb{i}", [128, NT * 128], BF16,
                                         side="right")) for i in range(2)]
                mmp = [E2(nc.psum_tensor(f"mm3{i}", [128, 512], F32)) for i in range(2)]
                nrmp = E2(nc.psum_tensor("nrmp3", [1, 512], F32))

                s_lam.inc(GP.dma_start(
                    wlam_bf[:, :].rearrange("p (c n) -> p c n", c=NT),
                    wlam.rearrange("(c p) n -> p c n", p=128)), 16)
                lam_w = s_lam.n
                wdq_r = wdq.rearrange("(c p) n -> p c n", p=128)

                mm_user = [0, 0]
                wm_hist, pe_m = [], []
                for m in range(8):
                    bi = m % 2
                    if m >= 2:
                        s_pe.wait(GP, pe_m[m - 2])
                    s_wk2[bi].inc(GP.dma_start(
                        wqb[bi][:, :].rearrange("p (c n) -> p c n", c=NT),
                        wdq_r[:, :, m * 128:(m + 1) * 128]), 16)
                    wm_hist.append(s_wk2[bi].n)
                    for tg in range(4):
                        pi = (m * 4 + tg) % 2
                        if m == 0 and tg == 0:
                            s_act.wait(TE, xn_ready)
                            s_dve.wait(TE, xn_ready_dve)
                            s_dve.wait(TE)
                        s_wk2[bi].wait(TE, wm_hist[m])
                        s_act.wait(TE, mm_user[pi])
                        for ch in range(NT):
                            s_pe.inc(TE.matmul(
                                mmp[pi][:, :],
                                wqb[bi][:, ch * 128:(ch + 1) * 128],
                                xn_fm[:, ch * L + tg * 512:ch * L + (tg + 1) * 512],
                                start=(ch == 0), stop=(ch == NT - 1)))
                        s_pe.wait(SC)
                        SCw.copy(
                            cq_fm[:, m * L + tg * 512:m * L + (tg + 1) * 512],
                            mmp[pi][:, :])
                        mm_user[pi] = s_act.n
                    pe_m.append(s_pe.n)

                wuq_r = wuq.rearrange("(c p) n -> p c n", p=128)
                uq_hist, pe_qh = [], []
                q_spill = 0
                nrm_pe = 0
                pre_free = [0] * 4
                rot_free = [0] * 4
                for qh in range(NQH):
                    bi = qh % 2
                    if qh >= 2:
                        s_pe.wait(GP, pe_qh[qh - 2])
                    else:
                        s_pe.wait(GP, pe_m[qh - 2])
                    s_wk2[bi].inc(GP.dma_start(
                        wqb[bi][:, :].rearrange("p (c n) -> p c n", c=NT)[:, 0:8, :],
                        wuq_r[:, :, qh * 128:(qh + 1) * 128]), 16)
                    uq_hist.append(s_wk2[bi].n)
                    if qh < 8:
                        s_res.inc(GP.dma_start(
                            d1w_scr[qh * 256:(qh + 1) * 256, :],
                            d1w[qh * 256:(qh + 1) * 256, :]), 16)
                        s_res.inc(GP.dma_start(
                            d2w_scr[qh * 128:(qh + 1) * 128, :],
                            d2w[qh * 128:(qh + 1) * 128, :]), 16)
                    elif qh < 12:
                        s_res.inc(GP.dma_start(
                            wout_scr[(qh - 8) * 256:(qh - 7) * 256, :],
                            wout[(qh - 8) * 256:(qh - 7) * 256, :]), 16)
                    for tg in range(4):
                        pi = tg % 2
                        lo, hi = tg * 512, (tg + 1) * 512
                        s_wk2[bi].wait(TE, uq_hist[qh])
                        s_act.wait(TE, mm_user[pi])
                        for c in range(8):
                            s_pe.inc(TE.matmul(
                                mmp[pi][:, :],
                                wqb[bi][:, c * 128:(c + 1) * 128],
                                cq_fm[:, c * L + lo:c * L + hi],
                                start=(c == 0), stop=(c == 7)))
                        s_pe.wait(SC)
                        s_dve.wait(SC, pre_free[tg])
                        SCw.activation(pre[:, lo:hi], mmp[pi][:, :],
                                       AF.Identity, scale=c_qsc[:, :])
                        mm_user[pi] = s_act.n
                        evn = s_act.n
                        s_act.wait(SY, evn)
                        s_dve.wait(SY, rot_free[tg])
                        s_rt4[tg].inc(SY.dma_start(rot[0:64, lo:hi],
                                                   pre[64:128, lo:hi]), 16)
                        s_rt4[tg].inc(SY.dma_start(rot[64:128, lo:hi],
                                                   pre[0:64, lo:hi]), 16)
                        rotw = s_rt4[tg].n
                        s_act.wait(VE, evn)
                        if qh == 0 and tg == 0:
                            s_lc.wait(VE, lc_all)
                        s_rt4[tg].wait(VE, rotw)
                        VEw.tensor_mul(pre[:, lo:hi], pre[:, lo:hi],
                                       cosk_s[:, lo:hi])
                        pre_free[tg] = s_dve.n
                        VEw.tensor_mul(rot[:, lo:hi], rot[:, lo:hi],
                                       sinkS_s[:, lo:hi])
                        rot_free[tg] = s_dve.n
                        if tg == 0:
                            s_sp.wait(VE, q_spill)
                        VEw.tensor_tensor(rq_bf[:, lo:hi], pre[:, lo:hi],
                                          rot[:, lo:hi], ALU.add)
                        if tg == 0:
                            s_pe.wait(VE, nrm_pe)
                        VEw.tensor_mul(rsq[:, lo:hi], rq_bf[:, lo:hi],
                                       rq_bf[:, lo:hi])
                    pe_qh.append(s_pe.n)
                    qdone = s_dve.n
                    s_dve.wait(GP, qdone)
                    s_sp.inc(GP.dma_start(q_scr[qh, :, :], rq_bf[:, :]), 16)
                    q_spill = s_sp.n
                    s_dve.wait(TE, qdone)
                    for tg in range(4):
                        s_pe.inc(TE.matmul(nrmp[0:1, :], ones_cb[:, :],
                                           rsq[:, tg * 512:(tg + 1) * 512],
                                           start=(tg == 0), stop=(tg == 3),
                                           skip_group_check=True))
                    nrm_pe = s_pe.n
                    s_pe.wait(VE, nrm_pe)
                    VEw.reduce_max(mq2[0:1, qh:qh + 1], nrmp[0:1, :],
                                   axis=AX.X)

                for tg in range(4):
                    s_act.wait(TE)
                    s_dve.wait(TE)
                    s_lam.wait(TE, lam_w)
                    for ch in range(NT):
                        s_pe.inc(TE.matmul(
                            mmp[0][0:NKH, :],
                            wlam_bf[:, ch * NKH:(ch + 1) * NKH],
                            xn_fm[:, ch * L + tg * 512:ch * L + (tg + 1) * 512],
                            start=(ch == 0), stop=(ch == NT - 1)))
                    s_pe.wait(SC)
                    s_lc.wait(SC, lc_all)
                    SCw.activation(lamT[0:NKH, tg * 512:(tg + 1) * 512],
                                            mmp[0][0:NKH, :], AF.Sigmoid,
                                            bias=blam_s[0:NKH, :])

            with ExitStack() as p2:
                E2 = p2.enter_context
                ckv_fm = E2(nc.sbuf_tensor("ckv_fm", [128, 4 * L], BF16, side="right"))
                wuk_bf = E2(nc.sbuf_tensor("wuk_bf", [128, 4 * 1024], BF16, side="right"))
                wkvb = [E2(nc.sbuf_tensor(f"wkvb{i}", [128, NT * 128], BF16,
                                          side="right")) for i in range(2)]
                pre = E2(nc.sbuf_tensor("pre", [128, L], F32, side="right"))
                rot = E2(nc.sbuf_tensor("rot", [128, L], F32, side="right"))
                rk_bf = E2(nc.sbuf_tensor("rk_bf", [128, L], BF16, side="right"))
                rsq = E2(nc.sbuf_tensor("rsq", [128, L], BF16, side="right"))
                mmp = [E2(nc.psum_tensor(f"mmp{i}", [128, 512], F32)) for i in range(2)]
                nrmp = E2(nc.psum_tensor("nrmp", [1, 512], F32))

                s_gl.inc(GP.dma_start(
                    wuk_bf[:, :].rearrange("p (c n) -> p c n", c=4),
                    wuk.rearrange("(c p) n -> p c n", p=128)), 16)
                wready = s_gl.n
                wdkv_r = wdkv.rearrange("(c p) n -> p c n", p=128)

                s_act.wait(TE)
                s_dve.wait(TE)
                evh = []
                wm_hist, pe_m = [], []
                for m in range(4):
                    bi = m % 2
                    if m >= 2:
                        s_pe.wait(GP, pe_m[m - 2])
                    else:
                        s_pe.wait(GP)
                    s_wk2[bi].inc(GP.dma_start(
                        wkvb[bi][:, :].rearrange("p (c n) -> p c n", c=NT),
                        wdkv_r[:, :, m * 128:(m + 1) * 128]), 16)
                    wm_hist.append(s_wk2[bi].n)
                    for tg in range(4):
                        pi = (m * 4 + tg) % 2
                        s_wk2[bi].wait(TE, wm_hist[m])
                        if len(evh) >= 2:
                            s_act.wait(TE, evh[-2])
                        for ch in range(NT):
                            s_pe.inc(TE.matmul(
                                mmp[pi][:, :],
                                wkvb[bi][:, ch * 128:(ch + 1) * 128],
                                xn_fm[:, ch * L + tg * 512:ch * L + (tg + 1) * 512],
                                start=(ch == 0), stop=(ch == NT - 1)))
                        s_pe.wait(SC)
                        SCw.copy(
                            ckv_fm[:, m * L + tg * 512:m * L + (tg + 1) * 512],
                            mmp[pi][:, :])
                        evh.append(s_act.n)
                    pe_m.append(s_pe.n)

                k_spill = s_sp.n
                nrm_pe = 0
                pre_free = [0] * 4
                rot_free = [0] * 4
                for kh in range(NKH):
                    for tg in range(4):
                        lo, hi = tg * 512, (tg + 1) * 512
                        if kh == 0 and tg == 0:
                            s_gl.wait(TE, wready)
                        if len(evh) >= 2:
                            s_act.wait(TE, evh[-2])
                        for c in range(4):
                            s_pe.inc(TE.matmul(
                                mmp[tg % 2][:, :],
                                wuk_bf[:, c * 1024 + kh * 128:c * 1024 + (kh + 1) * 128],
                                ckv_fm[:, c * L + lo:c * L + hi],
                                start=(c == 0), stop=(c == 3)))
                        s_pe.wait(SC)
                        s_dve.wait(SC, pre_free[tg])
                        SCw.copy(pre[:, lo:hi], mmp[tg % 2][:, :])
                        evh.append(s_act.n)
                        evn = s_act.n
                        s_act.wait(SY, evn)
                        s_dve.wait(SY, rot_free[tg])
                        s_rt4[tg].inc(SY.dma_start(rot[0:64, lo:hi],
                                                   pre[64:128, lo:hi]), 16)
                        s_rt4[tg].inc(SY.dma_start(rot[64:128, lo:hi],
                                                   pre[0:64, lo:hi]), 16)
                        rotw = s_rt4[tg].n
                        s_act.wait(VE, evn)
                        s_rt4[tg].wait(VE, rotw)
                        VEw.tensor_mul(pre[:, lo:hi], pre[:, lo:hi],
                                       cosk_s[:, lo:hi])
                        pre_free[tg] = s_dve.n
                        VEw.tensor_mul(rot[:, lo:hi], rot[:, lo:hi],
                                       sinkS_s[:, lo:hi])
                        rot_free[tg] = s_dve.n
                        if tg == 0:
                            s_sp.wait(VE, k_spill)
                        VEw.tensor_tensor(rk_bf[:, lo:hi], pre[:, lo:hi],
                                          rot[:, lo:hi], ALU.add)
                        if tg == 0:
                            s_pe.wait(VE, nrm_pe)
                        VEw.tensor_mul(rsq[:, lo:hi], rk_bf[:, lo:hi],
                                       rk_bf[:, lo:hi])
                    kdone = s_dve.n
                    s_dve.wait(GP, kdone)
                    s_sp.inc(GP.dma_start(k_scr[kh, :, :], rk_bf[:, :]), 16)
                    k_spill = s_sp.n
                    s_dve.wait(TE, kdone)
                    for tg in range(4):
                        s_pe.inc(TE.matmul(nrmp[0:1, :], ones_cb[:, :],
                                           rsq[:, tg * 512:(tg + 1) * 512],
                                           start=(tg == 0), stop=(tg == 3),
                                           skip_group_check=True))
                    nrm_pe = s_pe.n
                    s_pe.wait(VE, nrm_pe)
                    VEw.reduce_max(mk2[0:1, kh:kh + 1], nrmp[0:1, :],
                                   axis=AX.X)

                s_act.wait(GP)
                s_sp.inc(GP.dma_start(ckv_scr[:, :], ckv_fm[:, :]), 16)
                ckv_spilled = s_sp.n

                for qh in range(NQH):
                    kh = qh // 2
                    s_act.wait(VE)
                    s_pe.wait(VE)
                    VEw.tensor_tensor(sc11[0:1, :], mq2[0:1, qh:qh + 1],
                                               mk2[0:1, kh:kh + 1], ALU.mult)
                    s_dve.wait(SC)
                    SCw.activation(sc11[0:1, :], sc11[0:1, :], AF.Ln)
                    SCw.activation(sc11[0:1, :], sc11[0:1, :], AF.Exp,
                                            scale=c_half[0:1, :])
                    s_act.wait(TE)
                    s_pe.inc(TE.matmul(mmp[1][:, 0:1], ones_rf[0:1, :],
                                       sc11[0:1, 0:1], start=True, stop=True))
                    s_pe.wait(SC)
                    SCw.activation(negB[:, qh:qh + 1], mmp[1][:, 0:1],
                                            AF.Identity, scale=c_n102[:, :])

        v_full = pvX.enter_context(
            nc.sbuf_tensor("v_full", [128, NKH * NT * 128], BF16, side="left"))
        with ExitStack() as p2b:
            E2 = p2b.enter_context
            ckv2 = E2(nc.sbuf_tensor("ckv2", [128, 4 * L], BF16, side="right"))
            wuv_bf = E2(nc.sbuf_tensor("wuv_bf", [128, 4 * 1024], BF16, side="right"))
            mmp = [E2(nc.psum_tensor(f"mmv{i}", [128, 512], F32)) for i in range(2)]
            s_sp.wait(GP, ckv_spilled)
            s_gl.inc(GP.dma_start(ckv2[:, :], ckv_scr[:, :]), 16)
            s_gl.inc(GP.dma_start(
                wuv_bf[:, :].rearrange("p (c n) -> p c n", c=4),
                wuv.rearrange("(c p) n -> p c n", p=128)), 16)
            vready = s_gl.n
            s_act.wait(TE)
            s_dve.wait(TE)
            evh = []
            for kt in range(NT):
                for og in range(2):
                    pi = (kt * 2 + og) % 2
                    if kt == 0 and og == 0:
                        s_gl.wait(TE, vready)
                    if len(evh) >= 2:
                        s_act.wait(TE, evh[-2])
                    for c in range(4):
                        s_pe.inc(TE.matmul(
                            mmp[pi][:, :],
                            ckv2[:, c * L + kt * 128:c * L + (kt + 1) * 128],
                            wuv_bf[:, c * 1024 + og * 512:c * 1024 + (og + 1) * 512],
                            start=(c == 0), stop=(c == 3)))
                    s_pe.wait(SC)
                    dst = v_full[:, :].rearrange(
                        "p (h t d) -> p h t d", h=NKH, t=NT)[
                        :, og * 4:(og + 1) * 4, kt, :]
                    src = mmp[pi][:, :].rearrange("p (h d) -> p h d", h=4)
                    SCw.copy(dst, src)
                    evh.append(s_act.n)

        with ExitStack() as p35:
            E2 = p35.enter_context
            xno_fm = E2(nc.sbuf_tensor("xno_fm", [128, NT * 1024], BF16, side="right"))
            with ExitStack() as p35a:
                E3 = p35a.enter_context
                xt = [E3(nc.sbuf_tensor(f"xot{i}", [128, D], F32, side="right"))
                      for i in range(2)]
                xnbf = [E3(nc.sbuf_tensor(f"xobf{i}", [128, D], BF16, side="right"))
                        for i in range(2)]
                trp = [E3(nc.psum_tensor(f"otrp{i}", [128, 512], BF16))
                       for i in range(2)]
                ld_hist, act_xn, pe_tile, evac_hist = [], [], [], []
                s_dve.wait(SY)
                s_x2[0].inc(SY.dma_start(xt[0][:, :], x_own[0:128, :]), 16)
                ld_hist.append(s_x2[0].n)
                for tt in range(8):
                    bi = tt % 2
                    if tt + 1 < 8:
                        if tt >= 1:
                            s_act.wait(SY, act_xn[tt - 1])
                        s_x2[(tt + 1) % 2].inc(
                            SY.dma_start(xt[(tt + 1) % 2][:, :],
                                         x_own[(tt + 1) * 128:(tt + 2) * 128, :]), 16)
                        ld_hist.append(s_x2[(tt + 1) % 2].n)
                    s_x2[bi].wait(VE, ld_hist[tt])
                    s_x2[bi].wait(SC, ld_hist[tt])
                    ln_stats_bn(xt[bi], c_eps5, r=bi)
                    s_dve.wait(SC)
                    if tt >= 2:
                        s_pe.wait(SC, pe_tile[tt - 2])
                    for grp in range(4):
                        SCw.activation(xnbf[bi][:, grp * 512:(grp + 1) * 512],
                                       xt[bi][:, grp * 512:(grp + 1) * 512],
                                       AF.Identity,
                                       scale=v_rstd[:, bi:bi + 1],
                                       bias=v_mb[:, bi:bi + 1])
                        s_act.wait(TE)
                        if len(evac_hist) >= 2:
                            s_dve.wait(TE, evac_hist[-2])
                        for j in range(4):
                            ch = grp * 4 + j
                            s_pe.inc(TE.transpose(
                                trp[grp % 2][:, j * 128:(j + 1) * 128],
                                xnbf[bi][:, ch * 128:(ch + 1) * 128], ident_s[:, :]))
                        s_pe.wait(VE)
                        dst = xno_fm[:, :].rearrange("p (c t) -> p c t", c=NT)[
                            :, grp * 4:(grp + 1) * 4, tt * 128:(tt + 1) * 128]
                        src = trp[grp % 2][:, :].rearrange("p (c t) -> p c t", c=4)
                        VEw.tensor_copy(dst, src)
                        evac_hist.append(s_dve.n)
                    act_xn.append(s_act.n)
                    pe_tile.append(s_pe.n)

            with ExitStack() as p35b:
                E3 = p35b.enter_context
                gwb = E3(nc.sbuf_tensor("gwb", [128, NT * 512], BF16, side="right"))
                gstage = [E3(nc.sbuf_tensor(f"gst{i}", [128, 512], F32, side="right"))
                          for i in range(2)]
                gp_ps = [E3(nc.psum_tensor(f"gp{i}", [128, 512], F32))
                         for i in range(2)]
                gatew_r = gatew.rearrange("(c p) n -> p c n", p=128)
                s_act.wait(SY)
                s_pe.wait(SY)
                s_act.wait(TE)
                s_dve.wait(TE)
                s_rg.inc(GP.dma_start(row_gb[:, :], row1(gateb)), 16)
                gb_ld = s_rg.n
                st_hist = []
                gp_user = [0, 0]
                for og in range(4):
                    s_pe.wait(GP)
                    s_wk2[0].inc(GP.dma_start(
                        gwb[:, :].rearrange("p (c n) -> p c n", c=NT),
                        gatew_r[:, :, og * 512:(og + 1) * 512]), 16)
                    gw_now = s_wk2[0].n
                    for qt in range(8):
                        pi = qt % 2
                        s_wk2[0].wait(TE, gw_now)
                        if og == 0 and qt == 0:
                            s_rg.wait(TE, gb_ld)
                        if len(st_hist) >= 2:
                            sem, val = st_hist[-2]
                            sem.wait(TE, val)
                        s_act.wait(TE, gp_user[pi])
                        for ch in range(NT):
                            s_pe.inc(TE.matmul(
                                gp_ps[pi][:, :],
                                xno_fm[:, ch * 1024 + qt * 128:ch * 1024 + (qt + 1) * 128],
                                gwb[:, ch * 512:(ch + 1) * 512],
                                start=(ch == 0), stop=False,
                                skip_group_check=True))
                        s_pe.inc(TE.matmul(
                            gp_ps[pi][:, :], ones_rb[0:1, 0:128],
                            row_gb[0:1, og * 512:(og + 1) * 512],
                            start=False, stop=True, skip_group_check=True))
                        s_pe.wait(SC)
                        SCw.activation(gstage[pi][:, :], gp_ps[pi][:, :],
                                                AF.Sigmoid)
                        gp_user[pi] = s_act.n
                        s_act.wait(SY)
                        s_st2[pi].inc(SY.dma_start(
                            gate_scr[qt * 128:(qt + 1) * 128,
                                     og * 512:(og + 1) * 512], gstage[pi][:, :]), 16)
                        st_hist.append((s_st2[pi], s_st2[pi].n))

        combT = pvX.enter_context(
            nc.sbuf_tensor("combT", [128, NKH * L], BF16, side="left"))

        with ExitStack() as p4:
            E2 = p4.enter_context

            th_d1pe = [0, 0]
            with ExitStack() as p4a:
                E3 = p4a.enter_context
                awb = E3(nc.sbuf_tensor("awb", [128, NT * 256], BF16, side="right"))
                th4p = E3(nc.psum_tensor("th4p", [4, 512], F32))
                s_act.wait(SY)
                s_pe.wait(SY)
                s_row.inc(SY.dma_start(row_a[0:1, 0:256], row1(aggb)), 16)
                agb_ld = s_row.n
                s_pe.wait(GP)
                s_act.wait(GP)
                s_st2[0].wait(GP)
                s_st2[1].wait(GP)
                s_wk2[0].inc(GP.dma_start(
                    awb[:, :].rearrange("p (c n) -> p c n", c=NT),
                    aggw.rearrange("(c p) n -> p c n", p=128)), 16)
                aw_ld = s_wk2[0].n
                s_wk2[0].wait(TE, aw_ld)
                s_row.wait(TE, agb_ld)
                s_act.wait(TE)
                s_dve.wait(TE)
                for ch in range(NT):
                    s_pe.inc(TE.matmul(
                        th4p[0:4, 0:256],
                        xnl_fm[:, ch * 4:(ch + 1) * 4],
                        awb[:, ch * 256:(ch + 1) * 256],
                        start=(ch == 0), stop=False, skip_group_check=True))
                s_pe.inc(TE.matmul(
                    th4p[0:4, 0:256], ones_rf[0:1, 0:4], row_a[0:1, 0:256],
                    start=False, stop=True, skip_group_check=True))
                th_d1pe[0] = th_d1pe[1] = s_pe.n
                s_pe.wait(SC)
                SCw.copy(s4tmp[0:4, 0:256], th4p[0:4, 0:256])
                s_act.wait(GP)
                s_ct.inc(GP.dma_start(cc_tps[:, :], s4tmp[0:4, 0:256]), 16)
                GP.wait_ge(s_ct.h, s_ct.n)
                s_cc.inc(GP.collective_compute(
                    "AllGather", ALU.bypass,
                    replica_groups=[[0, 1, 2, 3, 4, 5, 6, 7]],
                    ins=[cc_tps[:, :].opt()], outs=[cc_ag[:, :].opt()]))

            kbuf = [E2(nc.sbuf_tensor(f"kbuf{i}", [128, L], BF16, side="right"))
                    for i in range(2)]
            qbuf = [E2(nc.sbuf_tensor(f"qbuf{i}", [128, L], BF16, side="right"))
                    for i in range(2)]
            prb = [E2(nc.sbuf_tensor(f"prb{i}", [128, 512], BF16, side="right"))
                   for i in range(2)]
            s1b = E2(nc.sbuf_tensor("s1b", [128, 512], F32, side="right"))
            s2b = E2(nc.sbuf_tensor("s2b", [128, 512], F32, side="right"))
            tmp1 = E2(nc.sbuf_tensor("tmp1", [128, 512], F32, side="right"))
            tmp2 = E2(nc.sbuf_tensor("tmp2", [128, 512], F32, side="right"))
            lamx = E2(nc.sbuf_tensor("lamx", [1, L], F32, side="right"))
            rrow = E2(nc.sbuf_tensor("rrow", [1, 2560], F32, side="right"))
            rrowb = E2(nc.sbuf_tensor("rrowb", [1, 1024], BF16, side="right"))
            t2T = E2(nc.sbuf_tensor("t2T", [128, 32], BF16, side="right"))
            tlnT = E2(nc.sbuf_tensor("tlnT", [128, 64], BF16, side="right"))
            bcwb = E2(nc.sbuf_tensor("bcwb", [128, NT * 256], BF16, side="right"))
            d1wb = [E2(nc.sbuf_tensor(f"d1wb{i}", [128, 2048], BF16, side="right"))
                    for i in range(2)]
            d2wb = [E2(nc.sbuf_tensor(f"d2wb{i}", [128, 4096], BF16, side="right"))
                    for i in range(2)]
            scp = [E2(nc.psum_tensor(f"scp{i}", [128, 512], F32)) for i in range(2)]
            atp = [E2(nc.psum_tensor(f"atp{i}", [128, 512], F32)) for i in range(2)]
            denp = [E2(nc.psum_tensor(f"denp{i}", [1, 512], F32)) for i in range(2)]
            auxp = E2(nc.psum_tensor("auxp", [128, 512], F32))
            thp = E2(nc.psum_tensor("thp", [128, 512], F32))

            kq_hist = {}
            q_hist = {}
            pending_s2 = [None]
            comb1_pe = [0]

            def comb1_part():
                kh_, g_, s2d = pending_s2[0]
                pending_s2[0] = None
                s_dve.wait(TE, s2d)
                s_act.wait(TE)
                s_pe.inc(TE.matmul(auxp[:, :], ones_rb[0:1, :],
                                   rrowb[0:1, 512:1024], start=True, stop=True))
                comb1_pe[0] = s_pe.n
                s_pe.wait(SC)
                s_dve.wait(SC, at_bank_user[1])
                SCw.copy(s2b[:, :], auxp[:, :])
                e2 = s_act.n
                s_act.wait(VE, e2)
                VEw.tensor_mul(tmp2[:, :], atp[1][:, :], s2b[:, :])
                at_bank_user[1] = s_dve.n
                VEw.tensor_tensor(
                    combT[:, kh_ * L + g_ * 512:kh_ * L + (g_ + 1) * 512],
                    tmp1[:, :], tmp2[:, :], ALU.subtract)
            pe_kh_end = [0] * NKH
            pe_br_end = {}
            lam_hist = [0] * (NKH + 1)
            sc_bank_user = [0, 0]
            pr_bank_user = [0, 0]
            at_bank_user = [0, 0]
            den_bank_user = [0, 0]
            den_ready = [0, 0]
            cc_done = 0

            def load_k(kh):
                if kh >= 2:
                    s_pe.wait(SY, pe_kh_end[kh - 2])
                s_kb2[kh % 2].inc(SY.dma_start(kbuf[kh % 2][:, :],
                                               k_scr[kh, :, :]), 16)
                kq_hist[kh] = s_kb2[kh % 2].n

            def load_q(kh, br):
                j = kh * 2 + br
                if j >= 2:
                    s_pe.wait(SY, pe_br_end[j - 2])
                s_qb2[j % 2].inc(SY.dma_start(qbuf[j % 2][:, :],
                                              q_scr[j, :, :]), 16)
                q_hist[j] = s_qb2[j % 2].n

            def load_lam(kh):
                s_dve.wait(GP)
                s_act.wait(GP)
                s_lam.inc(GP.dma_start(lamx[0:1, :], lamT[kh:kh + 1, :]), 16)
                lam_hist[kh] = s_lam.n

            tln_mark = [0]
            d1_hist = [None, None]

            def d1_load(blk):
                sl = d1w_scr[blk * 256:(blk + 1) * 256, :].rearrange(
                    "(c p) n -> p c n", p=128)
                bi = blk % 2
                if bi == 0:
                    s_wk2[0].inc(GP.dma_start(
                        d1wb[0][:, :].rearrange("p (c n) -> p c n", c=2), sl), 16)
                    d1_hist[0] = (s_wk2[0], s_wk2[0].n)
                else:
                    s_d1s.inc(SY.dma_start(
                        d1wb[1][:, :].rearrange("p (c n) -> p c n", c=2), sl), 16)
                    d1_hist[1] = (s_d1s, s_d1s.n)

            def thought_partA(step):
                if step == 0:
                    thought_init_finalize()
                else:
                    thought_step_finalize(step - 1)
                s_act.wait(VE)
                ln_stats(thought, tln_f, 4, c_eps6, r=2)
                s_dve.wait(SC)
                SCw.activation(tln_f[0:4, :], thought[0:4, :], AF.Identity,
                                        scale=v_rstd[0:4, 2:3], bias=v_mb[0:4, 2:3])
                tln_mark[0] = s_act.n
                if step == 0:
                    s_res.wait(GP)
                    s_res.wait(SY)
                s_pe.wait(GP, th_d1pe[0])
                d1_load(0)
                s_pe.wait(SY, th_d1pe[1])
                d1_load(1)

            def thought_partB(step):
                s_act.wait(TE, tln_mark[0])
                s_dve.wait(TE)
                for ch in range(NT):
                    s_pe.inc(TE.transpose(thp[:, 64 + ch * 4:68 + ch * 4],
                                          tln_f[0:4, ch * 128:(ch + 1) * 128],
                                          identf_s[0:4, 0:4]))
                s_pe.wait(SC)
                for ch in range(NT):
                    SCw.activation(tlnT[:, ch * 4:(ch + 1) * 4],
                                            thp[:, 64 + ch * 4:68 + ch * 4],
                                            AF.Identity,
                                            scale=png_s[:, ch:ch + 1],
                                            bias=pnb_s[:, ch:ch + 1])
                tln_bf = s_act.n
                pe_blk = []
                for blk in range(8):
                    bi = blk % 2
                    if blk >= 2:
                        if bi == 0:
                            s_pe.wait(GP, pe_blk[blk - 2])
                        else:
                            s_pe.wait(SY, pe_blk[blk - 2])
                        d1_load(blk)
                    wsem, wval = d1_hist[bi]
                    wsem.wait(TE, wval)
                    if blk == 0:
                        s_act.wait(TE, tln_bf)
                    for ci in range(2):
                        ch = blk * 2 + ci
                        for hb in range(8):
                            s_pe.inc(TE.matmul(
                                thp[:, hb * 4:(hb + 1) * 4],
                                d1wb[bi][:, ci * 1024 + hb * 128:
                                          ci * 1024 + (hb + 1) * 128],
                                tlnT[:, ch * 4:(ch + 1) * 4],
                                start=(ch == 0 and hb == 0),
                                stop=(ch == NT - 1 and hb == 7),
                                skip_group_check=True))
                    pe_blk.append(s_pe.n)
                s_pe.wait(SC)
                for hb in range(8):
                    SCw.activation(t2T[:, hb * 4:(hb + 1) * 4],
                                            thp[:, hb * 4:(hb + 1) * 4], AF.Gelu,
                                            bias=d1b_s[:, hb:hb + 1])
                geld = s_act.n
                s_act.wait(SY)
                s_pe.wait(SY)
                s_row.inc(SY.dma_start(row_a[:, :], row1(d2b)), 16)
                s_row.wait(VE)
                s_pe.wait(VE)
                VEw.tensor_scalar_mul(row_ab[0:1, :], row_a[0:1, :], 0.125)
                d2brow = s_dve.n
                d2w_r = d2w_scr.rearrange("(c p) n -> p c n", p=128)
                pe_og2 = []
                d2h = [None, None]
                for og in range(4):
                    bi = og % 2
                    if bi == 0:
                        s_pe.wait(GP, pe_og2[og - 2] if og >= 2 else pe_blk[6])
                        s_wk2[0].inc(GP.dma_start(
                            d2wb[0][:, :].rearrange("p (c n) -> p c n", c=8),
                            d2w_r[:, :, og * 512:(og + 1) * 512]), 16)
                        d2h[0] = (s_wk2[0], s_wk2[0].n)
                    else:
                        s_pe.wait(SY, pe_og2[og - 2] if og >= 2 else pe_blk[7])
                        s_d2s.inc(SY.dma_start(
                            d2wb[1][:, :].rearrange("p (c n) -> p c n", c=8),
                            d2w_r[:, :, og * 512:(og + 1) * 512]), 16)
                        d2h[1] = (s_d2s, s_d2s.n)
                    wsem, wval = d2h[bi]
                    if og > 0:
                        s_dve.wait(TE)
                    wsem.wait(TE, wval)
                    if og == 0:
                        s_act.wait(TE, geld)
                        s_dve.wait(TE, d2brow)
                    for hb in range(8):
                        s_pe.inc(TE.matmul(
                            thp[0:4, 0:512], t2T[:, hb * 4:(hb + 1) * 4],
                            d2wb[bi][:, hb * 512:(hb + 1) * 512],
                            start=(hb == 0), stop=False,
                            skip_group_check=True))
                    s_pe.inc(TE.matmul(
                        thp[0:4, 0:512], ones_rb[0:1, 0:4],
                        row_ab[0:1, og * 512:(og + 1) * 512],
                        start=False, stop=True, skip_group_check=True))
                    s_pe.wait(VE)
                    VEw.tensor_copy(s4tmp[0:4, og * 512:(og + 1) * 512],
                                             thp[0:4, 0:512])
                    pe_og2.append(s_pe.n)
                th_d1pe[0] = pe_og2[2]
                th_d1pe[1] = pe_og2[3]
                s_dve.wait(GP)
                s_ct.inc(GP.dma_start(cc_tin[:, :], s4tmp[0:4, :]), 16)
                GP.wait_ge(s_ct.h, s_ct.n)
                s_cc.inc(GP.collective_compute(
                    "AllReduce", ALU.add,
                    replica_groups=[[0, 1, 2, 3, 4, 5, 6, 7]],
                    ins=[cc_tin[:, :].opt()], outs=[cc_tout[:, :].opt()]))

            def thought_init_finalize():
                nonlocal cc_done
                cc_done += 1
                GP.wait_ge(s_cc.h, cc_done)
                s_dve.wait(GP)
                s_ct.inc(GP.dma_start(
                    thought[0:4, :].rearrange("b (r n) -> b r n", r=8),
                    cc_ag.rearrange("(r b) n -> b r n", b=4)), 16)
                s_ct.wait(VE)
                s_ct.wait(SC)

            def thought_step_finalize(step):
                nonlocal cc_done
                cc_done += 1
                GP.wait_ge(s_cc.h, cc_done)
                s_dve.wait(GP)
                if DEBUG:
                    GP.wait_ge(s_dbg.h, s_dbg.n)
                s_ct.inc(GP.dma_start(s4tmp[0:4, :], cc_tout[:, :]), 16)
                s_ct.wait(VE)
                if DEBUG:
                    s_ct.wait(GP)
                    s_dbg.inc(GP.dma_start(cout_dbg[step, :, :], s4tmp[0:4, :]), 16)
                    s_dbg.wait(VE)
                VEw.tensor_add(thought[0:4, :], thought[0:4, :],
                                        s4tmp[0:4, :])
                if DEBUG:
                    s_dve.wait(GP)
                    s_dbg.inc(GP.dma_start(th_dbg[1 + step, :, :], thought[0:4, :]), 16)

            s_act.wait(TE)
            s_dve.wait(TE)
            s_sp.wait(SY)
            s_wk2[0].wait(SY)
            s_wk2[1].wait(SY)
            s_pe.wait(SY)
            s_st2[0].wait(GP)
            s_st2[1].wait(GP)
            s_st2[0].wait(SY)
            s_st2[1].wait(SY)
            load_k(0)
            load_q(0, 0)
            load_q(0, 1)
            load_lam(0)
            for kh in range(NKH):
                bi = kh % 2
                if kh + 1 < NKH:
                    load_k(kh + 1)
                s_kb2[bi].wait(TE, kq_hist[kh])
                for g in range(4):
                    nkt = 4 * g + 4
                    pv_done = 0
                    for br in range(2):
                        j = kh * 2 + br
                        ab = br
                        s_qb2[j % 2].wait(TE, q_hist[j])
                        s_dve.wait(TE, at_bank_user[ab])
                        s_act.wait(TE, den_bank_user[ab])
                        pend = {}

                        def emit_score(kt, kh=kh, g=g, j=j, bi=bi, pend=pend):
                            lo = (kt - 4 * g) * 128 if kt >= 4 * g else 0
                            sb = kt % 2
                            s_act.wait(TE, sc_bank_user[sb])
                            s_pe.inc(TE.matmul(
                                scp[sb][:, lo:512],
                                kbuf[bi][:, kt * 128:(kt + 1) * 128],
                                qbuf[j % 2][:, g * 512 + lo:(g + 1) * 512],
                                start=True, stop=True, skip_group_check=True))
                            s_pe.wait(SC)
                            s_pe.wait(SC, pr_bank_user[sb])
                            s_act.inc(SC.activation(
                                prb[sb][:, lo:512], scp[sb][:, lo:512], AF.Exp,
                                bias=negB[:, j:j + 1]))
                            sc_bank_user[sb] = s_act.n
                            expn = s_act.n
                            exp_dve = 0
                            if kt >= 4 * g:
                                s_act.wait(VE, expn)
                                s_dve.inc(VE.tensor_mul(prb[sb][:, lo:lo + 128],
                                                        prb[sb][:, lo:lo + 128],
                                                        mask_s[:, :]))
                                exp_dve = s_dve.n
                            pend[kt] = (lo, sb, expn, exp_dve)

                        def emit_pv(kt, nkt, kh=kh, ab=ab, pend=pend):
                            lo, sb, expn, exp_dve = pend.pop(kt)
                            s_act.wait(TE, expn)
                            if exp_dve:
                                s_dve.wait(TE, exp_dve)
                            s_pe.inc(TE.matmul(
                                atp[ab][:, lo:512],
                                v_full[:, (kh * NT + kt) * 128:(kh * NT + kt + 1) * 128],
                                prb[sb][:, lo:512],
                                start=(kt == 0), stop=(kt == nkt - 1),
                                skip_group_check=True))
                            s_pe.inc(TE.matmul(
                                denp[ab][0:1, lo:512], ones_cb[:, :],
                                prb[sb][:, lo:512],
                                start=(kt == 0), stop=(kt == nkt - 1),
                                skip_group_check=True))
                            pr_bank_user[sb] = s_pe.n

                        def comb0_part():
                            s_act.wait(TE, den_ready[0])
                            s_act.wait(TE)
                            s_pe.inc(TE.matmul(auxp[:, :], ones_rb[0:1, :],
                                               rrowb[0:1, 0:512],
                                               start=True, stop=True))
                            s_pe.wait(SC)
                            s_dve.wait(SC, at_bank_user[0])
                            SCw.copy(s1b[:, :], auxp[:, :])
                            e1 = s_act.n
                            s_act.wait(VE, e1)
                            VEw.tensor_mul(tmp1[:, :], atp[0][:, :], s1b[:, :])
                            at_bank_user[0] = s_dve.n

                        emit_score(0)
                        for kt in range(1, nkt):
                            emit_score(kt)
                            emit_pv(kt - 1, nkt)
                            if br == 0 and kt == 2 and pending_s2[0] is not None:
                                comb1_part()
                            if br == 1 and kt == 3:
                                comb0_part()
                        emit_pv(nkt - 1, nkt)
                        pv_done = s_pe.n
                        if br == 0:
                            s_pe.wait(SC, pv_done)
                            f0, r0 = recip_acc(rrowb[0:1, 0:512],
                                               denp[0][0:1, :],
                                               rrow[0:1, 1536:2048])
                            den_bank_user[0] = f0
                            den_ready[0] = r0
                        if g == 3:
                            pe_br_end[j] = s_pe.n
                            if kh + 1 < NKH:
                                load_q(kh + 1, br)
                    s_pe.wait(SC, pv_done)
                    f1, r1 = recip_acc(rrow[0:1, 512:1024], denp[1][0:1, :],
                                       rrow[0:1, 2048:2560])
                    den_bank_user[1] = f1
                    den_ready[1] = r1
                    s_lam.wait(VE, lam_hist[kh])
                    s_act.wait(VE, r1)
                    s_pe.wait(VE, comb1_pe[0])
                    VEw.tensor_tensor(rrowb[0:1, 512:1024],
                                               lamx[0:1, g * 512:(g + 1) * 512],
                                               rrow[0:1, 512:1024], ALU.mult)
                    pending_s2[0] = (kh, g, s_dve.n)
                    if g == 1 and kh in (1, 3, 5, 6):
                        thought_partA((kh - 1) // 2 if kh < 6 else 3)
                pe_kh_end[kh] = s_pe.n
                if kh + 1 < NKH:
                    load_lam(kh + 1)
                if kh in (1, 3, 5, 6):
                    thought_partB((kh - 1) // 2 if kh < 6 else 3)
                elif kh == 7:
                    thought_step_finalize(3)
            if pending_s2[0] is not None:
                comb1_part()

            s_dve.wait(TE)
            s_act.wait(TE)
            for ch in range(NT):
                s_pe.inc(TE.transpose(thp[:, 64 + ch * 4:68 + ch * 4],
                                      thought[0:4, ch * 128:(ch + 1) * 128],
                                      identf_s[0:4, 0:4]))
            s_pe.wait(SC)
            SCw.copy(tlnT[:, :], thp[:, 64:128])
            thT_bf = s_act.n
            s_act.wait(SY)
            s_pe.wait(SY)
            s_row.inc(SY.dma_start(row_a[0:1, 0:256], row1(bcb)), 16)
            bcb_ld = s_row.n
            s_pe.wait(GP)
            s_wk2[0].inc(GP.dma_start(
                bcwb[:, :].rearrange("p (c n) -> p c n", c=NT),
                bcw.rearrange("(c p) n -> p c n", p=128)), 16)
            bw_ld = s_wk2[0].n
            s_wk2[0].wait(TE, bw_ld)
            s_act.wait(TE, thT_bf)
            s_row.wait(TE, bcb_ld)
            s_dve.wait(TE)
            for ch in range(NT):
                s_pe.inc(TE.matmul(
                    thp[0:4, 0:256], tlnT[:, ch * 4:(ch + 1) * 4],
                    bcwb[:, ch * 256:(ch + 1) * 256], start=(ch == 0),
                    stop=False, skip_group_check=True))
            s_pe.inc(TE.matmul(thp[0:4, 0:256], ones_rf[0:1, 0:4],
                               row_a[0:1, 0:256],
                               start=False, stop=True, skip_group_check=True))
            s_pe.wait(SC)
            SCw.copy(s4tmp[0:4, 0:256], thp[0:4, 0:256])
            pe_bc_mm = s_pe.n
            act_bc = s_act.n
            dve_bc = s_dve.n
            s_pe.wait(GP, pe_bc_mm)
            s_act.wait(GP, act_bc)
            s_dve.wait(GP, dve_bc)
            s_ct.inc(GP.dma_start(cc_tps[:, :], s4tmp[0:4, 0:256]), 16)
            GP.wait_ge(s_ct.h, s_ct.n)
            s_cc.inc(GP.collective_compute(
                "AllGather", ALU.bypass,
                replica_groups=[[0, 1, 2, 3, 4, 5, 6, 7]],
                ins=[cc_tps[:, :].opt()], outs=[cc_ag2[:, :].opt()]))
            cc_done += 1
            GP.wait_ge(s_cc.h, cc_done)
            s_ct.inc(GP.dma_start(
                s4tmp[0:4, :].rearrange("b (r n) -> b r n", r=8),
                cc_ag2.rearrange("(r b) n -> b r n", b=4)), 16)
            bc_ct = s_ct.n

        rs_done = [0, 0, 0, 0]
        with ExitStack() as p56:
            E2 = p56.enter_context
            wob = [E2(nc.sbuf_tensor(f"wob{i}", [128, 8 * 512], BF16, side="right"))
                   for i in range(2)]
            postg = [E2(nc.sbuf_tensor(f"postg{i}", [128, 512], F32, side="right"))
                     for i in range(2)]
            xt6 = [E2(nc.sbuf_tensor(f"xt6{i}", [128, D], F32, side="right"))
                   for i in range(2)]
            gt6s = E2(nc.sbuf_tensor("gt6s", [128, D], F32, side="right"))
            xn6 = [E2(nc.sbuf_tensor(f"xn6{i}", [128, D], F32, side="right"))
                   for i in range(2)]
            goB = E2(nc.sbuf_tensor("goB", [128, D], F32, side="right"))
            boB = E2(nc.sbuf_tensor("boB", [128, D], F32, side="right"))
            v_bn6 = E2(nc.sbuf_tensor("v_bn6", [128, 48], F32, side="right"))
            v_ag = E2(nc.sbuf_tensor("v_ag", [128, 4], F32, side="right"))
            pop = [E2(nc.psum_tensor(f"pop{i}", [128, 512], F32)) for i in range(2)]
            p6p = E2(nc.psum_tensor("p6p", [128, 512], F32))
            fin6 = [xt6[0][:, 0:1024], xt6[1][:, 0:1024],
                    xt6[0][:, 1024:2048], xt6[1][:, 1024:2048]]
            at6 = [xn6[0][:, 0:1024], xn6[1][:, 0:1024],
                   xn6[0][:, 1024:2048], xn6[1][:, 1024:2048]]
            pe_og, st_hist, ow_hist = [], [], []

            wos_r = wout_scr.rearrange("(c p) n -> p c n", p=128)
            s_pe.wait(SY, pe_bc_mm)
            s_act.wait(SY, act_bc)
            s_dve.wait(SY, dve_bc)
            s_res.wait(SY)
            for og01 in range(2):
                s_ow2[og01].inc(SY.dma_start(
                    wob[og01][:, :].rearrange("p (c n) -> p c n", c=8),
                    wos_r[:, :, og01 * 512:(og01 + 1) * 512]), 16)
                ow_hist.append(s_ow2[og01].n)

            s_act.wait(SY)
            s_pe.wait(SY)
            s_dve.wait(SY)
            s_row.inc(SY.dma_start(row_a[:, :], row1(outg)), 16)
            s_row.inc(SY.dma_start(row_b[:, :], row1(outb)), 16)
            s_row.wait(TE)
            s_act.wait(TE)
            s_dve.wait(TE)
            bcast_rows(p6p, row_a, goB, 128)
            bcast_rows(p6p, row_b, boB, 128)

            def ln_stats_vg(tile, eps_ap, r):
                s_act.wait(VE)
                s_gpc.wait(VE)
                for k in range(4):
                    VEw.bn_stats(v_bn6[:, r * 24 + k * 6:r * 24 + (k + 1) * 6],
                                 tile[:, k * 512:(k + 1) * 512])
                VEw.bn_aggr(v_ag[:, 2 * r:2 * r + 2],
                            v_bn6[:, r * 24:(r + 1) * 24])
                s_dve.wait(SC)
                s_gpc.wait(SC)
                SCw.activation(v_rstd[:, r:r + 1], v_ag[:, 2 * r + 1:2 * r + 2],
                               AF.Ln, bias=eps_ap[:, :])
                SCw.activation(v_rstd[:, r:r + 1], v_rstd[:, r:r + 1],
                               AF.Exp, scale=c_nhalf[:, :])
                s_act.wait(VE)
                VEw.tensor_scalar(v_mb[:, r:r + 1], v_ag[:, 2 * r:2 * r + 1],
                                  v_rstd[:, r:r + 1], -1.0, ALU.mult, ALU.mult)

            ffx_hist = []
            ld6 = []
            gread_mark = [0]

            def p6a_tile(qt):
                bi = qt % 2
                if qt >= 2:
                    sem, val = ffx_hist[qt - 2]
                    sem.wait(SY, val)
                sx = s_x2[bi]
                sx.inc(SY.dma_start(xt6[bi][:, :],
                                    x_own[qt * 128:(qt + 1) * 128, :]), 16)
                if qt >= 1:
                    s_dve.wait(SY, gread_mark[0])
                sx.inc(SY.dma_start(gt6s[:, :],
                                    gate_scr[qt * 128:(qt + 1) * 128, :]), 16)
                ld6.append(sx.n)
                sx.wait(VE, ld6[qt])
                ln_stats_vg(xt6[bi], c_eps5, r=bi)
                s_act.wait(VE)
                VEw.tensor_scalar(xn6[bi][:, :], xt6[bi][:, :],
                                  v_rstd[:, bi:bi + 1], v_mb[:, bi:bi + 1],
                                  ALU.mult, ALU.add)
                VEw.tensor_mul(gt6s[:, :], gt6s[:, :], bc_bc[:, :])
                VEw.tensor_add(gt6s[:, :], gt6s[:, :], xn6[bi][:, :])
                ln_stats_vg(gt6s, c_eps6, r=bi)
                s_act.wait(VE)
                VEw.tensor_scalar(xn6[bi][:, :], gt6s[:, :],
                                  v_rstd[:, bi:bi + 1], v_mb[:, bi:bi + 1],
                                  ALU.mult, ALU.add)
                gread_mark[0] = s_dve.n
                VEw.tensor_mul(xn6[bi][:, :], xn6[bi][:, :], goB[:, :])
                VEw.tensor_add(xn6[bi][:, :], xn6[bi][:, :], boB[:, :])
                VEw.tensor_add(xn6[bi][:, :], xn6[bi][:, :], xt6[bi][:, :])
                fin = s_dve.n
                s_dve.wait(SY, fin)
                s_st2[bi].inc(SY.dma_start(
                    ffx_scr[qt * 128:(qt + 1) * 128, :], xn6[bi][:, :]), 16)
                ffx_hist.append((s_st2[bi], s_st2[bi].n))

            dve_pre = s_dve.n
            pop_user = [0, 0]

            def p5_og_block(og):
                bi = og % 2
                h = og // 2
                if og == 0:
                    s_dve.wait(TE, dve_pre)
                for qt in range(NT):
                    pi = qt % 2
                    s_ow2[bi].wait(TE, ow_hist[og])
                    if len(st_hist) >= 2:
                        sem, val = st_hist[-2]
                        sem.wait(TE, val)
                    s_dve.wait(TE, pop_user[pi])
                    for ch in range(NKH):
                        s_pe.inc(TE.matmul(
                            pop[pi][:, :],
                            combT[:, ch * L + qt * 128:ch * L + (qt + 1) * 128],
                            wob[bi][:, ch * 512:(ch + 1) * 512],
                            start=(ch == 0), stop=(ch == NKH - 1)))
                    s_pe.wait(VE)
                    VEw.tensor_copy(postg[pi][:, :], pop[pi][:, :])
                    pop_user[pi] = s_dve.n
                    s_dve.wait(SY, pop_user[pi])
                    s_po8[og * 2 + pi].inc(SY.dma_start(
                        po_scr[og, qt * 128:(qt + 1) * 128, :],
                        postg[pi][:, :]), 16)
                    st_hist.append((s_po8[og * 2 + pi], s_po8[og * 2 + pi].n))
                pe_og.append(s_pe.n)

            def og_prefetch(og):
                bi = og % 2
                s_pe.wait(SY, pe_og[og - 2])
                s_ow2[bi].inc(SY.dma_start(
                    wob[bi][:, :].rearrange("p (c n) -> p c n", c=8),
                    wos_r[:, :, og * 512:(og + 1) * 512]), 16)
                ow_hist.append(s_ow2[bi].n)

            def rs_emit(og):
                s_po8[og * 2].wait(GP)
                s_po8[og * 2 + 1].wait(GP)
                s_cc.inc(GP.collective_compute(
                    "ReduceScatter", ALU.add,
                    replica_groups=[[0, 1], [2, 3], [4, 5], [6, 7]],
                    ins=[po_scr[og, :, :].opt()],
                    outs=[ao_scr[og, :, :].opt()]))
                rs_done[og] = s_cc.n

            p5_og_block(0)
            rs_emit(0)
            s_ct.wait(TE, bc_ct)
            s_pe.wait(SC)
            for og in range(4):
                s_dve.wait(TE)
                s_act.wait(TE)
                s_pe.inc(TE.matmul(p6p[0:1, :], oneh4[0:4, :],
                                   s4tmp[0:4, og * 512:(og + 1) * 512],
                                   start=True, stop=True))
                s_pe.wait(SC)
                SCw.copy(row_b[0:1, og * 512:(og + 1) * 512], p6p[0:1, :])
            bcast_rows(p6p, row_b, bc_bc, 128)
            p6a_tile(0)
            p5_og_block(1)
            rs_emit(1)
            p6a_tile(1)
            p6a_tile(2)
            og_prefetch(2)
            p5_og_block(2)
            rs_emit(2)
            p6a_tile(3)
            p6a_tile(4)
            og_prefetch(3)
            p5_og_block(3)
            rs_emit(3)
            p6a_tile(5)
            p6a_tile(6)
            p6a_tile(7)

            it = 0
            ob_hist = [0, 0, 0, 0]
            for half in range(2):
                GP.wait_ge(s_cc.h, rs_done[half * 2 + 1])
                SY.wait_ge(s_cc.h, rs_done[half * 2 + 1])
                for qt in range(8):
                    b = it % 4
                    Q = GP if b % 2 == 0 else SY
                    if it >= 4:
                        s_ob4[b].wait(Q, ob_hist[b])
                    if it < 4:
                        for sem, val in (ffx_hist[6], ffx_hist[7]):
                            sem.wait(Q, val)
                    sem, val = ffx_hist[qt]
                    sem.wait(Q, val)
                    sx = s_fb4[b]
                    sx.inc(Q.dma_start(
                        fin6[b][:, :],
                        ffx_scr[qt * 128:(qt + 1) * 128,
                                half * 1024:(half + 1) * 1024]), 16)
                    sx.inc(Q.dma_start(
                        at6[b][:, 0:512],
                        ao_scr[half * 2, qt * 128:(qt + 1) * 128, :]), 16)
                    sx.inc(Q.dma_start(
                        at6[b][:, 512:1024],
                        ao_scr[half * 2 + 1, qt * 128:(qt + 1) * 128, :]), 16)
                    sx.wait(VE, sx.n)
                    VEw.tensor_add(fin6[b][:, :], fin6[b][:, :], at6[b][:, :])
                    s_dve.wait(Q)
                    s_ob4[b].inc(Q.dma_start(
                        out[qt * 128:(qt + 1) * 128,
                            half * 1024:(half + 1) * 1024], fin6[b][:, :]), 16)
                    ob_hist[b] = s_ob4[b].n
                    it += 1
            for b in range(4):
                s_ob4[b].wait(GP, ob_hist[b])
            if DEBUG:
                GP.wait_ge(s_dbg.h, s_dbg.n)
        pvX.close()

    return nc


_prog_cache = {}


def _get_program():
    if "nc" not in _prog_cache:
        _prog_cache["nc"] = build_program()
    return _prog_cache["nc"]


def _consts():
    pos = np.arange(L, dtype=np.float64)
    inv = 1.0 / (10000.0 ** (np.arange(0, DH, 2, dtype=np.float64) / DH))
    fr = np.outer(pos, inv)
    emb = np.concatenate([fr, fr], axis=1)
    cos_fm = np.ascontiguousarray(np.cos(emb).T.astype(np.float32))
    sin_fm = np.sin(emb).T.astype(np.float32)
    sinS = np.ascontiguousarray(
        np.concatenate([-sin_fm[:64], sin_fm[64:]], axis=0).astype(np.float32))
    identity = np.eye(128).astype(ml_dtypes.bfloat16)
    identityf = np.eye(128).astype(np.float32)
    kk, qq = np.meshgrid(np.arange(128), np.arange(128), indexing="ij")
    mask = (kk <= qq).astype(ml_dtypes.bfloat16)
    return cos_fm, sinS, identity, identityf, mask


def _core_inputs(ii, c, consts):
    x = ii["L0_attn_x"]
    cos_fm, sinS, identity, identityf, mask = consts
    b, hh, th = c // 2, c % 2, c % 2
    A = np.ascontiguousarray
    return {
        "x_b": A(x[b]),
        "x_own": A(x[b, th * 1024:(th + 1) * 1024]),
        "x_last4": A(x[:, L - 1, :]),
        "wdkv": A(ii["L0_attn_W_DKV"]),
        "wuk": A(ii["L0_attn_W_UK"][:, hh * 1024:(hh + 1) * 1024]),
        "wuv": A(ii["L0_attn_W_UV"][:, hh * 1024:(hh + 1) * 1024]),
        "wdq": A(ii["L0_attn_W_DQ"]),
        "wuq": A(ii["L0_attn_W_UQ"][:, hh * 2048:(hh + 1) * 2048]),
        "wlam": A(ii["L0_attn_W_lam"][:, hh * 8:(hh + 1) * 8]),
        "blam": A(ii["L0_attn_b_lam"][hh * 8:(hh + 1) * 8]),
        "wout": A(ii["L0_attn_W_out"][hh * 1024:(hh + 1) * 1024, :]),
        "gatew": A(ii["L0_ffn_gate_w"]),
        "gateb": A(ii["L0_ffn_gate_b"]),
        "aggw": A(ii["L0_ffn_agg_w"][:, c * 256:(c + 1) * 256]),
        "aggb": A(ii["L0_ffn_agg_b"][c * 256:(c + 1) * 256]),
        "bcw": A(ii["L0_ffn_bc_w"][:, c * 256:(c + 1) * 256]),
        "bcb": A(ii["L0_ffn_bc_b"][c * 256:(c + 1) * 256]),
        "d1w": A(ii["L0_ffn_d1_w"][:, c * 1024:(c + 1) * 1024]),
        "d1b": A(ii["L0_ffn_d1_b"][c * 1024:(c + 1) * 1024]),
        "d2w": A(ii["L0_ffn_d2_w"][c * 1024:(c + 1) * 1024, :]),
        "d2b": A(ii["L0_ffn_d2_b"]),
        "ing": A(ii["L0_ffn_in_gamma"]), "inb": A(ii["L0_ffn_in_beta"]),
        "png": A(ii["L0_ffn_pn_gamma"]), "pnb": A(ii["L0_ffn_pn_beta"]),
        "outg": A(ii["L0_ffn_out_gamma"]), "outb": A(ii["L0_ffn_out_beta"]),
        "cosk": cos_fm, "sinkS": sinS,
        "ident": identity, "identf": identityf, "maskdiag": mask,
        "onehot4": A(np.eye(4, dtype=np.float32)[:, b:b + 1]),
    }


def kernel(**inputs):
    ii = {k: np.asarray(v) for k, v in inputs.items()}
    consts = _consts()
    nc = _get_program()
    in_maps = [_core_inputs(ii, c, consts) for c in range(8)]
    res = run_bass_kernel_spmd(nc, in_maps, core_ids=list(range(8)))
    y = np.empty((B, L, D), dtype=np.float32)
    for c in range(8):
        b, th = c // 2, c % 2
        y[b, th * 1024:(th + 1) * 1024, :] = np.asarray(res.results[c]["out"])
    return y


# version 30
# speedup vs baseline: 1.0322x; 1.0086x over previous
import numpy as np
import ml_dtypes
from contextlib import ExitStack

import concourse.bass as bass
import concourse.mybir as mybir
from concourse.bass_utils import run_bass_kernel_spmd

F32 = mybir.dt.float32
BF16 = mybir.dt.bfloat16
AF = mybir.ActivationFunctionType
ALU = mybir.AluOpType
AX = mybir.AxisListType
F32R = mybir.dt.float32r


def r32(ap):
    return ap.bitcast(F32R)

B, L, D = 4, 2048, 2048
NT = 16
NKH, DH = 8, 128
NQH = 16
QSC = float(1.0 / np.sqrt(128.0))

DEBUG = False


class Sem:
    def __init__(self, stack, nc, name):
        self.h = stack.enter_context(nc.semaphore(name))
        self.n = 0

    def inc(self, instr, k=1):
        instr.then_inc(self.h, k)
        self.n += k
        return instr

    def wait(self, eng, val=None):
        v = self.n if val is None else val
        assert v <= self.n, f"sem wait {v} > emitted {self.n}"
        if v > 0:
            eng.wait_ge(self.h, v)


def build_program():
    nc = bass.Bass("TRN2", target_bir_lowering=False)
    TE, SC, VE, SY, GP = nc.tensor, nc.scalar, nc.vector, nc.sync, nc.gpsimd

    def din(name, shape, dt=F32):
        return nc.dram_tensor(name, shape, dt, kind="ExternalInput").ap()

    def dscr(name, shape, dt=F32):
        kind = "ExternalOutput" if DEBUG else "Internal"
        return nc.dram_tensor(name, shape, dt, kind=kind).ap()

    x_b = din("x_b", [L, D])
    x_own = din("x_own", [L // 2, D])
    x_last4 = din("x_last4", [4, D])
    wdkv = din("wdkv", [D, 512])
    wuk = din("wuk", [512, NKH * DH])
    wuv = din("wuv", [512, NKH * DH])
    wdq = din("wdq", [D, 1024])
    wuq = din("wuq", [1024, NQH * DH])
    wlam = din("wlam", [D, NKH])
    blam = din("blam", [NKH])
    wout = din("wout", [NKH * DH, D])
    gatew = din("gatew", [D, D])
    gateb = din("gateb", [D])
    aggw = din("aggw", [D, 256])
    aggb = din("aggb", [256])
    bcw = din("bcw", [D, 256])
    bcb = din("bcb", [256])
    ing = din("ing", [D]); inb = din("inb", [D])
    png = din("png", [D]); pnb = din("pnb", [D])
    outg = din("outg", [D]); outb = din("outb", [D])
    d1w = din("d1w", [D, 1024])
    d1b = din("d1b", [1024])
    d2w = din("d2w", [1024, D])
    d2b = din("d2b", [D])
    cosk = din("cosk", [128, L])
    sinkS = din("sinkS", [128, L])
    ident = din("ident", [128, 128], BF16)
    identf = din("identf", [128, 128], F32)
    maskdiag = din("maskdiag", [128, 128], BF16)
    onehot4 = din("onehot4", [4, 1])

    out = nc.dram_tensor("out", [L // 2, D], F32, kind="ExternalOutput").ap()

    k_scr = dscr("k_scr", [NKH, 128, L], BF16)
    q_scr = dscr("q_scr", [NQH, 128, L], BF16)
    gate_scr = dscr("gate_scr", [L // 2, D])
    po_scr = dscr("po_scr", [4, L, 512])
    ao_scr = dscr("ao_scr", [4, L // 2, 512])
    cc_tin = dscr("cc_tin", [4, D]) if DEBUG else nc.dram_tensor("cc_tin", [4, D], F32).ap()
    cc_tout = dscr("cc_tout", [4, D]) if DEBUG else nc.dram_tensor("cc_tout", [4, D], F32).ap()
    cc_ag = nc.dram_tensor("cc_ag", [32, 256], F32).ap()
    cc_ag2 = nc.dram_tensor("cc_ag2", [32, 256], F32).ap()
    cc_tps = nc.dram_tensor("cc_tps", [4, 256], F32).ap()
    ckv_scr = nc.dram_tensor("ckv_scr", [128, 4 * L], BF16).ap()
    ffx_scr = nc.dram_tensor("ffx_scr", [L // 2, D], F32).ap()
    d1w_scr = nc.dram_tensor("d1w_scr", [D, 1024], BF16).ap()
    d2w_scr = nc.dram_tensor("d2w_scr", [1024, D], BF16).ap()
    wout_scr = nc.dram_tensor("wout_scr", [1024, D], BF16).ap()
    if DEBUG:
        th_dbg = nc.dram_tensor("th_dbg", [6, 4, D], F32, kind="ExternalOutput").ap()
        tln_dbg = nc.dram_tensor("tln_dbg", [4, D], F32, kind="ExternalOutput").ap()
        t2_dbg = nc.dram_tensor("t2_dbg", [128, 32], F32, kind="ExternalOutput").ap()
        cin_dbg = nc.dram_tensor("cin_dbg", [4, 4, D], F32, kind="ExternalOutput").ap()
        cout_dbg = nc.dram_tensor("cout_dbg", [4, 4, D], F32, kind="ExternalOutput").ap()
        tln_dbg2 = nc.dram_tensor("tln_dbg2", [4, 4, D], F32, kind="ExternalOutput").ap()
        s_dbg = Sem if False else None

    st = ExitStack()
    with st:
        E = st.enter_context
        s_pe = Sem(st, nc, "s_pe")
        s_act = Sem(st, nc, "s_act")
        s_dve = Sem(st, nc, "s_dve")
        s_pl = Sem(st, nc, "s_pl")
        s_cc = Sem(st, nc, "s_cc")
        s_lc = Sem(st, nc, "s_lc")
        s_row = Sem(st, nc, "s_row")
        s_x2 = [Sem(st, nc, f"s_x{i}") for i in range(2)]
        s_st2 = [Sem(st, nc, f"s_s{i}") for i in range(2)]
        s_gl = Sem(st, nc, "s_gl")
        s_wk2 = [Sem(st, nc, f"s_w{i}") for i in range(2)]
        s_rot = Sem(st, nc, "s_rot")
        s_rt4 = [Sem(st, nc, f"s_rt{i}") for i in range(4)]
        s_kb2 = [Sem(st, nc, f"s_k{i}") for i in range(2)]
        s_qb2 = [Sem(st, nc, f"s_q{i}") for i in range(2)]
        s_lam = Sem(st, nc, "s_lam")
        s_sp = Sem(st, nc, "s_sp")
        s_ct = Sem(st, nc, "s_ct")
        s_dbg = Sem(st, nc, "s_dbg")
        s_out = Sem(st, nc, "s_out")
        s_res = Sem(st, nc, "s_res")
        s_d1s = Sem(st, nc, "s_d1s")
        s_d2s = Sem(st, nc, "s_d2s")
        s_rg = Sem(st, nc, "s_rg")
        s_gpc = Sem(st, nc, "s_gpc")
        s_po8 = [Sem(st, nc, f"s_po{i}") for i in range(8)]
        s_ow2 = [Sem(st, nc, f"s_ow{i}") for i in range(2)]
        s_owg = [Sem(st, nc, f"s_og{i}") for i in range(2)]
        s_fb4 = [Sem(st, nc, f"s_fb{i}") for i in range(4)]
        s_ob4 = [Sem(st, nc, f"s_ob{i}") for i in range(4)]

        _block = E(nc.Block())

        class _EngWrap:
            def __init__(self, eng, sem):
                self._eng = eng
                self._sem = sem

            def __getattr__(self, name):
                f = getattr(self._eng, name)
                sem, eng = self._sem, self._eng

                def g(*a, **k):
                    sem.wait(eng)
                    return sem.inc(f(*a, **k))
                return g

        VEw = _EngWrap(VE, s_dve)
        SCw = _EngWrap(SC, s_act)
        GPw = _EngWrap(GP, s_gpc)

        def recip_acc(out, in_, scratch):
            SCw.activation(scratch, in_, AF.Ln)
            freed = s_act.n
            SCw.activation(out, scratch, AF.Exp, scale=c_neg1[0:1, :])
            return freed, s_act.n

        ident_s = E(nc.sbuf_tensor("ident_s", [128, 128], BF16, side="left"))
        identf_s = E(nc.sbuf_tensor("identf_s", [128, 128], F32, side="left"))
        mask_s = E(nc.sbuf_tensor("mask_s", [128, 128], BF16, side="left"))
        ones_rf = E(nc.sbuf_tensor("ones_rf", [1, 128], F32, side="left"))
        ones_rb = E(nc.sbuf_tensor("ones_rb", [1, 128], BF16, side="left"))
        row_gb = E(nc.sbuf_tensor("row_gb", [1, D], BF16, side="left"))
        row_ab = row_gb
        ones_cb = E(nc.sbuf_tensor("ones_cb", [128, 1], BF16, side="left"))
        oneh4 = E(nc.sbuf_tensor("oneh4", [4, 1], F32, side="left"))
        c_eps5 = E(nc.sbuf_tensor("c_eps5", [128, 1], F32, side="left"))
        c_eps6 = E(nc.sbuf_tensor("c_eps6", [128, 1], F32, side="left"))
        c_half = E(nc.sbuf_tensor("c_half", [128, 1], F32, side="left"))
        c_nhalf = E(nc.sbuf_tensor("c_nhalf", [128, 1], F32, side="left"))
        c_n102 = E(nc.sbuf_tensor("c_n102", [128, 1], F32, side="left"))
        c_neg1 = E(nc.sbuf_tensor("c_neg1", [128, 1], F32, side="left"))
        c_qsc = E(nc.sbuf_tensor("c_qsc", [128, 1], F32, side="left"))
        lamT = E(nc.sbuf_tensor("lamT", [NKH, L], F32, side="left"))
        negB = E(nc.sbuf_tensor("negB", [128, NQH], F32, side="left"))
        mk2 = E(nc.sbuf_tensor("mk2", [1, NKH], F32, side="left"))
        mq2 = E(nc.sbuf_tensor("mq2", [1, NQH], F32, side="left"))
        m4t = E(nc.sbuf_tensor("m4t", [1, 4], F32, side="left"))
        v_bnG = E(nc.sbuf_tensor("v_bnG", [128, 48], F32, side="left"))
        v_agG = E(nc.sbuf_tensor("v_agG", [128, 4], F32, side="left"))
        sc11 = E(nc.sbuf_tensor("sc11", [1, 1], F32, side="left"))
        thought = E(nc.sbuf_tensor("thought", [4, D], F32, side="left"))
        png_s = E(nc.sbuf_tensor("png_s", [128, NT], F32, side="left"))
        pnb_s = E(nc.sbuf_tensor("pnb_s", [128, NT], F32, side="left"))
        d1b_s = E(nc.sbuf_tensor("d1b_s", [128, 8], F32, side="left"))
        xnl_fm = E(nc.sbuf_tensor("xnl_fm", [128, NT * 4], BF16, side="left"))
        row_a = E(nc.sbuf_tensor("row_a", [1, D], F32, side="left"))
        row_b = E(nc.sbuf_tensor("row_b", [1, D], F32, side="left"))
        s4tmp = E(nc.sbuf_tensor("s4tmp", [4, D], F32, side="left"))
        tln_f = E(nc.sbuf_tensor("tln_f", [4, D], F32, side="left"))
        bc_bc = E(nc.sbuf_tensor("bc_bc", [128, D], F32, side="left"))
        v_sum = E(nc.sbuf_tensor("v_sum", [128, 4], F32, side="left"))
        v_ssq = E(nc.sbuf_tensor("v_ssq", [128, 4], F32, side="left"))
        v_var = E(nc.sbuf_tensor("v_var", [128, 4], F32, side="left"))
        v_rstd = E(nc.sbuf_tensor("v_rstd", [128, 4], F32, side="left"))
        v_mb = E(nc.sbuf_tensor("v_mb", [128, 4], F32, side="left"))

        for t, val in ((ones_rf, 1.0), (ones_rb, 1.0),
                       (ones_cb, 1.0), (c_eps5, 1e-5),
                       (c_eps6, 1e-6), (c_half, 0.5), (c_nhalf, -0.5),
                       (c_n102, -1.02), (c_neg1, -1.0), (c_qsc, QSC)):
            s_pl.inc(GP.memset(t[:, :], val))
        s_lc.inc(SY.dma_start(ident_s[:, :], ident[:, :]), 16)
        s_lc.inc(SY.dma_start(identf_s[:, :], identf[:, :]), 16)
        s_lc.inc(SY.dma_start(mask_s[:, :], maskdiag[:, :]), 16)
        s_lc.inc(SY.dma_start(oneh4[:, :], onehot4[:, :]), 16)
        s_lc.inc(SY.dma_start(d1b_s[:, :], d1b.rearrange("(j p) -> p j", p=128),
                              allow_slow_non_contiguous=True), 16)
        s_lc.inc(SY.dma_start(png_s[:, :], png.rearrange("(j p) -> p j", p=128),
                              allow_slow_non_contiguous=True), 16)
        s_lc.inc(SY.dma_start(pnb_s[:, :], pnb.rearrange("(j p) -> p j", p=128),
                              allow_slow_non_contiguous=True), 16)
        lc_consts = s_lc.n

        def row1(ap1d):
            return ap1d.rearrange("(a d) -> a d", a=1)

        def ln_stats(tile, sq, n, eps_ap, r=0):
            vs = v_sum[0:n, r:r + 1]
            vq = v_ssq[0:n, r:r + 1]
            vv = v_var[0:n, r:r + 1]
            vr = v_rstd[0:n, r:r + 1]
            vm = v_mb[0:n, r:r + 1]
            s_act.wait(VE)
            VEw.reduce_sum(vs, tile[0:n, 0:D], axis=AX.X)
            s_dve.wait(SC)
            SCw.activation(sq[0:n, 0:D], tile[0:n, 0:D], AF.Square, accum_out=vq)
            s_act.wait(VE)
            VEw.tensor_scalar_mul(vs, vs, 1.0 / D)
            VEw.tensor_scalar_mul(vv, vq, 1.0 / D)
            VEw.tensor_scalar(vq, vs, vs, None, ALU.mult)
            VEw.tensor_sub(vv, vv, vq)
            s_dve.wait(SC)
            SCw.activation(vr, vv, AF.Ln, bias=eps_ap[0:n, :])
            SCw.activation(vr, vr, AF.Exp, scale=c_nhalf[0:n, :])
            s_act.wait(VE)
            VEw.tensor_scalar(vm, vs, vr, -1.0, ALU.mult, ALU.mult)

        def ln_stats_bn(tile, eps_ap, r):
            s_act.wait(VE)
            for k in range(4):
                VEw.bn_stats(v_bnG[:, r * 24 + k * 6:r * 24 + (k + 1) * 6],
                             tile[:, k * 512:(k + 1) * 512])
            VEw.bn_aggr(v_agG[:, 2 * r:2 * r + 2], v_bnG[:, r * 24:(r + 1) * 24])
            s_dve.wait(SC)
            SCw.activation(v_rstd[:, r:r + 1], v_agG[:, 2 * r + 1:2 * r + 2],
                           AF.Ln, bias=eps_ap[:, :])
            SCw.activation(v_rstd[:, r:r + 1], v_rstd[:, r:r + 1],
                           AF.Exp, scale=c_nhalf[:, :])
            s_act.wait(VE)
            VEw.tensor_scalar(v_mb[:, r:r + 1], v_agG[:, 2 * r:2 * r + 1],
                              v_rstd[:, r:r + 1], -1.0, ALU.mult, ALU.mult)

        def bcast_rows(ps, row_src, dst, n):
            for og in range(4):
                s_pe.wait(TE)
                s_act.wait(TE)
                s_dve.wait(TE)
                s_pe.inc(TE.matmul(ps[0:n, 0:512], ones_rf[0:1, 0:n],
                                   row_src[0:1, og * 512:(og + 1) * 512],
                                   start=True, stop=True))
                s_pe.wait(SC)
                SCw.copy(dst[0:n, og * 512:(og + 1) * 512], ps[0:n, 0:512])

        with ExitStack() as p0:
            E0 = p0.enter_context
            x4 = E0(nc.sbuf_tensor("x4", [4, D], F32, side="right"))
            xn4 = E0(nc.sbuf_tensor("xn4", [4, D], F32, side="right"))
            g4 = E0(nc.sbuf_tensor("g4", [4, D], F32, side="right"))
            b4 = E0(nc.sbuf_tensor("b4", [4, D], F32, side="right"))
            xnl_bf = E0(nc.sbuf_tensor("xnl_bf", [4, D], BF16, side="right"))
            ps4 = E0(nc.psum_tensor("ps4", [4, 512], F32))
            pstr0 = E0(nc.psum_tensor("pstr0", [128, NT * 4], BF16))

            s_x2[0].inc(SY.dma_start(x4[:, :], x_last4[:, :]), 16)
            s_row.inc(SY.dma_start(row_a[:, :], row1(ing)), 16)
            s_row.inc(SY.dma_start(row_b[:, :], row1(inb)), 16)
            for eng in (VE, SC, TE):
                s_lc.wait(eng)
                s_row.wait(eng)
                s_x2[0].wait(eng)
                s_pl.wait(eng)
            bcast_rows(ps4, row_a, g4, 4)
            bcast_rows(ps4, row_b, b4, 4)
            ln_stats(x4, xn4, 4, c_eps5)
            s_dve.wait(SC)
            SCw.activation(xn4[0:4, :], x4[0:4, :], AF.Identity,
                                    scale=v_rstd[0:4, 0:1], bias=v_mb[0:4, 0:1])
            ln_stats(xn4, tln_f, 4, c_eps6)
            s_dve.wait(SC)
            SCw.activation(tln_f[0:4, :], xn4[0:4, :], AF.Identity,
                                    scale=v_rstd[0:4, 0:1], bias=v_mb[0:4, 0:1])
            s_act.wait(VE)
            VEw.tensor_mul(tln_f[0:4, :], tln_f[0:4, :], g4[0:4, :])
            VEw.tensor_add(tln_f[0:4, :], tln_f[0:4, :], b4[0:4, :])
            VEw.tensor_copy(xnl_bf[0:4, :], tln_f[0:4, :])
            s_dve.wait(TE)
            for ch in range(NT):
                s_pe.inc(TE.transpose(pstr0[:, ch * 4:(ch + 1) * 4],
                                      xnl_bf[0:4, ch * 128:(ch + 1) * 128],
                                      ident_s[0:4, 0:4]))
            s_pe.wait(SC)
            SCw.copy(xnl_fm[:, :], pstr0[:, :])

        with ExitStack() as p13:
            E1 = p13.enter_context
            xn_fm = E1(nc.sbuf_tensor("xn_fm", [128, NT * L], BF16, side="right"))
            cosk_s = E1(nc.sbuf_tensor("cosk_s", [128, L], F32, side="right"))
            sinkS_s = E1(nc.sbuf_tensor("sinkS_s", [128, L], F32, side="right"))
            blam_s = E1(nc.sbuf_tensor("blam_s", [NKH, 1], F32, side="right"))
            s_act.wait(SY)
            s_lc.inc(SY.dma_start(cosk_s[:, :], cosk[:, :]), 16)
            s_lc.inc(SY.dma_start(sinkS_s[:, :], sinkS[:, :]), 16)
            s_lc.inc(SY.dma_start(blam_s[:, :],
                                  blam.rearrange("(p a) -> p a", a=1)), 16)
            lc_all = s_lc.n

            with ExitStack() as p1:
                E2 = p1.enter_context
                xt = [E2(nc.sbuf_tensor(f"xt{i}", [128, D], F32, side="right"))
                      for i in range(2)]
                xnbf = [E2(nc.sbuf_tensor(f"xnbf{i}", [128, D], BF16, side="right"))
                        for i in range(2)]
                trp = [E2(nc.psum_tensor(f"trp{i}", [128, 512], BF16))
                       for i in range(2)]
                ld_hist, act_xn, pe_tile, evac_hist = [], [], [], []
                s_x2[0].inc(SY.dma_start(xt[0][:, :], x_b[0:128, :]), 16)
                ld_hist.append(s_x2[0].n)
                for tt in range(NT):
                    bi = tt % 2
                    if tt + 1 < NT:
                        if tt >= 1:
                            s_act.wait(SY, act_xn[tt - 1])
                        s_x2[(tt + 1) % 2].inc(
                            SY.dma_start(xt[(tt + 1) % 2][:, :],
                                         x_b[(tt + 1) * 128:(tt + 2) * 128, :]), 16)
                        ld_hist.append(s_x2[(tt + 1) % 2].n)
                    s_x2[bi].wait(VE, ld_hist[tt])
                    s_x2[bi].wait(SC, ld_hist[tt])
                    ln_stats_bn(xt[bi], c_eps5, r=bi)
                    s_dve.wait(SC)
                    if tt >= 2:
                        s_pe.wait(SC, pe_tile[tt - 2])
                    for grp in range(4):
                        SCw.activation(xnbf[bi][:, grp * 512:(grp + 1) * 512],
                                       xt[bi][:, grp * 512:(grp + 1) * 512],
                                       AF.Identity,
                                       scale=v_rstd[:, bi:bi + 1],
                                       bias=v_mb[:, bi:bi + 1])
                        s_act.wait(TE)
                        if len(evac_hist) >= 2:
                            s_dve.wait(TE, evac_hist[-2])
                        for j in range(4):
                            ch = grp * 4 + j
                            s_pe.inc(TE.transpose(
                                trp[grp % 2][:, j * 128:(j + 1) * 128],
                                xnbf[bi][:, ch * 128:(ch + 1) * 128], ident_s[:, :]))
                        s_pe.wait(VE)
                        dst = xn_fm[:, :].rearrange("p (c t) -> p c t", c=NT)[
                            :, grp * 4:(grp + 1) * 4, tt * 128:(tt + 1) * 128]
                        src = trp[grp % 2][:, :].rearrange("p (c t) -> p c t", c=4)
                        VEw.tensor_copy(dst, src)
                        evac_hist.append(s_dve.n)
                    act_xn.append(s_act.n)
                    pe_tile.append(s_pe.n)
            xn_ready_dve = s_dve.n
            xn_ready = s_act.n
            pvX = ExitStack()

            with ExitStack() as p3:
                E2 = p3.enter_context
                cq_fm = E2(nc.sbuf_tensor("cq_fm", [128, 8 * L], BF16, side="right"))
                pre = E2(nc.sbuf_tensor("pre3", [128, L], F32, side="right"))
                rot = E2(nc.sbuf_tensor("rot3", [128, L], F32, side="right"))
                rq_bf = E2(nc.sbuf_tensor("rq_bf", [128, L], BF16, side="right"))
                rsq = E2(nc.sbuf_tensor("rsq3", [128, L], BF16, side="right"))
                wlam_bf = E2(nc.sbuf_tensor("wlam_bf", [128, NT * NKH], BF16,
                                            side="right"))
                wqb = [E2(nc.sbuf_tensor(f"wqb{i}", [128, NT * 128], BF16,
                                         side="right")) for i in range(2)]
                mmp = [E2(nc.psum_tensor(f"mm3{i}", [128, 512], F32)) for i in range(2)]
                nrmp = E2(nc.psum_tensor("nrmp3", [1, 512], F32))

                s_lam.inc(GP.dma_start(
                    wlam_bf[:, :].rearrange("p (c n) -> p c n", c=NT),
                    wlam.rearrange("(c p) n -> p c n", p=128)), 16)
                lam_w = s_lam.n
                wdq_r = wdq.rearrange("(c p) n -> p c n", p=128)

                mm_user = [0, 0]
                wm_hist, pe_m = [], []
                for m in range(8):
                    bi = m % 2
                    if m >= 2:
                        s_pe.wait(GP, pe_m[m - 2])
                    s_wk2[bi].inc(GP.dma_start(
                        wqb[bi][:, :].rearrange("p (c n) -> p c n", c=NT),
                        wdq_r[:, :, m * 128:(m + 1) * 128]), 16)
                    wm_hist.append(s_wk2[bi].n)
                    for tg in range(4):
                        pi = (m * 4 + tg) % 2
                        if m == 0 and tg == 0:
                            s_act.wait(TE, xn_ready)
                            s_dve.wait(TE, xn_ready_dve)
                            s_dve.wait(TE)
                        s_wk2[bi].wait(TE, wm_hist[m])
                        s_act.wait(TE, mm_user[pi])
                        for ch in range(NT):
                            s_pe.inc(TE.matmul(
                                mmp[pi][:, :],
                                wqb[bi][:, ch * 128:(ch + 1) * 128],
                                xn_fm[:, ch * L + tg * 512:ch * L + (tg + 1) * 512],
                                start=(ch == 0), stop=(ch == NT - 1)))
                        s_pe.wait(SC)
                        SCw.copy(
                            cq_fm[:, m * L + tg * 512:m * L + (tg + 1) * 512],
                            mmp[pi][:, :])
                        mm_user[pi] = s_act.n
                    pe_m.append(s_pe.n)

                wuq_r = wuq.rearrange("(c p) n -> p c n", p=128)
                uq_hist, pe_qh = [], []
                q_spill = 0
                nrm_pe = 0
                pre_free = [0] * 4
                rot_free = [0] * 4
                for qh in range(NQH):
                    bi = qh % 2
                    if qh >= 2:
                        s_pe.wait(GP, pe_qh[qh - 2])
                    else:
                        s_pe.wait(GP, pe_m[qh - 2])
                    s_wk2[bi].inc(GP.dma_start(
                        wqb[bi][:, :].rearrange("p (c n) -> p c n", c=NT)[:, 0:8, :],
                        wuq_r[:, :, qh * 128:(qh + 1) * 128]), 16)
                    uq_hist.append(s_wk2[bi].n)
                    if qh < 8:
                        s_res.inc(GP.dma_start(
                            d1w_scr[qh * 256:(qh + 1) * 256, :],
                            d1w[qh * 256:(qh + 1) * 256, :]), 16)
                        s_res.inc(GP.dma_start(
                            d2w_scr[qh * 128:(qh + 1) * 128, :],
                            d2w[qh * 128:(qh + 1) * 128, :]), 16)
                    elif qh < 12:
                        s_res.inc(GP.dma_start(
                            wout_scr[(qh - 8) * 256:(qh - 7) * 256, :],
                            wout[(qh - 8) * 256:(qh - 7) * 256, :]), 16)
                    for tg in range(4):
                        pi = tg % 2
                        lo, hi = tg * 512, (tg + 1) * 512
                        s_wk2[bi].wait(TE, uq_hist[qh])
                        s_act.wait(TE, mm_user[pi])
                        for c in range(8):
                            s_pe.inc(TE.matmul(
                                mmp[pi][:, :],
                                wqb[bi][:, c * 128:(c + 1) * 128],
                                cq_fm[:, c * L + lo:c * L + hi],
                                start=(c == 0), stop=(c == 7)))
                        s_pe.wait(SC)
                        s_dve.wait(SC, pre_free[tg])
                        SCw.activation(pre[:, lo:hi], mmp[pi][:, :],
                                       AF.Identity, scale=c_qsc[:, :])
                        mm_user[pi] = s_act.n
                        evn = s_act.n
                        s_act.wait(SY, evn)
                        s_dve.wait(SY, rot_free[tg])
                        s_rt4[tg].inc(SY.dma_start(rot[0:64, lo:hi],
                                                   pre[64:128, lo:hi]), 16)
                        s_rt4[tg].inc(SY.dma_start(rot[64:128, lo:hi],
                                                   pre[0:64, lo:hi]), 16)
                        rotw = s_rt4[tg].n
                        s_act.wait(VE, evn)
                        if qh == 0 and tg == 0:
                            s_lc.wait(VE, lc_all)
                        s_rt4[tg].wait(VE, rotw)
                        VEw.tensor_mul(pre[:, lo:hi], pre[:, lo:hi],
                                       cosk_s[:, lo:hi])
                        pre_free[tg] = s_dve.n
                        VEw.tensor_mul(rot[:, lo:hi], rot[:, lo:hi],
                                       sinkS_s[:, lo:hi])
                        rot_free[tg] = s_dve.n
                        if tg == 0:
                            s_sp.wait(VE, q_spill)
                        VEw.tensor_tensor(rq_bf[:, lo:hi], pre[:, lo:hi],
                                          rot[:, lo:hi], ALU.add)
                        if tg == 0:
                            s_pe.wait(VE, nrm_pe)
                        VEw.tensor_mul(rsq[:, lo:hi], rq_bf[:, lo:hi],
                                       rq_bf[:, lo:hi])
                    pe_qh.append(s_pe.n)
                    qdone = s_dve.n
                    s_dve.wait(GP, qdone)
                    s_sp.inc(GP.dma_start(q_scr[qh, :, :], rq_bf[:, :]), 16)
                    q_spill = s_sp.n
                    s_dve.wait(TE, qdone)
                    for tg in range(4):
                        s_pe.inc(TE.matmul(nrmp[0:1, :], ones_cb[:, :],
                                           rsq[:, tg * 512:(tg + 1) * 512],
                                           start=(tg == 0), stop=(tg == 3),
                                           skip_group_check=True))
                    nrm_pe = s_pe.n
                    s_pe.wait(VE, nrm_pe)
                    VEw.reduce_max(mq2[0:1, qh:qh + 1], nrmp[0:1, :],
                                   axis=AX.X)

                for tg in range(4):
                    s_act.wait(TE)
                    s_dve.wait(TE)
                    s_lam.wait(TE, lam_w)
                    for ch in range(NT):
                        s_pe.inc(TE.matmul(
                            mmp[0][0:NKH, :],
                            wlam_bf[:, ch * NKH:(ch + 1) * NKH],
                            xn_fm[:, ch * L + tg * 512:ch * L + (tg + 1) * 512],
                            start=(ch == 0), stop=(ch == NT - 1)))
                    s_pe.wait(SC)
                    s_lc.wait(SC, lc_all)
                    SCw.activation(lamT[0:NKH, tg * 512:(tg + 1) * 512],
                                            mmp[0][0:NKH, :], AF.Sigmoid,
                                            bias=blam_s[0:NKH, :])

            with ExitStack() as p2:
                E2 = p2.enter_context
                ckv_fm = E2(nc.sbuf_tensor("ckv_fm", [128, 4 * L], BF16, side="right"))
                wuk_bf = E2(nc.sbuf_tensor("wuk_bf", [128, 4 * 1024], BF16, side="right"))
                wkvb = [E2(nc.sbuf_tensor(f"wkvb{i}", [128, NT * 128], BF16,
                                          side="right")) for i in range(2)]
                pre = E2(nc.sbuf_tensor("pre", [128, L], F32, side="right"))
                rot = E2(nc.sbuf_tensor("rot", [128, L], F32, side="right"))
                rk_bf = E2(nc.sbuf_tensor("rk_bf", [128, L], BF16, side="right"))
                rsq = E2(nc.sbuf_tensor("rsq", [128, L], BF16, side="right"))
                mmp = [E2(nc.psum_tensor(f"mmp{i}", [128, 512], F32)) for i in range(2)]
                nrmp = E2(nc.psum_tensor("nrmp", [1, 512], F32))

                s_gl.inc(GP.dma_start(
                    wuk_bf[:, :].rearrange("p (c n) -> p c n", c=4),
                    wuk.rearrange("(c p) n -> p c n", p=128)), 16)
                wready = s_gl.n
                wdkv_r = wdkv.rearrange("(c p) n -> p c n", p=128)

                s_act.wait(TE)
                s_dve.wait(TE)
                evh = []
                wm_hist, pe_m = [], []
                for m in range(4):
                    bi = m % 2
                    if m >= 2:
                        s_pe.wait(GP, pe_m[m - 2])
                    else:
                        s_pe.wait(GP)
                    s_wk2[bi].inc(GP.dma_start(
                        wkvb[bi][:, :].rearrange("p (c n) -> p c n", c=NT),
                        wdkv_r[:, :, m * 128:(m + 1) * 128]), 16)
                    wm_hist.append(s_wk2[bi].n)
                    for tg in range(4):
                        pi = (m * 4 + tg) % 2
                        s_wk2[bi].wait(TE, wm_hist[m])
                        if len(evh) >= 2:
                            s_act.wait(TE, evh[-2])
                        for ch in range(NT):
                            s_pe.inc(TE.matmul(
                                mmp[pi][:, :],
                                wkvb[bi][:, ch * 128:(ch + 1) * 128],
                                xn_fm[:, ch * L + tg * 512:ch * L + (tg + 1) * 512],
                                start=(ch == 0), stop=(ch == NT - 1)))
                        s_pe.wait(SC)
                        SCw.copy(
                            ckv_fm[:, m * L + tg * 512:m * L + (tg + 1) * 512],
                            mmp[pi][:, :])
                        evh.append(s_act.n)
                    pe_m.append(s_pe.n)

                k_spill = s_sp.n
                nrm_pe = 0
                pre_free = [0] * 4
                rot_free = [0] * 4
                for kh in range(NKH):
                    for tg in range(4):
                        lo, hi = tg * 512, (tg + 1) * 512
                        if kh == 0 and tg == 0:
                            s_gl.wait(TE, wready)
                        if len(evh) >= 2:
                            s_act.wait(TE, evh[-2])
                        for c in range(4):
                            s_pe.inc(TE.matmul(
                                mmp[tg % 2][:, :],
                                wuk_bf[:, c * 1024 + kh * 128:c * 1024 + (kh + 1) * 128],
                                ckv_fm[:, c * L + lo:c * L + hi],
                                start=(c == 0), stop=(c == 3)))
                        s_pe.wait(SC)
                        s_dve.wait(SC, pre_free[tg])
                        SCw.copy(pre[:, lo:hi], mmp[tg % 2][:, :])
                        evh.append(s_act.n)
                        evn = s_act.n
                        s_act.wait(SY, evn)
                        s_dve.wait(SY, rot_free[tg])
                        s_rt4[tg].inc(SY.dma_start(rot[0:64, lo:hi],
                                                   pre[64:128, lo:hi]), 16)
                        s_rt4[tg].inc(SY.dma_start(rot[64:128, lo:hi],
                                                   pre[0:64, lo:hi]), 16)
                        rotw = s_rt4[tg].n
                        s_act.wait(VE, evn)
                        s_rt4[tg].wait(VE, rotw)
                        VEw.tensor_mul(pre[:, lo:hi], pre[:, lo:hi],
                                       cosk_s[:, lo:hi])
                        pre_free[tg] = s_dve.n
                        VEw.tensor_mul(rot[:, lo:hi], rot[:, lo:hi],
                                       sinkS_s[:, lo:hi])
                        rot_free[tg] = s_dve.n
                        if tg == 0:
                            s_sp.wait(VE, k_spill)
                        VEw.tensor_tensor(rk_bf[:, lo:hi], pre[:, lo:hi],
                                          rot[:, lo:hi], ALU.add)
                        if tg == 0:
                            s_pe.wait(VE, nrm_pe)
                        VEw.tensor_mul(rsq[:, lo:hi], rk_bf[:, lo:hi],
                                       rk_bf[:, lo:hi])
                    kdone = s_dve.n
                    s_dve.wait(GP, kdone)
                    s_sp.inc(GP.dma_start(k_scr[kh, :, :], rk_bf[:, :]), 16)
                    k_spill = s_sp.n
                    s_dve.wait(TE, kdone)
                    for tg in range(4):
                        s_pe.inc(TE.matmul(nrmp[0:1, :], ones_cb[:, :],
                                           rsq[:, tg * 512:(tg + 1) * 512],
                                           start=(tg == 0), stop=(tg == 3),
                                           skip_group_check=True))
                    nrm_pe = s_pe.n
                    s_pe.wait(VE, nrm_pe)
                    VEw.reduce_max(mk2[0:1, kh:kh + 1], nrmp[0:1, :],
                                   axis=AX.X)

                s_act.wait(GP)
                s_sp.inc(GP.dma_start(ckv_scr[:, :], ckv_fm[:, :]), 16)
                ckv_spilled = s_sp.n

                for qh in range(NQH):
                    kh = qh // 2
                    s_act.wait(VE)
                    s_pe.wait(VE)
                    VEw.tensor_tensor(sc11[0:1, :], mq2[0:1, qh:qh + 1],
                                               mk2[0:1, kh:kh + 1], ALU.mult)
                    s_dve.wait(SC)
                    SCw.activation(sc11[0:1, :], sc11[0:1, :], AF.Ln)
                    SCw.activation(sc11[0:1, :], sc11[0:1, :], AF.Exp,
                                            scale=c_half[0:1, :])
                    s_act.wait(TE)
                    s_pe.inc(TE.matmul(mmp[1][:, 0:1], ones_rf[0:1, :],
                                       sc11[0:1, 0:1], start=True, stop=True))
                    s_pe.wait(SC)
                    SCw.activation(negB[:, qh:qh + 1], mmp[1][:, 0:1],
                                            AF.Identity, scale=c_n102[:, :])

        v_full = pvX.enter_context(
            nc.sbuf_tensor("v_full", [128, NKH * NT * 128], BF16, side="left"))
        with ExitStack() as p2b:
            E2 = p2b.enter_context
            ckv2 = E2(nc.sbuf_tensor("ckv2", [128, 4 * L], BF16, side="right"))
            wuv_bf = E2(nc.sbuf_tensor("wuv_bf", [128, 4 * 1024], BF16, side="right"))
            mmp = [E2(nc.psum_tensor(f"mmv{i}", [128, 512], F32)) for i in range(2)]
            s_sp.wait(GP, ckv_spilled)
            s_gl.inc(GP.dma_start(ckv2[:, :], ckv_scr[:, :]), 16)
            s_gl.inc(GP.dma_start(
                wuv_bf[:, :].rearrange("p (c n) -> p c n", c=4),
                wuv.rearrange("(c p) n -> p c n", p=128)), 16)
            vready = s_gl.n
            s_act.wait(TE)
            s_dve.wait(TE)
            evh = []
            for kt in range(NT):
                for og in range(2):
                    pi = (kt * 2 + og) % 2
                    if kt == 0 and og == 0:
                        s_gl.wait(TE, vready)
                    if len(evh) >= 2:
                        s_act.wait(TE, evh[-2])
                    for c in range(4):
                        s_pe.inc(TE.matmul(
                            mmp[pi][:, :],
                            ckv2[:, c * L + kt * 128:c * L + (kt + 1) * 128],
                            wuv_bf[:, c * 1024 + og * 512:c * 1024 + (og + 1) * 512],
                            start=(c == 0), stop=(c == 3)))
                    s_pe.wait(SC)
                    dst = v_full[:, :].rearrange(
                        "p (h t d) -> p h t d", h=NKH, t=NT)[
                        :, og * 4:(og + 1) * 4, kt, :]
                    src = mmp[pi][:, :].rearrange("p (h d) -> p h d", h=4)
                    SCw.copy(dst, src)
                    evh.append(s_act.n)

        with ExitStack() as p35:
            E2 = p35.enter_context
            xno_fm = E2(nc.sbuf_tensor("xno_fm", [128, NT * 1024], BF16, side="right"))
            with ExitStack() as p35a:
                E3 = p35a.enter_context
                xt = [E3(nc.sbuf_tensor(f"xot{i}", [128, D], F32, side="right"))
                      for i in range(2)]
                xnbf = [E3(nc.sbuf_tensor(f"xobf{i}", [128, D], BF16, side="right"))
                        for i in range(2)]
                trp = [E3(nc.psum_tensor(f"otrp{i}", [128, 512], BF16))
                       for i in range(2)]
                ld_hist, act_xn, pe_tile, evac_hist = [], [], [], []
                s_dve.wait(SY)
                s_x2[0].inc(SY.dma_start(xt[0][:, :], x_own[0:128, :]), 16)
                ld_hist.append(s_x2[0].n)
                for tt in range(8):
                    bi = tt % 2
                    if tt + 1 < 8:
                        if tt >= 1:
                            s_act.wait(SY, act_xn[tt - 1])
                        s_x2[(tt + 1) % 2].inc(
                            SY.dma_start(xt[(tt + 1) % 2][:, :],
                                         x_own[(tt + 1) * 128:(tt + 2) * 128, :]), 16)
                        ld_hist.append(s_x2[(tt + 1) % 2].n)
                    s_x2[bi].wait(VE, ld_hist[tt])
                    s_x2[bi].wait(SC, ld_hist[tt])
                    ln_stats_bn(xt[bi], c_eps5, r=bi)
                    s_dve.wait(SC)
                    if tt >= 2:
                        s_pe.wait(SC, pe_tile[tt - 2])
                    for grp in range(4):
                        SCw.activation(xnbf[bi][:, grp * 512:(grp + 1) * 512],
                                       xt[bi][:, grp * 512:(grp + 1) * 512],
                                       AF.Identity,
                                       scale=v_rstd[:, bi:bi + 1],
                                       bias=v_mb[:, bi:bi + 1])
                        s_act.wait(TE)
                        if len(evac_hist) >= 2:
                            s_dve.wait(TE, evac_hist[-2])
                        for j in range(4):
                            ch = grp * 4 + j
                            s_pe.inc(TE.transpose(
                                trp[grp % 2][:, j * 128:(j + 1) * 128],
                                xnbf[bi][:, ch * 128:(ch + 1) * 128], ident_s[:, :]))
                        s_pe.wait(VE)
                        dst = xno_fm[:, :].rearrange("p (c t) -> p c t", c=NT)[
                            :, grp * 4:(grp + 1) * 4, tt * 128:(tt + 1) * 128]
                        src = trp[grp % 2][:, :].rearrange("p (c t) -> p c t", c=4)
                        VEw.tensor_copy(dst, src)
                        evac_hist.append(s_dve.n)
                    act_xn.append(s_act.n)
                    pe_tile.append(s_pe.n)

            with ExitStack() as p35b:
                E3 = p35b.enter_context
                gwb = E3(nc.sbuf_tensor("gwb", [128, NT * 512], BF16, side="right"))
                gstage = [E3(nc.sbuf_tensor(f"gst{i}", [128, 512], F32, side="right"))
                          for i in range(2)]
                gp_ps = [E3(nc.psum_tensor(f"gp{i}", [128, 512], F32))
                         for i in range(2)]
                gatew_r = gatew.rearrange("(c p) n -> p c n", p=128)
                s_act.wait(SY)
                s_pe.wait(SY)
                s_act.wait(TE)
                s_dve.wait(TE)
                s_rg.inc(GP.dma_start(row_gb[:, :], row1(gateb)), 16)
                gb_ld = s_rg.n
                st_hist = []
                gp_user = [0, 0]
                for og in range(4):
                    s_pe.wait(GP)
                    s_wk2[0].inc(GP.dma_start(
                        gwb[:, :].rearrange("p (c n) -> p c n", c=NT),
                        gatew_r[:, :, og * 512:(og + 1) * 512]), 16)
                    gw_now = s_wk2[0].n
                    for qt in range(8):
                        pi = qt % 2
                        s_wk2[0].wait(TE, gw_now)
                        if og == 0 and qt == 0:
                            s_rg.wait(TE, gb_ld)
                        if len(st_hist) >= 2:
                            sem, val = st_hist[-2]
                            sem.wait(TE, val)
                        s_act.wait(TE, gp_user[pi])
                        for ch in range(NT):
                            s_pe.inc(TE.matmul(
                                gp_ps[pi][:, :],
                                xno_fm[:, ch * 1024 + qt * 128:ch * 1024 + (qt + 1) * 128],
                                gwb[:, ch * 512:(ch + 1) * 512],
                                start=(ch == 0), stop=False,
                                skip_group_check=True))
                        s_pe.inc(TE.matmul(
                            gp_ps[pi][:, :], ones_rb[0:1, 0:128],
                            row_gb[0:1, og * 512:(og + 1) * 512],
                            start=False, stop=True, skip_group_check=True))
                        s_pe.wait(SC)
                        SCw.activation(gstage[pi][:, :], gp_ps[pi][:, :],
                                                AF.Sigmoid)
                        gp_user[pi] = s_act.n
                        s_act.wait(SY)
                        s_st2[pi].inc(SY.dma_start(
                            gate_scr[qt * 128:(qt + 1) * 128,
                                     og * 512:(og + 1) * 512], gstage[pi][:, :]), 16)
                        st_hist.append((s_st2[pi], s_st2[pi].n))

        combT = pvX.enter_context(
            nc.sbuf_tensor("combT", [128, NKH * L], BF16, side="left"))

        with ExitStack() as p4:
            E2 = p4.enter_context

            th_d1pe = [0, 0]
            with ExitStack() as p4a:
                E3 = p4a.enter_context
                awb = E3(nc.sbuf_tensor("awb", [128, NT * 256], BF16, side="right"))
                th4p = E3(nc.psum_tensor("th4p", [4, 512], F32))
                s_act.wait(SY)
                s_pe.wait(SY)
                s_row.inc(SY.dma_start(row_a[0:1, 0:256], row1(aggb)), 16)
                agb_ld = s_row.n
                s_pe.wait(GP)
                s_act.wait(GP)
                s_st2[0].wait(GP)
                s_st2[1].wait(GP)
                s_wk2[0].inc(GP.dma_start(
                    awb[:, :].rearrange("p (c n) -> p c n", c=NT),
                    aggw.rearrange("(c p) n -> p c n", p=128)), 16)
                aw_ld = s_wk2[0].n
                s_wk2[0].wait(TE, aw_ld)
                s_row.wait(TE, agb_ld)
                s_act.wait(TE)
                s_dve.wait(TE)
                for ch in range(NT):
                    s_pe.inc(TE.matmul(
                        th4p[0:4, 0:256],
                        xnl_fm[:, ch * 4:(ch + 1) * 4],
                        awb[:, ch * 256:(ch + 1) * 256],
                        start=(ch == 0), stop=False, skip_group_check=True))
                s_pe.inc(TE.matmul(
                    th4p[0:4, 0:256], ones_rf[0:1, 0:4], row_a[0:1, 0:256],
                    start=False, stop=True, skip_group_check=True))
                th_d1pe[0] = th_d1pe[1] = s_pe.n
                s_pe.wait(SC)
                SCw.copy(s4tmp[0:4, 0:256], th4p[0:4, 0:256])
                s_act.wait(GP)
                s_ct.inc(GP.dma_start(cc_tps[:, :], s4tmp[0:4, 0:256]), 16)
                GP.wait_ge(s_ct.h, s_ct.n)
                s_cc.inc(GP.collective_compute(
                    "AllGather", ALU.bypass,
                    replica_groups=[[0, 1, 2, 3, 4, 5, 6, 7]],
                    ins=[cc_tps[:, :].opt()], outs=[cc_ag[:, :].opt()]))

            kbuf = [E2(nc.sbuf_tensor(f"kbuf{i}", [128, L], BF16, side="right"))
                    for i in range(2)]
            qbuf = [E2(nc.sbuf_tensor(f"qbuf{i}", [128, L], BF16, side="right"))
                    for i in range(2)]
            prb = [E2(nc.sbuf_tensor(f"prb{i}", [128, 512], BF16, side="right"))
                   for i in range(2)]
            s1b = E2(nc.sbuf_tensor("s1b", [128, 512], F32, side="right"))
            s2b = E2(nc.sbuf_tensor("s2b", [128, 512], F32, side="right"))
            tmp1 = E2(nc.sbuf_tensor("tmp1", [128, 512], F32, side="right"))
            tmp2 = E2(nc.sbuf_tensor("tmp2", [128, 512], F32, side="right"))
            lamx = E2(nc.sbuf_tensor("lamx", [1, L], F32, side="right"))
            rrow = E2(nc.sbuf_tensor("rrow", [1, 2560], F32, side="right"))
            rrowb = E2(nc.sbuf_tensor("rrowb", [1, 1024], BF16, side="right"))
            t2T = E2(nc.sbuf_tensor("t2T", [128, 32], BF16, side="right"))
            tlnT = E2(nc.sbuf_tensor("tlnT", [128, 64], BF16, side="right"))
            bcwb = E2(nc.sbuf_tensor("bcwb", [128, NT * 256], BF16, side="right"))
            d1wb = [E2(nc.sbuf_tensor(f"d1wb{i}", [128, 2048], BF16, side="right"))
                    for i in range(2)]
            d2wb = [E2(nc.sbuf_tensor(f"d2wb{i}", [128, 4096], BF16, side="right"))
                    for i in range(2)]
            scp = [E2(nc.psum_tensor(f"scp{i}", [128, 512], F32)) for i in range(2)]
            atp = [E2(nc.psum_tensor(f"atp{i}", [128, 512], F32)) for i in range(2)]
            denp = [E2(nc.psum_tensor(f"denp{i}", [1, 512], F32)) for i in range(2)]
            auxp = E2(nc.psum_tensor("auxp", [128, 512], F32))
            thp = E2(nc.psum_tensor("thp", [128, 512], F32))

            kq_hist = {}
            q_hist = {}
            pending_s2 = [None]
            comb1_pe = [0]

            def comb1_part():
                kh_, g_, s2d = pending_s2[0]
                pending_s2[0] = None
                s_dve.wait(TE, s2d)
                s_act.wait(TE)
                s_pe.inc(TE.matmul(auxp[:, :], ones_rb[0:1, :],
                                   rrowb[0:1, 512:1024], start=True, stop=True))
                comb1_pe[0] = s_pe.n
                s_pe.wait(SC)
                s_dve.wait(SC, at_bank_user[1])
                SCw.copy(s2b[:, :], auxp[:, :])
                e2 = s_act.n
                s_act.wait(VE, e2)
                VEw.tensor_mul(tmp2[:, :], atp[1][:, :], s2b[:, :])
                at_bank_user[1] = s_dve.n
                VEw.tensor_tensor(
                    combT[:, kh_ * L + g_ * 512:kh_ * L + (g_ + 1) * 512],
                    tmp1[:, :], tmp2[:, :], ALU.subtract)
            pe_kh_end = [0] * NKH
            pe_br_end = {}
            lam_hist = [0] * (NKH + 1)
            sc_bank_user = [0, 0]
            pr_bank_user = [0, 0]
            at_bank_user = [0, 0]
            den_bank_user = [0, 0]
            den_ready = [0, 0]
            cc_done = 0

            def load_k(kh):
                if kh >= 2:
                    s_pe.wait(SY, pe_kh_end[kh - 2])
                s_kb2[kh % 2].inc(SY.dma_start(kbuf[kh % 2][:, :],
                                               k_scr[kh, :, :]), 16)
                kq_hist[kh] = s_kb2[kh % 2].n

            def load_q(kh, br):
                j = kh * 2 + br
                if j >= 2:
                    s_pe.wait(SY, pe_br_end[j - 2])
                s_qb2[j % 2].inc(SY.dma_start(qbuf[j % 2][:, :],
                                              q_scr[j, :, :]), 16)
                q_hist[j] = s_qb2[j % 2].n

            def load_lam(kh):
                s_dve.wait(GP)
                s_act.wait(GP)
                s_lam.inc(GP.dma_start(lamx[0:1, :], lamT[kh:kh + 1, :]), 16)
                lam_hist[kh] = s_lam.n

            tln_mark = [0]
            d1_hist = [None, None]

            def d1_load(blk):
                sl = d1w_scr[blk * 256:(blk + 1) * 256, :].rearrange(
                    "(c p) n -> p c n", p=128)
                bi = blk % 2
                if bi == 0:
                    s_wk2[0].inc(GP.dma_start(
                        d1wb[0][:, :].rearrange("p (c n) -> p c n", c=2), sl), 16)
                    d1_hist[0] = (s_wk2[0], s_wk2[0].n)
                else:
                    s_d1s.inc(SY.dma_start(
                        d1wb[1][:, :].rearrange("p (c n) -> p c n", c=2), sl), 16)
                    d1_hist[1] = (s_d1s, s_d1s.n)

            def thought_partA(step):
                if step == 0:
                    thought_init_finalize()
                else:
                    thought_step_finalize(step - 1)
                s_act.wait(VE)
                ln_stats(thought, tln_f, 4, c_eps6, r=2)
                s_dve.wait(SC)
                SCw.activation(tln_f[0:4, :], thought[0:4, :], AF.Identity,
                                        scale=v_rstd[0:4, 2:3], bias=v_mb[0:4, 2:3])
                tln_mark[0] = s_act.n
                if step == 0:
                    s_res.wait(GP)
                    s_res.wait(SY)
                s_pe.wait(GP, th_d1pe[0])
                d1_load(0)
                s_pe.wait(SY, th_d1pe[1])
                d1_load(1)

            def thought_partB(step):
                s_act.wait(TE, tln_mark[0])
                s_dve.wait(TE)
                for ch in range(NT):
                    s_pe.inc(TE.transpose(thp[:, 64 + ch * 4:68 + ch * 4],
                                          tln_f[0:4, ch * 128:(ch + 1) * 128],
                                          identf_s[0:4, 0:4]))
                s_pe.wait(SC)
                for ch in range(NT):
                    SCw.activation(tlnT[:, ch * 4:(ch + 1) * 4],
                                            thp[:, 64 + ch * 4:68 + ch * 4],
                                            AF.Identity,
                                            scale=png_s[:, ch:ch + 1],
                                            bias=pnb_s[:, ch:ch + 1])
                tln_bf = s_act.n
                pe_blk = []
                for blk in range(8):
                    bi = blk % 2
                    if blk >= 2:
                        if bi == 0:
                            s_pe.wait(GP, pe_blk[blk - 2])
                        else:
                            s_pe.wait(SY, pe_blk[blk - 2])
                        d1_load(blk)
                    wsem, wval = d1_hist[bi]
                    wsem.wait(TE, wval)
                    if blk == 0:
                        s_act.wait(TE, tln_bf)
                    for ci in range(2):
                        ch = blk * 2 + ci
                        for hb in range(8):
                            s_pe.inc(TE.matmul(
                                thp[:, hb * 4:(hb + 1) * 4],
                                d1wb[bi][:, ci * 1024 + hb * 128:
                                          ci * 1024 + (hb + 1) * 128],
                                tlnT[:, ch * 4:(ch + 1) * 4],
                                start=(ch == 0 and hb == 0),
                                stop=(ch == NT - 1 and hb == 7),
                                skip_group_check=True))
                    pe_blk.append(s_pe.n)
                s_pe.wait(SC)
                for hb in range(8):
                    SCw.activation(t2T[:, hb * 4:(hb + 1) * 4],
                                            thp[:, hb * 4:(hb + 1) * 4], AF.Gelu,
                                            bias=d1b_s[:, hb:hb + 1])
                geld = s_act.n
                s_act.wait(SY)
                s_pe.wait(SY)
                s_row.inc(SY.dma_start(row_a[:, :], row1(d2b)), 16)
                s_row.wait(VE)
                s_pe.wait(VE)
                VEw.tensor_scalar_mul(row_ab[0:1, :], row_a[0:1, :], 0.125)
                d2brow = s_dve.n
                d2w_r = d2w_scr.rearrange("(c p) n -> p c n", p=128)
                pe_og2 = []
                d2h = [None, None]
                for og in range(4):
                    bi = og % 2
                    if bi == 0:
                        s_pe.wait(GP, pe_og2[og - 2] if og >= 2 else pe_blk[6])
                        s_wk2[0].inc(GP.dma_start(
                            d2wb[0][:, :].rearrange("p (c n) -> p c n", c=8),
                            d2w_r[:, :, og * 512:(og + 1) * 512]), 16)
                        d2h[0] = (s_wk2[0], s_wk2[0].n)
                    else:
                        s_pe.wait(SY, pe_og2[og - 2] if og >= 2 else pe_blk[7])
                        s_d2s.inc(SY.dma_start(
                            d2wb[1][:, :].rearrange("p (c n) -> p c n", c=8),
                            d2w_r[:, :, og * 512:(og + 1) * 512]), 16)
                        d2h[1] = (s_d2s, s_d2s.n)
                    wsem, wval = d2h[bi]
                    if og > 0:
                        s_dve.wait(TE)
                    wsem.wait(TE, wval)
                    if og == 0:
                        s_act.wait(TE, geld)
                        s_dve.wait(TE, d2brow)
                    for hb in range(8):
                        s_pe.inc(TE.matmul(
                            thp[0:4, 0:512], t2T[:, hb * 4:(hb + 1) * 4],
                            d2wb[bi][:, hb * 512:(hb + 1) * 512],
                            start=(hb == 0), stop=False,
                            skip_group_check=True))
                    s_pe.inc(TE.matmul(
                        thp[0:4, 0:512], ones_rb[0:1, 0:4],
                        row_ab[0:1, og * 512:(og + 1) * 512],
                        start=False, stop=True, skip_group_check=True))
                    s_pe.wait(VE)
                    VEw.tensor_copy(s4tmp[0:4, og * 512:(og + 1) * 512],
                                             thp[0:4, 0:512])
                    pe_og2.append(s_pe.n)
                th_d1pe[0] = pe_og2[2]
                th_d1pe[1] = pe_og2[3]
                s_dve.wait(GP)
                s_ct.inc(GP.dma_start(cc_tin[:, :], s4tmp[0:4, :]), 16)
                GP.wait_ge(s_ct.h, s_ct.n)
                s_cc.inc(GP.collective_compute(
                    "AllReduce", ALU.add,
                    replica_groups=[[0, 1, 2, 3, 4, 5, 6, 7]],
                    ins=[cc_tin[:, :].opt()], outs=[cc_tout[:, :].opt()]))

            def thought_init_finalize():
                nonlocal cc_done
                cc_done += 1
                GP.wait_ge(s_cc.h, cc_done)
                s_dve.wait(GP)
                s_ct.inc(GP.dma_start(
                    thought[0:4, :].rearrange("b (r n) -> b r n", r=8),
                    cc_ag.rearrange("(r b) n -> b r n", b=4)), 16)
                s_ct.wait(VE)
                s_ct.wait(SC)

            def thought_step_finalize(step):
                nonlocal cc_done
                cc_done += 1
                GP.wait_ge(s_cc.h, cc_done)
                s_dve.wait(GP)
                if DEBUG:
                    GP.wait_ge(s_dbg.h, s_dbg.n)
                s_ct.inc(GP.dma_start(s4tmp[0:4, :], cc_tout[:, :]), 16)
                s_ct.wait(VE)
                if DEBUG:
                    s_ct.wait(GP)
                    s_dbg.inc(GP.dma_start(cout_dbg[step, :, :], s4tmp[0:4, :]), 16)
                    s_dbg.wait(VE)
                VEw.tensor_add(thought[0:4, :], thought[0:4, :],
                                        s4tmp[0:4, :])
                if DEBUG:
                    s_dve.wait(GP)
                    s_dbg.inc(GP.dma_start(th_dbg[1 + step, :, :], thought[0:4, :]), 16)

            s_act.wait(TE)
            s_dve.wait(TE)
            s_sp.wait(SY)
            s_wk2[0].wait(SY)
            s_wk2[1].wait(SY)
            s_pe.wait(SY)
            s_st2[0].wait(GP)
            s_st2[1].wait(GP)
            s_st2[0].wait(SY)
            s_st2[1].wait(SY)
            load_k(0)
            load_q(0, 0)
            load_q(0, 1)
            load_lam(0)
            for kh in range(NKH):
                bi = kh % 2
                if kh + 1 < NKH:
                    load_k(kh + 1)
                s_kb2[bi].wait(TE, kq_hist[kh])
                for g in range(4):
                    nkt = 4 * g + 4
                    pv_done = 0
                    for br in range(2):
                        j = kh * 2 + br
                        ab = br
                        s_qb2[j % 2].wait(TE, q_hist[j])
                        s_dve.wait(TE, at_bank_user[ab])
                        s_act.wait(TE, den_bank_user[ab])
                        pend = {}

                        def emit_score(kt, kh=kh, g=g, j=j, bi=bi, pend=pend):
                            lo = (kt - 4 * g) * 128 if kt >= 4 * g else 0
                            sb = kt % 2
                            s_act.wait(TE, sc_bank_user[sb])
                            s_pe.inc(TE.matmul(
                                scp[sb][:, lo:512],
                                kbuf[bi][:, kt * 128:(kt + 1) * 128],
                                qbuf[j % 2][:, g * 512 + lo:(g + 1) * 512],
                                start=True, stop=True, skip_group_check=True))
                            s_pe.wait(SC)
                            s_pe.wait(SC, pr_bank_user[sb])
                            s_act.inc(SC.activation(
                                prb[sb][:, lo:512], scp[sb][:, lo:512], AF.Exp,
                                bias=negB[:, j:j + 1]))
                            sc_bank_user[sb] = s_act.n
                            expn = s_act.n
                            exp_dve = 0
                            if kt >= 4 * g:
                                s_act.wait(VE, expn)
                                s_dve.inc(VE.tensor_mul(prb[sb][:, lo:lo + 128],
                                                        prb[sb][:, lo:lo + 128],
                                                        mask_s[:, :]))
                                exp_dve = s_dve.n
                            pend[kt] = (lo, sb, expn, exp_dve)

                        def emit_pv(kt, nkt, kh=kh, ab=ab, pend=pend):
                            lo, sb, expn, exp_dve = pend.pop(kt)
                            s_act.wait(TE, expn)
                            if exp_dve:
                                s_dve.wait(TE, exp_dve)
                            s_pe.inc(TE.matmul(
                                atp[ab][:, lo:512],
                                v_full[:, (kh * NT + kt) * 128:(kh * NT + kt + 1) * 128],
                                prb[sb][:, lo:512],
                                start=(kt == 0), stop=(kt == nkt - 1),
                                skip_group_check=True))
                            s_pe.inc(TE.matmul(
                                denp[ab][0:1, lo:512], ones_cb[:, :],
                                prb[sb][:, lo:512],
                                start=(kt == 0), stop=(kt == nkt - 1),
                                skip_group_check=True))
                            pr_bank_user[sb] = s_pe.n

                        def comb0_part():
                            s_act.wait(TE, den_ready[0])
                            s_act.wait(TE)
                            s_pe.inc(TE.matmul(auxp[:, :], ones_rb[0:1, :],
                                               rrowb[0:1, 0:512],
                                               start=True, stop=True))
                            s_pe.wait(SC)
                            s_dve.wait(SC, at_bank_user[0])
                            SCw.copy(s1b[:, :], auxp[:, :])
                            e1 = s_act.n
                            s_act.wait(VE, e1)
                            VEw.tensor_mul(tmp1[:, :], atp[0][:, :], s1b[:, :])
                            at_bank_user[0] = s_dve.n

                        emit_score(0)
                        for kt in range(1, nkt):
                            emit_score(kt)
                            emit_pv(kt - 1, nkt)
                            if br == 0 and kt == 2 and pending_s2[0] is not None:
                                comb1_part()
                            if br == 1 and kt == 3:
                                comb0_part()
                        emit_pv(nkt - 1, nkt)
                        pv_done = s_pe.n
                        if br == 0:
                            s_pe.wait(SC, pv_done)
                            f0, r0 = recip_acc(rrowb[0:1, 0:512],
                                               denp[0][0:1, :],
                                               rrow[0:1, 1536:2048])
                            den_bank_user[0] = f0
                            den_ready[0] = r0
                        if g == 3:
                            pe_br_end[j] = s_pe.n
                            if kh + 1 < NKH:
                                load_q(kh + 1, br)
                    s_pe.wait(SC, pv_done)
                    f1, r1 = recip_acc(rrow[0:1, 512:1024], denp[1][0:1, :],
                                       rrow[0:1, 2048:2560])
                    den_bank_user[1] = f1
                    den_ready[1] = r1
                    s_lam.wait(VE, lam_hist[kh])
                    s_act.wait(VE, r1)
                    s_pe.wait(VE, comb1_pe[0])
                    VEw.tensor_tensor(rrowb[0:1, 512:1024],
                                               lamx[0:1, g * 512:(g + 1) * 512],
                                               rrow[0:1, 512:1024], ALU.mult)
                    pending_s2[0] = (kh, g, s_dve.n)
                    if g == 1 and kh in (1, 3, 5, 6):
                        thought_partA((kh - 1) // 2 if kh < 6 else 3)
                pe_kh_end[kh] = s_pe.n
                if kh + 1 < NKH:
                    load_lam(kh + 1)
                if kh in (1, 3, 5, 6):
                    thought_partB((kh - 1) // 2 if kh < 6 else 3)
                elif kh == 7:
                    thought_step_finalize(3)
            if pending_s2[0] is not None:
                comb1_part()

            s_dve.wait(TE)
            s_act.wait(TE)
            for ch in range(NT):
                s_pe.inc(TE.transpose(thp[:, 64 + ch * 4:68 + ch * 4],
                                      thought[0:4, ch * 128:(ch + 1) * 128],
                                      identf_s[0:4, 0:4]))
            s_pe.wait(SC)
            SCw.copy(tlnT[:, :], thp[:, 64:128])
            thT_bf = s_act.n
            s_act.wait(SY)
            s_pe.wait(SY)
            s_row.inc(SY.dma_start(row_a[0:1, 0:256], row1(bcb)), 16)
            bcb_ld = s_row.n
            s_pe.wait(GP)
            s_wk2[0].inc(GP.dma_start(
                bcwb[:, :].rearrange("p (c n) -> p c n", c=NT),
                bcw.rearrange("(c p) n -> p c n", p=128)), 16)
            bw_ld = s_wk2[0].n
            s_wk2[0].wait(TE, bw_ld)
            s_act.wait(TE, thT_bf)
            s_row.wait(TE, bcb_ld)
            s_dve.wait(TE)
            for ch in range(NT):
                s_pe.inc(TE.matmul(
                    thp[0:4, 0:256], tlnT[:, ch * 4:(ch + 1) * 4],
                    bcwb[:, ch * 256:(ch + 1) * 256], start=(ch == 0),
                    stop=False, skip_group_check=True))
            s_pe.inc(TE.matmul(thp[0:4, 0:256], ones_rf[0:1, 0:4],
                               row_a[0:1, 0:256],
                               start=False, stop=True, skip_group_check=True))
            s_pe.wait(SC)
            SCw.copy(s4tmp[0:4, 0:256], thp[0:4, 0:256])
            pe_bc_mm = s_pe.n
            act_bc = s_act.n
            dve_bc = s_dve.n
            s_pe.wait(GP, pe_bc_mm)
            s_act.wait(GP, act_bc)
            s_dve.wait(GP, dve_bc)
            s_ct.inc(GP.dma_start(cc_tps[:, :], s4tmp[0:4, 0:256]), 16)
            GP.wait_ge(s_ct.h, s_ct.n)
            s_cc.inc(GP.collective_compute(
                "AllGather", ALU.bypass,
                replica_groups=[[0, 1, 2, 3, 4, 5, 6, 7]],
                ins=[cc_tps[:, :].opt()], outs=[cc_ag2[:, :].opt()]))
            cc_done += 1
            GP.wait_ge(s_cc.h, cc_done)
            s_ct.inc(GP.dma_start(
                s4tmp[0:4, :].rearrange("b (r n) -> b r n", r=8),
                cc_ag2.rearrange("(r b) n -> b r n", b=4)), 16)
            bc_ct = s_ct.n

        rs_done = [0, 0, 0, 0]
        with ExitStack() as p56:
            E2 = p56.enter_context
            wob = [E2(nc.sbuf_tensor(f"wob{i}", [128, 8 * 512], BF16, side="right"))
                   for i in range(2)]
            postg = [E2(nc.sbuf_tensor(f"postg{i}", [128, 512], F32, side="right"))
                     for i in range(2)]
            xt6 = [E2(nc.sbuf_tensor(f"xt6{i}", [128, D], F32, side="right"))
                   for i in range(2)]
            gt6s = E2(nc.sbuf_tensor("gt6s", [128, D], F32, side="right"))
            xn6 = [E2(nc.sbuf_tensor(f"xn6{i}", [128, D], F32, side="right"))
                   for i in range(2)]
            goB = E2(nc.sbuf_tensor("goB", [128, D], F32, side="right"))
            boB = E2(nc.sbuf_tensor("boB", [128, D], F32, side="right"))
            v_bn6 = E2(nc.sbuf_tensor("v_bn6", [128, 48], F32, side="right"))
            v_ag = E2(nc.sbuf_tensor("v_ag", [128, 4], F32, side="right"))
            pop = [E2(nc.psum_tensor(f"pop{i}", [128, 512], F32)) for i in range(2)]
            p6p = E2(nc.psum_tensor("p6p", [128, 512], F32))
            fin6 = [xt6[0][:, 0:1024], xt6[1][:, 0:1024],
                    xt6[0][:, 1024:2048], xt6[1][:, 1024:2048]]
            at6 = [xn6[0][:, 0:1024], xn6[1][:, 0:1024],
                   xn6[0][:, 1024:2048], xn6[1][:, 1024:2048]]
            pe_og, st_hist, ow_hist = [], [], []

            wos_r = wout_scr.rearrange("(c p) n -> p c n", p=128)
            s_pe.wait(SY, pe_bc_mm)
            s_act.wait(SY, act_bc)
            s_dve.wait(SY, dve_bc)
            s_res.wait(SY)
            for og01 in range(2):
                s_ow2[og01].inc(SY.dma_start(
                    wob[og01][:, :].rearrange("p (c n) -> p c n", c=8),
                    wos_r[:, :, og01 * 512:(og01 + 1) * 512]), 16)
                ow_hist.append((s_ow2[og01], s_ow2[og01].n))

            s_act.wait(SY)
            s_pe.wait(SY)
            s_dve.wait(SY)
            s_row.inc(SY.dma_start(row_a[:, :], row1(outg)), 16)
            s_row.inc(SY.dma_start(row_b[:, :], row1(outb)), 16)
            s_row.wait(TE)
            s_act.wait(TE)
            s_dve.wait(TE)
            bcast_rows(p6p, row_a, goB, 128)
            bcast_rows(p6p, row_b, boB, 128)

            def ln_stats_vg(tile, eps_ap, r):
                s_act.wait(VE)
                s_gpc.wait(VE)
                for k in range(4):
                    VEw.bn_stats(v_bn6[:, r * 24 + k * 6:r * 24 + (k + 1) * 6],
                                 tile[:, k * 512:(k + 1) * 512])
                VEw.bn_aggr(v_ag[:, 2 * r:2 * r + 2],
                            v_bn6[:, r * 24:(r + 1) * 24])
                s_dve.wait(SC)
                s_gpc.wait(SC)
                SCw.activation(v_rstd[:, r:r + 1], v_ag[:, 2 * r + 1:2 * r + 2],
                               AF.Ln, bias=eps_ap[:, :])
                SCw.activation(v_rstd[:, r:r + 1], v_rstd[:, r:r + 1],
                               AF.Exp, scale=c_nhalf[:, :])
                s_act.wait(VE)
                VEw.tensor_scalar(v_mb[:, r:r + 1], v_ag[:, 2 * r:2 * r + 1],
                                  v_rstd[:, r:r + 1], -1.0, ALU.mult, ALU.mult)

            ffx_hist = []
            ld6 = []
            gread_mark = [0]

            def p6a_tile(qt):
                bi = qt % 2
                if qt >= 2:
                    sem, val = ffx_hist[qt - 2]
                    sem.wait(SY, val)
                sx = s_x2[bi]
                sx.inc(SY.dma_start(xt6[bi][:, :],
                                    x_own[qt * 128:(qt + 1) * 128, :]), 16)
                if qt >= 1:
                    s_dve.wait(SY, gread_mark[0])
                sx.inc(SY.dma_start(gt6s[:, :],
                                    gate_scr[qt * 128:(qt + 1) * 128, :]), 16)
                ld6.append(sx.n)
                sx.wait(VE, ld6[qt])
                ln_stats_vg(xt6[bi], c_eps5, r=bi)
                s_act.wait(VE)
                VEw.tensor_scalar(xn6[bi][:, :], xt6[bi][:, :],
                                  v_rstd[:, bi:bi + 1], v_mb[:, bi:bi + 1],
                                  ALU.mult, ALU.add)
                VEw.tensor_mul(gt6s[:, :], gt6s[:, :], bc_bc[:, :])
                VEw.tensor_add(gt6s[:, :], gt6s[:, :], xn6[bi][:, :])
                ln_stats_vg(gt6s, c_eps6, r=bi)
                s_act.wait(VE)
                VEw.tensor_scalar(xn6[bi][:, :], gt6s[:, :],
                                  v_rstd[:, bi:bi + 1], v_mb[:, bi:bi + 1],
                                  ALU.mult, ALU.add)
                gread_mark[0] = s_dve.n
                VEw.tensor_mul(xn6[bi][:, :], xn6[bi][:, :], goB[:, :])
                VEw.tensor_add(xn6[bi][:, :], xn6[bi][:, :], boB[:, :])
                VEw.tensor_add(xn6[bi][:, :], xn6[bi][:, :], xt6[bi][:, :])
                fin = s_dve.n
                s_dve.wait(SY, fin)
                s_st2[bi].inc(SY.dma_start(
                    ffx_scr[qt * 128:(qt + 1) * 128, :], xn6[bi][:, :]), 16)
                ffx_hist.append((s_st2[bi], s_st2[bi].n))

            dve_pre = s_dve.n
            pop_user = [0, 0]

            def p5_og_block(og):
                bi = og % 2
                h = og // 2
                if og == 0:
                    s_dve.wait(TE, dve_pre)
                for qt in range(NT):
                    pi = qt % 2
                    wsem, wval = ow_hist[og]
                    wsem.wait(TE, wval)
                    if len(st_hist) >= 2:
                        sem, val = st_hist[-2]
                        sem.wait(TE, val)
                    s_dve.wait(TE, pop_user[pi])
                    for ch in range(NKH):
                        s_pe.inc(TE.matmul(
                            pop[pi][:, :],
                            combT[:, ch * L + qt * 128:ch * L + (qt + 1) * 128],
                            wob[bi][:, ch * 512:(ch + 1) * 512],
                            start=(ch == 0), stop=(ch == NKH - 1)))
                    s_pe.wait(VE)
                    VEw.tensor_copy(postg[pi][:, :], pop[pi][:, :])
                    pop_user[pi] = s_dve.n
                    s_dve.wait(SY, pop_user[pi])
                    s_po8[og * 2 + pi].inc(SY.dma_start(
                        po_scr[og, qt * 128:(qt + 1) * 128, :],
                        postg[pi][:, :]), 16)
                    st_hist.append((s_po8[og * 2 + pi], s_po8[og * 2 + pi].n))
                pe_og.append(s_pe.n)

            def og_prefetch(og):
                bi = og % 2
                s_pe.wait(GP, pe_og[og - 2])
                s_owg[bi].inc(GP.dma_start(
                    wob[bi][:, :].rearrange("p (c n) -> p c n", c=8),
                    wos_r[:, :, og * 512:(og + 1) * 512]), 16)
                ow_hist.append((s_owg[bi], s_owg[bi].n))

            def rs_emit(og):
                s_po8[og * 2].wait(GP)
                s_po8[og * 2 + 1].wait(GP)
                s_cc.inc(GP.collective_compute(
                    "ReduceScatter", ALU.add,
                    replica_groups=[[0, 1], [2, 3], [4, 5], [6, 7]],
                    ins=[po_scr[og, :, :].opt()],
                    outs=[ao_scr[og, :, :].opt()]))
                rs_done[og] = s_cc.n

            p5_og_block(0)
            rs_emit(0)
            s_ct.wait(TE, bc_ct)
            s_pe.wait(SC)
            for og in range(4):
                s_dve.wait(TE)
                s_act.wait(TE)
                s_pe.inc(TE.matmul(p6p[0:1, :], oneh4[0:4, :],
                                   s4tmp[0:4, og * 512:(og + 1) * 512],
                                   start=True, stop=True))
                s_pe.wait(SC)
                SCw.copy(row_b[0:1, og * 512:(og + 1) * 512], p6p[0:1, :])
            bcast_rows(p6p, row_b, bc_bc, 128)
            p6a_tile(0)
            p5_og_block(1)
            og_prefetch(2)
            rs_emit(1)
            p6a_tile(1)
            p6a_tile(2)
            p5_og_block(2)
            og_prefetch(3)
            rs_emit(2)
            p6a_tile(3)
            p6a_tile(4)
            p5_og_block(3)
            rs_emit(3)
            p6a_tile(5)
            p6a_tile(6)
            p6a_tile(7)

            it = 0
            ob_hist = [0, 0, 0, 0]
            for half in range(2):
                GP.wait_ge(s_cc.h, rs_done[half * 2 + 1])
                SY.wait_ge(s_cc.h, rs_done[half * 2 + 1])
                for qt in range(8):
                    b = it % 4
                    Q = GP if b % 2 == 0 else SY
                    if it >= 4:
                        s_ob4[b].wait(Q, ob_hist[b])
                    if it < 4:
                        for sem, val in (ffx_hist[6], ffx_hist[7]):
                            sem.wait(Q, val)
                    sem, val = ffx_hist[qt]
                    sem.wait(Q, val)
                    sx = s_fb4[b]
                    sx.inc(Q.dma_start(
                        fin6[b][:, :],
                        ffx_scr[qt * 128:(qt + 1) * 128,
                                half * 1024:(half + 1) * 1024]), 16)
                    sx.inc(Q.dma_start(
                        at6[b][:, 0:512],
                        ao_scr[half * 2, qt * 128:(qt + 1) * 128, :]), 16)
                    sx.inc(Q.dma_start(
                        at6[b][:, 512:1024],
                        ao_scr[half * 2 + 1, qt * 128:(qt + 1) * 128, :]), 16)
                    sx.wait(VE, sx.n)
                    VEw.tensor_add(fin6[b][:, :], fin6[b][:, :], at6[b][:, :])
                    s_dve.wait(Q)
                    s_ob4[b].inc(Q.dma_start(
                        out[qt * 128:(qt + 1) * 128,
                            half * 1024:(half + 1) * 1024], fin6[b][:, :]), 16)
                    ob_hist[b] = s_ob4[b].n
                    it += 1
            for b in range(4):
                s_ob4[b].wait(GP, ob_hist[b])
            if DEBUG:
                GP.wait_ge(s_dbg.h, s_dbg.n)
        pvX.close()

    return nc


_prog_cache = {}


def _get_program():
    if "nc" not in _prog_cache:
        _prog_cache["nc"] = build_program()
    return _prog_cache["nc"]


def _consts():
    pos = np.arange(L, dtype=np.float64)
    inv = 1.0 / (10000.0 ** (np.arange(0, DH, 2, dtype=np.float64) / DH))
    fr = np.outer(pos, inv)
    emb = np.concatenate([fr, fr], axis=1)
    cos_fm = np.ascontiguousarray(np.cos(emb).T.astype(np.float32))
    sin_fm = np.sin(emb).T.astype(np.float32)
    sinS = np.ascontiguousarray(
        np.concatenate([-sin_fm[:64], sin_fm[64:]], axis=0).astype(np.float32))
    identity = np.eye(128).astype(ml_dtypes.bfloat16)
    identityf = np.eye(128).astype(np.float32)
    kk, qq = np.meshgrid(np.arange(128), np.arange(128), indexing="ij")
    mask = (kk <= qq).astype(ml_dtypes.bfloat16)
    return cos_fm, sinS, identity, identityf, mask


def _core_inputs(ii, c, consts):
    x = ii["L0_attn_x"]
    cos_fm, sinS, identity, identityf, mask = consts
    b, hh, th = c // 2, c % 2, c % 2
    A = np.ascontiguousarray
    return {
        "x_b": A(x[b]),
        "x_own": A(x[b, th * 1024:(th + 1) * 1024]),
        "x_last4": A(x[:, L - 1, :]),
        "wdkv": A(ii["L0_attn_W_DKV"]),
        "wuk": A(ii["L0_attn_W_UK"][:, hh * 1024:(hh + 1) * 1024]),
        "wuv": A(ii["L0_attn_W_UV"][:, hh * 1024:(hh + 1) * 1024]),
        "wdq": A(ii["L0_attn_W_DQ"]),
        "wuq": A(ii["L0_attn_W_UQ"][:, hh * 2048:(hh + 1) * 2048]),
        "wlam": A(ii["L0_attn_W_lam"][:, hh * 8:(hh + 1) * 8]),
        "blam": A(ii["L0_attn_b_lam"][hh * 8:(hh + 1) * 8]),
        "wout": A(ii["L0_attn_W_out"][hh * 1024:(hh + 1) * 1024, :]),
        "gatew": A(ii["L0_ffn_gate_w"]),
        "gateb": A(ii["L0_ffn_gate_b"]),
        "aggw": A(ii["L0_ffn_agg_w"][:, c * 256:(c + 1) * 256]),
        "aggb": A(ii["L0_ffn_agg_b"][c * 256:(c + 1) * 256]),
        "bcw": A(ii["L0_ffn_bc_w"][:, c * 256:(c + 1) * 256]),
        "bcb": A(ii["L0_ffn_bc_b"][c * 256:(c + 1) * 256]),
        "d1w": A(ii["L0_ffn_d1_w"][:, c * 1024:(c + 1) * 1024]),
        "d1b": A(ii["L0_ffn_d1_b"][c * 1024:(c + 1) * 1024]),
        "d2w": A(ii["L0_ffn_d2_w"][c * 1024:(c + 1) * 1024, :]),
        "d2b": A(ii["L0_ffn_d2_b"]),
        "ing": A(ii["L0_ffn_in_gamma"]), "inb": A(ii["L0_ffn_in_beta"]),
        "png": A(ii["L0_ffn_pn_gamma"]), "pnb": A(ii["L0_ffn_pn_beta"]),
        "outg": A(ii["L0_ffn_out_gamma"]), "outb": A(ii["L0_ffn_out_beta"]),
        "cosk": cos_fm, "sinkS": sinS,
        "ident": identity, "identf": identityf, "maskdiag": mask,
        "onehot4": A(np.eye(4, dtype=np.float32)[:, b:b + 1]),
    }


def kernel(**inputs):
    ii = {k: np.asarray(v) for k, v in inputs.items()}
    consts = _consts()
    nc = _get_program()
    in_maps = [_core_inputs(ii, c, consts) for c in range(8)]
    res = run_bass_kernel_spmd(nc, in_maps, core_ids=list(range(8)))
    y = np.empty((B, L, D), dtype=np.float32)
    for c in range(8):
        b, th = c // 2, c % 2
        y[b, th * 1024:(th + 1) * 1024, :] = np.asarray(res.results[c]["out"])
    return y
